# version 13
# baseline (speedup 1.0000x reference)
import numpy as np
import ml_dtypes
import concourse.bass as bass
import concourse.mybir as mybir
from concourse.bass_utils import run_bass_kernel_spmd
from contextlib import ExitStack

F32 = mybir.dt.float32
BF16 = mybir.dt.bfloat16
U32 = mybir.dt.uint32
AF = mybir.ActivationFunctionType
ALU = mybir.AluOpType
AX = mybir.AxisListType

GA = 0.7978845608028654
GB = 0.044715
NEG = -3.0e38
ESZ = {F32: 4, BF16: 2, U32: 4}


def make_cfg(T=4096, C=1024, NH=16, HID=4096, E=8, TB=1024, NBIS=44, KTH=24576, NCORES=8):
    c = dict(T=T, C=C, NH=NH, HID=HID, E=E, TB=TB, NBIS=NBIS, KTH=KTH, NCORES=NCORES)
    c["D"] = 64
    c["DHE"] = 32
    c["NLC"] = NH // 2
    c["CC"] = C // 128
    c["TQ"] = T // 512
    c["NKT"] = T // 128
    c["NHC"] = HID // 128
    c["NTB"] = T // TB
    c["TT8"] = TB // 128
    c["NTS"] = max(1, TB // 512)
    c["TSW"] = TB // c["NTS"]
    c["ND2"] = max(1, C // 512)
    c["DW"] = C // c["ND2"]
    c["GH"] = min(8, c["NHC"])
    c["LGC"] = NCORES * E * T // 128
    assert C == NH * 64 and T % 512 == 0 and TB % 128 == 0 and HID % 512 == 0
    assert c["TSW"] <= 512 and c["DW"] <= 512
    return c


class Sem:
    def __init__(self, nc, ctx, name):
        self.h = ctx.enter_context(nc.semaphore(name))
        self.n = 0


class Res:
    __slots__ = ("w", "r")

    def __init__(self):
        self.w = None
        self.r = None


def deps(*items):
    return [it for it in items if it is not None]


class Prog:
    def __init__(self):
        self.ops = {"t": [], "v": [], "s": [], "g": [], "y": []}
        self.waited = {k: {} for k in self.ops}

    def emit(self, eng, builder, waits=(), inc=None, incn=1):
        seen = {}
        for (s, v) in waits:
            if v and v > 0:
                key = id(s.h)
                if key not in seen or seen[key][1] < v:
                    seen[key] = (s.h, v, s)
        done = self.waited[eng]
        wl = []
        for key, (h, v, s) in seen.items():
            if done.get(key, -1) >= v:
                continue
            done[key] = v
            wl.append((h, v))
        if inc is not None:
            inc.n += incn
        ih = inc.h if inc is not None else None

        def run(e, wl=wl, builder=builder, ih=ih, incn=incn):
            for h, v in wl:
                e.wait_ge(h, v)
            inst = builder(e)
            if ih is not None:
                inst.then_inc(ih, incn)
        self.ops[eng].append(run)
        return (inc, inc.n) if inc is not None else None


def rope_tables(T, DHE):
    inv = 1.0 / (10000.0 ** (np.arange(0, DHE, 2, dtype=np.float64) / DHE))
    ang = np.arange(T, dtype=np.float64)[:, None] * inv[None, :]
    emb = np.concatenate([ang, ang], axis=1)
    cosR = np.tile(np.cos(emb).T, (4, 1)).astype(ml_dtypes.bfloat16)
    sinR = np.tile(np.sin(emb).T, (4, 1)).astype(ml_dtypes.bfloat16)
    return cosR, sinR


def perm_matrix(DHE):
    Pm = np.zeros((DHE, DHE), dtype=np.float32)
    for i in range(DHE // 2):
        Pm[2 * i + 1, 2 * i] = -1.0
        Pm[2 * i, 2 * i + 1] = 1.0
    return Pm.astype(ml_dtypes.bfloat16)


def build_program(c):
    T, C, NH, HID, E, TB = c["T"], c["C"], c["NH"], c["HID"], c["E"], c["TB"]
    D, DHE, NLC, CC, TQ, NKT = c["D"], c["DHE"], c["NLC"], c["CC"], c["TQ"], c["NKT"]
    NHC, NTB, TT8 = c["NHC"], c["NTB"], c["TT8"]
    NTS, TSW, ND2, DW, GH = c["NTS"], c["TSW"], c["ND2"], c["DW"], c["GH"]
    NBIS, KTH, NCORES, LGC = c["NBIS"], c["KTH"], c["NCORES"], c["LGC"]
    UW = TB
    NG8 = NHC // GH
    NF1 = max(1, NHC // 4)
    NWV = (TT8 + 3) // 4

    nc = bass.Bass("TRN2", target_bir_lowering=False, debug=False)

    x_h = nc.dram_tensor("x", [T, C], F32, kind="ExternalInput")
    xT_h = nc.dram_tensor("xT", [C, T], F32, kind="ExternalInput")
    wq_h = nc.dram_tensor("wq", [C, C], F32, kind="ExternalInput")
    wk_h = nc.dram_tensor("wk", [C, C], F32, kind="ExternalInput")
    wv_h = nc.dram_tensor("wv", [C, C], F32, kind="ExternalInput")
    upcq_h = nc.dram_tensor("upcq", [D, DHE], F32, kind="ExternalInput")
    upeq_h = nc.dram_tensor("upeq", [D, DHE], F32, kind="ExternalInput")
    upeqT_h = nc.dram_tensor("upeqT", [DHE, D], F32, kind="ExternalInput")
    upck_h = nc.dram_tensor("upck", [D, DHE], F32, kind="ExternalInput")
    upek_h = nc.dram_tensor("upek", [D, DHE], F32, kind="ExternalInput")
    upekT_h = nc.dram_tensor("upekT", [DHE, D], F32, kind="ExternalInput")
    wvup_h = nc.dram_tensor("wvup", [D, D], F32, kind="ExternalInput")
    wc_h = nc.dram_tensor("wc", [C, C], F32, kind="ExternalInput")
    gwT_h = nc.dram_tensor("gwT", [C, E], F32, kind="ExternalInput")
    w1T_h = nc.dram_tensor("w1T", [E, C, HID], F32, kind="ExternalInput")
    w2T_h = nc.dram_tensor("w2T", [E, HID, C], F32, kind="ExternalInput")
    b1r_h = nc.dram_tensor("b1r", [E, 128, NHC], F32, kind="ExternalInput")
    b2_h = nc.dram_tensor("b2", [E, C], F32, kind="ExternalInput")
    out_h = nc.dram_tensor("out", [T, C], F32, kind="ExternalOutput")

    yT_d = nc.dram_tensor("yT_d", [C, T], BF16)
    x1_d = nc.dram_tensor("x1_d", [T, C], F32)
    x1T_d = nc.dram_tensor("x1T_d", [C, T], BF16)
    cc_in = nc.dram_tensor("cc_in", [E, T], F32)
    cc_out = nc.dram_tensor("cc_out", [NCORES * E, T], F32)
    w1c_d = nc.dram_tensor("w1c_d", [E, C, HID], BF16)
    w2c_d = nc.dram_tensor("w2c_d", [E, HID, C], BF16)

    mdiag_np = np.zeros((128, 128), dtype=np.float32)
    for kp in range(128):
        mdiag_np[kp, :kp] = NEG
    cosR, sinR = rope_tables(T, DHE)
    cpool_np = np.zeros((128, 4), np.float32)
    cpool_np[:, 0] = 1e-5
    consts = {
        "mdiag": mdiag_np,
        "mfull": np.full((128, 128), NEG, np.float32),
        "identf": np.eye(128, dtype=np.float32),
        "ones64c": np.ones((1, D), dtype=ml_dtypes.bfloat16),
        "ones128": np.ones((128, 128), dtype=np.float32),
        "pm": perm_matrix(DHE),
        "cosR": cosR,
        "sinR": sinR,
        "cpool": cpool_np,
    }
    cth = {k: nc.inline_tensor(v, k) for k, v in consts.items()}

    with ExitStack() as ctx:
        ARENA0 = nc.sbuf_base + 1536
        off = [ARENA0]

        def sba(name, shape, dt, at=None):
            fb = int(np.prod(shape[1:])) * ESZ[dt]
            o = (off[0] + 63) // 64 * 64 if at is None else at
            if at is None:
                off[0] = o + fb
            return nc.alloc_sbuf_tensor_at(name, shape, dt, offset=o)

        mdiag = sba("mdiag_sb", [128, 128], F32)
        mfull = sba("mfull_sb", [128, 128], F32)
        identf = sba("identf_sb", [128, 128], F32)
        ones64 = sba("ones64_sb", [1, D], BF16)
        ones128 = sba("ones128_sb", [128, 128], F32)
        cpool = sba("cpool_sb", [128, 4], F32)
        pm = sba("pm_sb", [DHE, DHE], BF16)
        upcq = sba("upcq_sb", [128, DHE], BF16)
        upeq = sba("upeq_sb", [128, DHE], BF16)
        upck = sba("upck_sb", [128, DHE], BF16)
        upek = sba("upek_sb", [128, DHE], BF16)
        upeqT = sba("upeqT_sb", [DHE, D], BF16)
        upekT = sba("upekT_sb", [DHE, D], BF16)
        wvup = sba("wvup_sb", [128, D], BF16)
        wpq = sba("wpq_sb", [128, DHE], BF16)
        wpk = sba("wpk_sb", [128, DHE], BF16)
        gwhi = sba("gwhi_sb", [128, CC * E], BF16)
        gwlo = sba("gwlo_sb", [128, CC * E], BF16)
        gwf = sba("gwf_sb", [128, CC * E], F32)
        logitsT = sba("logitsT_sb", [E, T], F32)
        g_tok = sba("g_tok_sb", [128, NKT * E], F32)
        b2b = sba("b2_sb", [E, C], F32)
        b1e = [sba(f"b1e{i}", [128, NHC], F32) for i in range(2)]
        bis_lo = sba("bis_lo", [128, 1], F32)
        bis_hi = sba("bis_hi", [128, 1], F32)
        bis_mid = sba("bis_mid", [128, 1], F32)
        bis_t1 = sba("bis_t1", [128, 1], F32)
        bis_cnt = sba("bis_cnt", [128, 1], F32)
        bis_tot = sba("bis_tot", [128, 1], F32)
        bis_p = sba("bis_p", [128, 1], U32)
        bis_pi = sba("bis_pi", [128, 1], U32)
        lo2 = sba("lo2_sb", [128, 1], F32)
        ln_nm = sba("ln_nm", [128, 1], F32)
        ln_vs = sba("ln_vs", [128, 1], F32)
        ln_t0 = sba("ln_t0", [128, 1], F32)
        ln_rstd = sba("ln_rstd", [128, 1], F32)
        a0 = (off[0] + 63) // 64 * 64

        off[0] = a0
        xT = sba("xT_sb", [128, CC * T], BF16)
        qT = sba("qT_sb", [128, T], BF16)
        kT = sba("kT_sb", [128, T], BF16)
        yTp = sba("yTp_sb", [128, T], BF16)
        vpair = sba("v_sb", [128, 2 * NKT * (D + 1)], BF16)
        latb = [sba(f"lat{i}", [128, T], BF16) for i in range(2)]
        wqb = [sba(f"wqb{i}", [128, CC * 128], BF16) for i in range(2)]
        wkb = [sba(f"wkb{i}", [128, CC * 128], BF16) for i in range(2)]
        wvb = [sba(f"wvb{i}", [128, CC * 128], BF16) for i in range(2)]
        cos = sba("cos_sb", [128, T], BF16)
        sin = sba("sin_sb", [128, T], BF16)
        expb = [sba(f"expb{i}", [128, 512], BF16) for i in range(4)]
        ysb = [sba(f"ysb{i}", [D + 1, 512], F32) for i in range(2)]
        rcp = [sba(f"rcp{i}", [1, 512], F32) for i in range(2)]
        rcpb = [sba(f"rcpb{i}", [1, 512], BF16) for i in range(2)]
        ropeA = sba("ropeA_sb", [128, 512], F32)
        ropeB = sba("ropeB_sb", [128, 512], F32)
        ropeC = sba("ropeC_sb", [128, 512], F32)
        ropeD = sba("ropeD_sb", [128, 512], F32)
        a_end = off[0]

        off[0] = a0
        wc_sb = sba("wc_sb", [128, CC * C], BF16)
        ytb = [sba(f"ytb{i}", [128, CC * 128], BF16) for i in range(2)]
        xb = [sba(f"xb{i}", [128, C], F32) for i in range(2)]
        r_sb = sba("r_sb", [128, C], F32)
        xc_sb = sba("xc_sb", [128, C], F32)
        sq_sb = sba("sq_sb", [128, C], F32)
        x1f = sba("x1f_sb", [128, C], F32)
        hi_b = sba("hi_sb", [128, C], BF16)
        lof = sba("lof_sb", [128, C], F32)
        x1Ts = [sba(f"x1Ts{i}", [128, CC * 128], BF16) for i in range(2)]
        loTs = sba("loTs_sb", [128, CC * 128], BF16)
        b_end = off[0]

        off[0] = a0
        x1T_sb = sba("x1T_sb", [128, CC * TB], BF16)
        h_sb = sba("h_sb", [128, NHC * TB], BF16)
        accs = sba("acc_sb", [128, TT8 * C], F32)
        w1b = [sba(f"w1b{i}", [128, CC * 512], BF16) for i in range(2)]
        w2b = [sba(f"w2b{i}", [128, GH * DW], BF16) for i in range(2)]
        x1r = [sba(f"x1r{i}_sb", [128, C], F32) for i in range(2)]
        lnx = [sba(f"lnx{i}_sb", [128, C], F32) for i in range(2)]
        lnt2 = sba("lnt2_sb", [128, 2], F32)
        lnn2 = sba("lnn2_sb", [128, 2], F32)
        lnv2 = sba("lnv2_sb", [128, 2], F32)
        lnr2 = sba("lnr2_sb", [128, 2], F32)
        d_end = off[0]
        assert max(a_end, b_end, d_end) < nc.sbuf_top - 64, (a_end, b_end, d_end, nc.sbuf_top)

        x1T_off = (a0 + 63) // 64 * 64
        h_off = (x1T_off + CC * TB * 2 + 63) // 64 * 64
        acc_off = (h_off + NHC * TB * 2 + 63) // 64 * 64
        if LGC * 6 + T * 4 <= TT8 * C * 4:
            lg_all = sba("lg_all_sb", [128, LGC], F32, at=acc_off)
            cmpb = sba("cmpb_sb", [128, LGC], BF16, at=acc_off + LGC * 4)
            gtmp = sba("gtmp_sb", [E, T], F32, at=acc_off + LGC * 6)
        else:
            lg_all = sba("lg_all_sb", [128, LGC], F32)
            cmpb = sba("cmpb_sb", [128, LGC], BF16)
            gtmp = sba("gtmp_sb", [E, T], F32)
            d_end = off[0]
            assert d_end < nc.sbuf_top - 64, d_end

        pb = [ctx.enter_context(nc.psum_tensor(f"pb{i}", [128, 512], F32)) for i in range(8)]
        pbr = [Res() for _ in range(8)]

        sld = Sem(nc, ctx, "sld")
        sl2 = Sem(nc, ctx, "sl2")
        wst = [Sem(nc, ctx, f"wst{i}") for i in range(2)]
        swc = Sem(nc, ctx, "swc")
        syt = [Sem(nc, ctx, f"syt{i}") for i in range(2)]
        sxb = [Sem(nc, ctx, f"sxb{i}") for i in range(2)]
        sx1s = Sem(nc, ctx, "sx1s")
        sxT = [Sem(nc, ctx, f"sxT{i}") for i in range(2)]
        syT = Sem(nc, ctx, "syT")
        scc = Sem(nc, ctx, "scc")
        slg = Sem(nc, ctx, "slg")
        sx1T = Sem(nc, ctx, "sx1T")
        sb1 = [Sem(nc, ctx, f"sb1{i}") for i in range(2)]
        sw1 = [Sem(nc, ctx, f"sw1{i}") for i in range(2)]
        sw2 = [Sem(nc, ctx, f"sw2{i}") for i in range(2)]
        sx1r = [Sem(nc, ctx, "sx1ra"), Sem(nc, ctx, "sx1rb")]
        sout = Sem(nc, ctx, "sout")
        te = Sem(nc, ctx, "te")
        ve = Sem(nc, ctx, "ve")
        se = Sem(nc, ctx, "se")
        ge = Sem(nc, ctx, "ge")
        scst = Sem(nc, ctx, "scst")
        ccs = Sem(nc, ctx, "ccs")

        nc.const_aps.register(F32, 1e-5, cpool[:, 0:1])

        block_holder = ctx

        P = Prog()

        def dma(eng, sem, out_ap, in_ap, waits=()):
            return P.emit(eng, lambda e, o=out_ap, i=in_ap: e.dma_start(out=o, in_=i),
                          waits=waits, inc=sem, incn=16)

        for name, dst in [("mdiag", mdiag), ("mfull", mfull), ("identf", identf),
                          ("ones64c", ones64), ("ones128", ones128), ("cpool", cpool),
                          ("pm", pm), ("cosR", cos), ("sinR", sin)]:
            dma("y", sl2, dst[:, :], cth[name][:, :])
        dma("y", sl2, gwf[:, :], gwT_h.ap().rearrange("(c p) e -> p c e", p=128))
        dma("y", sl2, b2b[:, :], b2_h[:, :])
        setup_l2 = (sl2, sl2.n)

        for dst, src in [(upcq, upcq_h), (upeq, upeq_h), (upck, upck_h), (upek, upek_h),
                         (wvup, wvup_h)]:
            dma("g", sld, dst[0:D, :], src[:, :])
            dma("g", sld, dst[D:128, :], src[:, :])
        dma("g", sld, upeqT[:, :], upeqT_h[:, :])
        dma("g", sld, upekT[:, :], upekT_h[:, :])
        dma("g", sld, gwhi[:, :], gwT_h.ap().rearrange("(c p) e -> p c e", p=128))
        for i in range(CC):
            dma("g", sld, xT[:, i * T:(i + 1) * T],
                xT_h.ap().rearrange("(c p) t -> p c t", p=128)[:, i, :])
        setup_ld = (sld, sld.n)
        pair_w_ld = {}
        for lci in range(min(2, NLC)):
            for wb, wh in ((wqb, wq_h), (wkb, wk_h), (wvb, wv_h)):
                dma("g", wst[lci % 2], wb[lci][:, :],
                    wh.ap().rearrange("(c p) l -> p c l", p=128)[:, :, lci * 128:(lci + 1) * 128])
            pair_w_ld[lci] = (wst[lci % 2], wst[lci % 2].n)

        gwlo_v = P.emit("v", lambda e: e.tensor_sub(gwlo[:, :], gwf[:, :], gwhi[:, :]),
                        waits=[setup_l2, setup_ld], inc=ve)
        nvl = 2 * NKT
        vset = None
        for j in range(nvl):
            vset = P.emit("v", lambda e, j=j: e.memset(
                vpair[:, j * (D + 1) + D: j * (D + 1) + D + 1], 1.0),
                waits=deps(setup_ld) if j == 0 else (),
                inc=ve if j == nvl - 1 else None)

        wp_done = None
        for wp, upT in ((wpq, upeqT), (wpk, upekT)):
            last_mm = None
            for h in range(2):
                last_mm = P.emit("t", lambda e, upT=upT, h=h: e.matmul(
                    pb[6][h * D:h * D + D, 0:DHE], upT[:, :], pm[:, :],
                    start=True, stop=True, tile_position=(0, h * D)),
                    waits=deps(setup_ld, setup_l2, vset, pbr[6].r), inc=te)
            pbr[6].w = last_mm
            wp_done = P.emit("s", lambda e, wp=wp: e.copy(wp[:, :], pb[6][:, 0:DHE]),
                             waits=deps(pbr[6].w), inc=se)
            pbr[6].r = wp_done

        NB4 = min(4, T // 512)
        THN = T // (512 * NB4)
        N8 = T // 512
        latr = [Res(), Res()]
        qTr, kTr, vpr, yTr = Res(), Res(), Res(), Res()
        expr = [Res() for _ in range(4)]
        ysbr = [Res(), Res()]
        ropeAr, ropeBr = Res(), Res()
        wbfree = {0: None, 1: None}

        cast_chunks = []
        for ei in range(E):
            cp1 = max(1, C // 128)
            r1 = C // cp1
            for p in range(cp1):
                cast_chunks.append((w1c_d[ei, p * r1:(p + 1) * r1, :],
                                    w1T_h[ei, p * r1:(p + 1) * r1, :]))
        for ei in range(E):
            cp2 = max(1, HID // 512)
            r2 = HID // cp2
            for p in range(cp2):
                cast_chunks.append((w2c_d[ei, p * r2:(p + 1) * r2, :],
                                    w2T_h[ei, p * r2:(p + 1) * r2, :]))

        for lc in range(NLC):
            for wi in range(3):
                wb = (wqb, wkb, wvb)[wi]
                lbi = 0 if wi != 1 else 1
                lbuf = latb[lbi]
                lrs = latr[lbi]
                last_lat_mm = None
                for th in range(THN):
                    bank_done = [None] * NB4
                    for cci in range(CC):
                        for n4 in range(NB4):
                            tw = deps(pbr[n4].r, lrs.r, pair_w_ld.get(lc), setup_ld) if cci == 0 else ()
                            isl = cci == CC - 1
                            bd = P.emit("t", lambda e, wb=wb, lc=lc, cci=cci, th=th, n4=n4, isl=isl: e.matmul(
                                pb[n4][:, :], wb[lc % 2][:, cci * 128:(cci + 1) * 128],
                                xT[:, cci * T + th * 512 * NB4 + n4 * 512:
                                   cci * T + th * 512 * NB4 + (n4 + 1) * 512],
                                start=(cci == 0), stop=isl),
                                waits=tw, inc=te if isl else None)
                            if isl:
                                bank_done[n4] = bd
                    last_lat_mm = bank_done[NB4 - 1]
                    for n4 in range(NB4):
                        pbr[n4].w = bank_done[n4]
                        ev = P.emit("v", lambda e, lbuf=lbuf, th=th, n4=n4: e.tensor_copy(
                            lbuf[:, th * 512 * NB4 + n4 * 512: th * 512 * NB4 + (n4 + 1) * 512],
                            pb[n4][:, :]),
                            waits=deps(bank_done[n4]), inc=ve)
                        pbr[n4].r = ev
                        lrs.w = ev
                wbfree[lc % 2] = last_lat_mm

                if wi < 2:
                    dstT = qT if wi == 0 else kT
                    dstR = qTr if wi == 0 else kTr
                    upc = upcq if wi == 0 else upck
                    upe = upeq if wi == 0 else upek
                    wpp = wpq if wi == 0 else wpk
                    last_up_mm = None
                    rsc = [(ropeA, ropeB, ropeAr), (ropeC, ropeD, ropeBr)]
                    for n8 in range(N8):
                        lo_, hi_ = n8 * 512, (n8 + 1) * 512
                        ba = 4 + 2 * (n8 % 2)
                        bb = ba + 1
                        rA, rB, rR = rsc[n8 % 2]
                        specs = [(upc, 0, ba, 0), (upe, 0, ba, DHE), (wpp, 0, bb, DHE),
                                 (upc, 64, ba, 64), (upe, 64, ba, 64 + DHE), (wpp, 64, bb, 64 + DHE)]
                        mmv = []
                        for si, (wsb, hb, bki, po) in enumerate(specs):
                            tw = deps(lrs.w, pbr[ba].r, pbr[bb].r, wp_done) if si == 0 else ()
                            mmv.append(P.emit("t", lambda e, wsb=wsb, hb=hb, bki=bki, po=po,
                                              lbuf=lbuf, lo_=lo_, hi_=hi_: e.matmul(
                                pb[bki][po:po + DHE, 0:512], wsb[hb:hb + D, :],
                                lbuf[hb:hb + D, lo_:hi_],
                                start=True, stop=True, tile_position=(hb, po)),
                                waits=tw, inc=te))
                        last_up_mm = mmv[-1]
                        lastv = None
                        for h in range(2):
                            hb = 64 * h
                            P.emit("v", lambda e, hb=hb, dstT=dstT, lo_=lo_, hi_=hi_, ba=ba: e.tensor_copy(
                                dstT[hb:hb + DHE, lo_:hi_], pb[ba][hb:hb + DHE, 0:512]),
                                waits=deps(mmv[3 * h + 0], dstR.r, rR.r))
                            P.emit("v", lambda e, hb=hb, lo_=lo_, hi_=hi_, ba=ba, rA=rA: e.tensor_mul(
                                rA[hb + DHE:hb + 2 * DHE, 0:512],
                                pb[ba][hb + DHE:hb + 2 * DHE, 0:512],
                                cos[hb + DHE:hb + 2 * DHE, lo_:hi_]),
                                waits=deps(mmv[3 * h + 1]))
                            t2v = P.emit("v", lambda e, hb=hb, lo_=lo_, hi_=hi_, bb=bb, rB=rB: e.tensor_mul(
                                rB[hb + DHE:hb + 2 * DHE, 0:512],
                                pb[bb][hb + DHE:hb + 2 * DHE, 0:512],
                                sin[hb + DHE:hb + 2 * DHE, lo_:hi_]),
                                waits=deps(mmv[3 * h + 2]), inc=ve)
                            lastv = P.emit("v", lambda e, hb=hb, dstT=dstT, lo_=lo_, hi_=hi_, rA=rA, rB=rB: e.tensor_add(
                                dstT[hb + DHE:hb + 2 * DHE, lo_:hi_],
                                rA[hb + DHE:hb + 2 * DHE, 0:512],
                                rB[hb + DHE:hb + 2 * DHE, 0:512]),
                                waits=deps(t2v), inc=ve)
                        pbr[ba].r = lastv
                        pbr[bb].r = lastv
                        rR.r = lastv
                        dstR.w = lastv
                    lrs.r = last_up_mm
                else:
                    last_v_mm = None
                    for h in range(2):
                        for tt in range(NKT):
                            bki = 6 + (tt % 2)
                            vn = P.emit("t", lambda e, h=h, tt=tt, bki=bki, lbuf=lbuf: e.matmul(
                                pb[bki][:, 0:D], lbuf[h * D:h * D + D, tt * 128:(tt + 1) * 128],
                                wvup[h * D:h * D + D, :], start=True, stop=True,
                                tile_position=(h * D, 0)),
                                waits=deps(lrs.w, pbr[bki].r), inc=te)
                            pbr[bki].w = vn
                            last_v_mm = vn
                            ev = P.emit("v", lambda e, h=h, tt=tt, bki=bki: e.tensor_copy(
                                vpair[:, (h * NKT + tt) * (D + 1):(h * NKT + tt) * (D + 1) + D],
                                pb[bki][:, 0:D]),
                                waits=deps(vn, vpr.r), inc=ve)
                            pbr[bki].r = ev
                            vpr.w = ev
                    lrs.r = last_v_mm

            if lc + 2 < NLC:
                for wb, wh in ((wqb, wq_h), (wkb, wk_h), (wvb, wv_h)):
                    dma("g", wst[lc % 2], wb[lc % 2][:, :],
                        wh.ap().rearrange("(c p) l -> p c l", p=128)[:, :, (lc + 2) * 128:(lc + 3) * 128],
                        waits=deps(wbfree[lc % 2]))
                pair_w_ld[lc + 2] = (wst[lc % 2], wst[lc % 2].n)

            ncst = len(cast_chunks)
            per = -(-ncst // NLC)
            for ck in range(lc * per, min((lc + 1) * per, ncst)):
                dst, srcp = cast_chunks[ck]
                dma("g", scst, dst, srcp)

            last_mm_s = None
            for qb in range(TQ):
                kcm = 4 * qb + 4
                mmy_done = [None, None]

                def emit_av(kc, h, qb=qb, kcm=kcm):
                    sbi = 2 * h + (kc % 2)
                    ebuf = expb[sbi]
                    sev = expr[sbi].w
                    yw = deps(sev)
                    if kc == 0:
                        yw = deps(sev, pbr[4 + h].r, vpr.w, yTr.r)
                    mmy = P.emit("t", lambda e, h=h, kc=kc, kcm=kcm, ebuf=ebuf: e.matmul(
                        pb[4 + h][0:D + 1, :],
                        vpair[:, (h * NKT + kc) * (D + 1):(h * NKT + kc + 1) * (D + 1)],
                        ebuf[:, :], start=(kc == 0), stop=(kc == kcm - 1)),
                        waits=yw, inc=te)
                    expr[sbi].r = mmy
                    if kc == kcm - 1:
                        pbr[4 + h].w = mmy
                        mmy_done[h] = mmy

                for kc in range(kcm):
                    par = kc % 2
                    for h in range(2):
                        sbi = 2 * h + par
                        mm = P.emit("t", lambda e, h=h, kc=kc, qb=qb, sbi=sbi: e.matmul(
                            pb[sbi][:, :], kT[h * D:h * D + D, kc * 128:(kc + 1) * 128],
                            qT[h * D:h * D + D, qb * 512:(qb + 1) * 512],
                            start=True, stop=True),
                            waits=deps(pbr[sbi].r, qTr.w, kTr.w), inc=te)
                        last_mm_s = mm
                        pbr[sbi].w = mm
                        rel = kc - 4 * qb
                        if rel >= 0:
                            mv = None
                            for tt in range(rel + 1):
                                msk = mfull if tt < rel else mdiag
                                mv = P.emit("v", lambda e, sbi=sbi, tt=tt, msk=msk: e.tensor_tensor(
                                    pb[sbi][:, tt * 128:(tt + 1) * 128],
                                    pb[sbi][:, tt * 128:(tt + 1) * 128], msk[:, :], ALU.add),
                                    waits=deps(mm) if tt == 0 else (),
                                    inc=ve if tt == rel else None)
                            pbr[sbi].w = mv
                        ebuf = expb[sbi]
                        sev = P.emit("s", lambda e, ebuf=ebuf, sbi=sbi: e.activation(
                            ebuf[:, :], pb[sbi][:, :], AF.Exp, scale=0.125),
                            waits=deps(pbr[sbi].w, expr[sbi].r), inc=se)
                        expr[sbi].w = sev
                        pbr[sbi].r = sev
                    if kc > 0:
                        for h in range(2):
                            emit_av(kc - 1, h)
                for h in range(2):
                    emit_av(kcm - 1, h)
                for h in range(2):
                    yev = P.emit("s", lambda e, h=h: e.copy(ysb[h][:, :], pb[4 + h][0:D + 1, :]),
                                 waits=deps(mmy_done[h], ysbr[h].r), inc=se)
                    ysbr[h].w = yev
                    pbr[4 + h].r = yev
                    rv = P.emit("v", lambda e, h=h: e.reciprocal(rcp[h][:, :], ysb[h][D:D + 1, :]),
                                waits=deps(yev), inc=ve)
                    rbv = P.emit("v", lambda e, h=h: e.tensor_copy(rcpb[h][:, :], rcp[h][:, :]),
                                 waits=deps(rv), inc=ve)
                    bct = P.emit("t", lambda e, h=h: e.matmul(
                        pb[6 + h][0:D, :], ones64[:, :], rcpb[h][:, :], start=True, stop=True),
                        waits=deps(rbv, pbr[6 + h].r), inc=te)
                    pbr[6 + h].w = bct
                    ym = P.emit("v", lambda e, h=h, qb=qb: e.tensor_tensor(
                        yTp[h * D:h * D + D, qb * 512:(qb + 1) * 512],
                        ysb[h][0:D, :], pb[6 + h][0:D, 0:512], ALU.mult),
                        waits=deps(bct, yTr.r), inc=ve)
                    ysbr[h].r = ym
                    pbr[6 + h].r = ym
                    yTr.w = ym
            qTr.r = last_mm_s
            kTr.r = last_mm_s
            st_y = dma("y", syT, yT_d[lc * 128:(lc + 1) * 128, :], yTp[:, :], waits=deps(yTr.w))
            yTr.r = st_y

        a_done = deps(yTr.r)

        PH = c.get("PHASES", "ABCD")
        if "B" not in PH:
            P.emit("y", lambda e: e.nop(), waits=[(syT, syT.n)])
            return _finish(nc, P, block_holder)
        cast_gate = [(scst, scst.n)] if scst.n else []
        dma("g", swc, wc_sb[:, :], wc_h.ap().rearrange("(c p) d -> p c d", p=128), waits=a_done)
        wc_ld = (swc, swc.n)
        ytbr = [Res(), Res()]
        xbr = [Res(), Res()]
        x1Tsr = [Res(), Res()]
        loTsr = Res()
        rR, xcR, sqR, x1R, hiR, lofR = Res(), Res(), Res(), Res(), Res(), Res()
        nmR, vsR, t0R, rstdR = Res(), Res(), Res(), Res()
        logits_done = None
        x1_store_done = None
        x1T_store_done = None
        NW = max(1, C // 512)
        WNl = C // NW

        def emit_ln(srcT, srcRes, xcT, xcRes, sqT, sqRes):
            rsv = P.emit("v", lambda e, srcT=srcT: e.reduce_sum(ln_t0[:, :], srcT[:, 0:C], axis=AX.X),
                         waits=deps(srcRes.w, t0R.r, t0R.w), inc=ve)
            t0R.w = rsv
            nmv = P.emit("v", lambda e: e.tensor_scalar(
                ln_nm[:, :], ln_t0[:, :], -1.0 / C, None, ALU.mult),
                waits=deps(rsv, nmR.r), inc=ve)
            nmR.w = nmv
            xcv = P.emit("s", lambda e, srcT=srcT, xcT=xcT: e.activation(
                xcT[:, 0:C], srcT[:, 0:C], AF.Identity, bias=ln_nm[:, 0:1]),
                waits=deps(nmv, xcRes.r, xcRes.w), inc=se)
            xcRes.w = xcv
            nmR.r = xcv
            srcRes.r = xcv
            sqv = P.emit("s", lambda e, xcT=xcT, sqT=sqT: e.activation(
                sqT[:, 0:C], xcT[:, 0:C], AF.Square, accum_out=ln_vs[:, 0:1]),
                waits=deps(xcv, sqRes.r, sqRes.w, vsR.r), inc=se)
            sqRes.w = sqv
            vsR.w = sqv
            lnv = P.emit("s", lambda e: e.activation(
                ln_t0[:, :], ln_vs[:, :], AF.Ln, scale=1.0 / C, bias=1e-5),
                waits=deps(sqv, nmv, t0R.r), inc=se)
            t0R.w = lnv
            vsR.r = lnv
            rsv2 = P.emit("s", lambda e: e.activation(
                ln_rstd[:, :], ln_t0[:, :], AF.Exp, scale=-0.5),
                waits=deps(lnv, rstdR.r), inc=se)
            rstdR.w = rsv2
            t0R.r = rsv2
            return rsv2

        for tt in range(NKT):
            par = tt % 2
            ytld = dma("y", syt[par], ytb[par][:, :],
                       yT_d.ap().rearrange("(c p) t -> p c t", p=128)[:, :, tt * 128:(tt + 1) * 128],
                       waits=deps(ytbr[par].r) + (a_done if tt < 2 else []))
            ytbr[par].w = ytld
            xld = dma("y", sxb[par], xb[par][:, :], x_h[tt * 128:(tt + 1) * 128, :],
                      waits=deps(xbr[par].r) + (a_done if tt < 2 else []))
            xbr[par].w = xld
            mm_done = [None] * NW
            for cci in range(CC):
                for wn in range(NW):
                    tw = deps(ytld, wc_ld, pbr[wn].r) if cci == 0 else ()
                    md = P.emit("t", lambda e, cci=cci, wn=wn, par=par: e.matmul(
                        pb[wn][:, 0:WNl], ytb[par][:, cci * 128:(cci + 1) * 128],
                        wc_sb[:, cci * C + wn * WNl: cci * C + (wn + 1) * WNl],
                        start=(cci == 0), stop=(cci == CC - 1)),
                        waits=tw, inc=te if cci == CC - 1 else None)
                    if cci == CC - 1:
                        mm_done[wn] = md
                        pbr[wn].w = md
            rv = None
            for wn in range(NW):
                rv = P.emit("v", lambda e, wn=wn, par=par: e.tensor_tensor(
                    r_sb[:, wn * WNl:(wn + 1) * WNl], pb[wn][:, 0:WNl],
                    xb[par][:, wn * WNl:(wn + 1) * WNl], ALU.add),
                    waits=deps(mm_done[wn], xld, rR.r) if wn == 0 else deps(mm_done[wn]),
                    inc=ve if wn == NW - 1 else None)
            for wn in range(NW):
                pbr[wn].r = rv
            rR.w = rv
            xbr[par].r = rv
            rstd_done = emit_ln(r_sb, rR, xc_sb, xcR, sq_sb, sqR)
            x1v = P.emit("v", lambda e: e.tensor_scalar(
                x1f[:, :], xc_sb[:, :], ln_rstd[:, 0:1], None, ALU.mult),
                waits=deps(rstd_done, x1R.r, x1_store_done), inc=ve)
            x1R.w = x1v
            rstdR.r = x1v
            xcR.r = x1v
            hiv = P.emit("v", lambda e: e.tensor_copy(hi_b[:, :], x1f[:, :]),
                         waits=deps(x1v, hiR.r), inc=ve)
            hiR.w = hiv
            lov = P.emit("v", lambda e: e.tensor_sub(lof[:, :], x1f[:, :], hi_b[:, :]),
                         waits=deps(hiv, lofR.r), inc=ve)
            lofR.w = lov
            hiR.r = lov
            x1_store_done = dma("y", sx1s, x1_d[tt * 128:(tt + 1) * 128, :], x1f[:, :],
                                waits=deps(x1v))
            ev_x_last = None
            lg_last = None
            tpl = None
            for cci in range(CC):
                bx = 2 + cci % 2
                bl = 5 + cci % 2
                tpx = P.emit("t", lambda e, cci=cci, bx=bx: e.transpose(
                    pb[bx][:, 0:128], x1f[:, cci * 128:(cci + 1) * 128], identf[:, :]),
                    waits=deps(x1v, pbr[bx].r), inc=te)
                pbr[bx].w = tpx
                ev_x = P.emit("s", lambda e, cci=cci, par=par, bx=bx: e.copy(
                    x1Ts[par][:, cci * 128:(cci + 1) * 128], pb[bx][:, 0:128]),
                    waits=deps(tpx, x1Tsr[par].r), inc=se)
                pbr[bx].r = ev_x
                ev_x_last = ev_x
                tpl = P.emit("t", lambda e, cci=cci, bl=bl: e.transpose(
                    pb[bl][:, 0:128], lof[:, cci * 128:(cci + 1) * 128], identf[:, :]),
                    waits=deps(lov, pbr[bl].r), inc=te)
                pbr[bl].w = tpl
                ev_l = P.emit("s", lambda e, cci=cci, bl=bl: e.copy(
                    loTs[:, cci * 128:(cci + 1) * 128], pb[bl][:, 0:128]),
                    waits=deps(tpl, loTsr.r if cci == 0 else None), inc=se)
                pbr[bl].r = ev_l
                loTsr.w = ev_l
                for j in range(3):
                    lhs = gwhi if j < 2 else gwlo
                    rhsT = x1Ts[par] if j != 1 else loTs
                    tw = deps(ev_x, ev_l, gwlo_v)
                    if cci == 0 and j == 0:
                        tw = tw + deps(pbr[4].r)
                    lg_last = P.emit("t", lambda e, cci=cci, j=j, lhs=lhs, rhsT=rhsT: e.matmul(
                        pb[4][0:E, 0:128], lhs[:, cci * E:(cci + 1) * E],
                        rhsT[:, cci * 128:(cci + 1) * 128],
                        start=(cci == 0 and j == 0), stop=(cci == CC - 1 and j == 2)),
                        waits=tw, inc=te if (cci == CC - 1 and j == 2) else None)
            pbr[4].w = lg_last
            loTsr.r = lg_last
            lofR.r = tpl
            lgev = P.emit("s", lambda e, tt=tt: e.copy(
                logitsT[:, tt * 128:(tt + 1) * 128], pb[4][0:E, 0:128]),
                waits=deps(lg_last), inc=se)
            pbr[4].r = lgev
            logits_done = lgev
            x1T_store_done = dma("y", sxT[par],
                                 x1T_d.ap().rearrange("(c p) t -> p c t", p=128)[:, :, tt * 128:(tt + 1) * 128],
                                 x1Ts[par][:, :], waits=deps(ev_x_last))
            x1Tsr[par].r = x1T_store_done
            x1R.r = x1T_store_done

        if "C" not in PH:
            P.emit("y", lambda e: e.nop(),
                   waits=[(syT, syT.n), (sx1s, sx1s.n), (sxT[0], sxT[0].n), (sxT[1], sxT[1].n)])
            return _finish(nc, P, block_holder)
        cin_st = dma("g", scc, cc_in[:, :], logitsT[:, :], waits=deps(logits_done))
        P.emit("g", lambda e: e.collective_compute(
            "AllGather", ALU.bypass, replica_groups=[list(range(NCORES))],
            ins=[cc_in.ap().opt()], outs=[cc_out.ap().opt()]),
            waits=deps(cin_st), inc=ccs, incn=1)
        lg_ld = dma("g", slg, lg_all[:, :],
                    cc_out.ap().rearrange("r (h c) -> (r h) c", h=2), waits=[(ccs, 1)])
        P.emit("v", lambda e: e.memset(bis_lo[:, :], -64.0), inc=ve)
        biv = P.emit("v", lambda e: e.memset(bis_hi[:, :], 64.0), inc=ve)
        smid = P.emit("s", lambda e: e.memzero(bis_mid[:, :]), inc=se)
        selv = None
        for i in range(NBIS):
            cmpv = P.emit("v", lambda e: e.tensor_scalar(
                cmpb[:, :], lg_all[:, :], bis_mid[:, 0:1], None, ALU.is_le),
                waits=deps(lg_ld, smid, biv), inc=ve)
            cntv = P.emit("v", lambda e: e.reduce_sum(bis_cnt[:, :], cmpb[:, :], axis=AX.X),
                          waits=deps(cmpv), inc=ve)
            cmm = P.emit("t", lambda e: e.matmul(pb[0][:, 0:1], ones128[:, :], bis_cnt[:, :],
                                                 start=True, stop=True),
                         waits=deps(cntv, pbr[0].r), inc=te)
            pbr[0].w = cmm
            totS = P.emit("s", lambda e: e.copy(bis_tot[:, :], pb[0][:, 0:1]),
                          waits=deps(cmm), inc=se)
            pbr[0].r = totS
            P.emit("v", lambda e: e.tensor_scalar(
                bis_p[:, :], bis_tot[:, :], float(KTH), None, ALU.is_ge),
                waits=deps(totS))
            pv = P.emit("v", lambda e: e.tensor_scalar(
                bis_pi[:, :], bis_tot[:, :], float(KTH), None, ALU.is_lt),
                waits=deps(totS), inc=ve)
            P.emit("v", lambda e: e.select(bis_hi[:, :], bis_p[:, :], bis_mid[:, :], bis_hi[:, :], add_drain=True),
                   waits=deps(pv))
            selv = P.emit("v", lambda e: e.select(bis_lo[:, :], bis_pi[:, :], bis_mid[:, :], bis_lo[:, :], add_drain=True),
                          inc=ve)
            if i < NBIS - 1:
                t1s = P.emit("s", lambda e: e.activation(
                    bis_t1[:, :], bis_lo[:, :], AF.Identity, bias=bis_hi[:, 0:1]),
                    waits=deps(selv), inc=se)
                smid = P.emit("s", lambda e: e.mul(bis_mid[:, :], bis_t1[:, :], 0.5),
                              waits=deps(t1s), inc=se)
        lo2s = P.emit("s", lambda e: e.copy(lo2[:, :], bis_lo[:, :]), waits=deps(selv), inc=se)
        gt1 = P.emit("v", lambda e: e.scalar_tensor_tensor(
            gtmp[:, :], logitsT[:, :], lo2[0:E, 0:1], logitsT[:, :], ALU.is_gt, ALU.mult),
            waits=deps(lo2s, cin_st), inc=ve)
        gatesv = P.emit("v", lambda e: e.tensor_copy(logitsT[:, :], gtmp[:, :]),
                        waits=deps(gt1), inc=ve)
        gtok_state = {"done": None}

        def emit_gtok():
            gts = None
            for tt in range(NKT):
                gtp = P.emit("t", lambda e, tt=tt: e.transpose(
                    pb[1][:, 0:E], logitsT[:, tt * 128:(tt + 1) * 128], identf[0:E, 0:E]),
                    waits=deps(gatesv, pbr[1].r), inc=te)
                pbr[1].w = gtp
                gts = P.emit("s", lambda e, tt=tt: e.copy(g_tok[:, tt * E:(tt + 1) * E], pb[1][:, 0:E]),
                             waits=deps(gtp), inc=se)
                pbr[1].r = gts
            gtok_state["done"] = gts

        if "D" not in PH:
            P.emit("y", lambda e: e.nop(),
                   waits=[(syT, syT.n), (sx1s, sx1s.n), (sxT[0], sxT[0].n), (sxT[1], sxT[1].n),
                          (scc, scc.n), (ccs, 1)])
            return _finish(nc, P, block_holder)
        x1TsbR = Res()
        hR = [Res() for _ in range(NHC)]
        b1R = [Res(), Res()]
        w1R = [Res(), Res()]
        w2R = [Res(), Res()]
        x1rR2 = [Res(), Res()]
        lnxR2 = [Res(), Res()]
        lt0R2 = [Res(), Res()]
        lnmR2 = [Res(), Res()]
        lvsR2 = [Res(), Res()]
        lrsR2 = [Res(), Res()]
        accdep = {}
        out_state = {"st": None}

        def emit_tbend_ln(tb, tt8):
            p8 = tt8 % 2
            gtt = tb * TT8 + tt8
            x1ld = dma("y", sx1r[p8], x1r[p8][:, :], x1_d[gtt * 128:(gtt + 1) * 128, :],
                       waits=deps(x1_store_done, x1rR2[p8].r, x1rR2[p8].w))
            x1rR2[p8].w = x1ld
            rv2 = P.emit("g", lambda e, tt8=tt8, p8=p8: e.tensor_tensor(
                lnx[p8][:, :], accs[:, tt8 * C:(tt8 + 1) * C], x1r[p8][:, :], ALU.add),
                waits=deps(x1ld, lnxR2[p8].r, lnxR2[p8].w) + [accdep[(tt8, d2)] for d2 in range(ND2)],
                inc=ge)
            lnxR2[p8].w = rv2
            x1rR2[p8].r = rv2
            msv = P.emit("s", lambda e, p8=p8: e.activation(
                x1r[p8][:, :], lnx[p8][:, :], AF.Identity, accum_out=lnt2[:, p8:p8 + 1]),
                waits=deps(rv2, x1rR2[p8].w, lt0R2[p8].r), inc=se)
            x1rR2[p8].w = msv
            lt0R2[p8].w = msv
            nmv = P.emit("v", lambda e, p8=p8: e.tensor_scalar(
                lnn2[:, p8:p8 + 1], lnt2[:, p8:p8 + 1], -1.0 / C, None, ALU.mult),
                waits=deps(msv, lnmR2[p8].r), inc=ve)
            lnmR2[p8].w = nmv
            lt0R2[p8].r = nmv
            xcv = P.emit("s", lambda e, p8=p8: e.activation(
                x1r[p8][:, :], lnx[p8][:, :], AF.Identity, bias=lnn2[:, p8:p8 + 1]),
                waits=deps(nmv, msv), inc=se)
            x1rR2[p8].w = xcv
            lnmR2[p8].r = xcv
            lnxR2[p8].r = xcv
            sqv = P.emit("s", lambda e, p8=p8: e.activation(
                lnx[p8][:, :], x1r[p8][:, :], AF.Square, accum_out=lnv2[:, p8:p8 + 1]),
                waits=deps(xcv, lvsR2[p8].r), inc=se)
            lnxR2[p8].w = sqv
            lvsR2[p8].w = sqv
            lnv = P.emit("s", lambda e, p8=p8: e.activation(
                lnt2[:, p8:p8 + 1], lnv2[:, p8:p8 + 1], AF.Ln, scale=1.0 / C, bias=1e-5),
                waits=deps(sqv, lt0R2[p8].r, lt0R2[p8].w), inc=se)
            lt0R2[p8].w = lnv
            lvsR2[p8].r = lnv
            rsv2 = P.emit("s", lambda e, p8=p8: e.activation(
                lnr2[:, p8:p8 + 1], lnt2[:, p8:p8 + 1], AF.Exp, scale=-0.5),
                waits=deps(lnv, lrsR2[p8].r), inc=se)
            lrsR2[p8].w = rsv2
            lt0R2[p8].r = rsv2
            ov = P.emit("v", lambda e, p8=p8: e.tensor_scalar(
                lnx[p8][:, :], x1r[p8][:, :], lnr2[:, p8:p8 + 1], None, ALU.mult),
                waits=deps(rsv2, sqv, out_state["st"]), inc=ve)
            lnxR2[p8].w = ov
            x1rR2[p8].r = ov
            lrsR2[p8].r = ov
            st = dma("y", sout, out_h[gtt * 128:(gtt + 1) * 128, :], lnx[p8][:, :],
                     waits=deps(ov))
            lnxR2[p8].r = st
            out_state["st"] = st
            for d2 in range(ND2):
                accdep[(tt8, d2)] = rv2

        b_done = [logits_done] if logits_done else []
        carry = {}

        for tb in range(NTB):
            bd0 = b_done + [(sxT[0], sxT[0].n), (sxT[1], sxT[1].n)] if tb == 0 else []
            if ("xT", tb) in carry:
                x1Tld = carry.pop(("xT", tb))
            else:
                xw = deps((sxT[0], sxT[0].n), (sxT[1], sxT[1].n), x1TsbR.r, logits_done)
                dma("y", sx1T, x1T_sb[:, 0:(CC // 2) * TB],
                    x1T_d.ap().rearrange("(c p) t -> p c t", p=128)[:, 0:CC // 2, tb * TB:(tb + 1) * TB],
                    waits=xw)
                x1Tld = dma("s", sx1T, x1T_sb[:, (CC // 2) * TB:],
                            x1T_d.ap().rearrange("(c p) t -> p c t", p=128)[:, CC // 2:CC, tb * TB:(tb + 1) * TB],
                            waits=xw)
                x1TsbR.w = x1Tld

            def emit_b2(tb=tb):
                for tt8 in range(TT8):
                    for d2 in range(ND2):
                        bki = 4 + (tt8 * ND2 + d2) % 2
                        bmm = P.emit("t", lambda e, tb=tb, tt8=tt8, d2=d2, bki=bki: e.matmul(
                            pb[bki][:, 0:DW],
                            logitsT[:, (tb * TT8 + tt8) * 128:(tb * TT8 + tt8 + 1) * 128],
                            b2b[:, d2 * DW:(d2 + 1) * DW], start=True, stop=True),
                            waits=deps(gatesv, pbr[bki].r, setup_l2), inc=te)
                        pbr[bki].w = bmm
                        bcv = P.emit("v", lambda e, tt8=tt8, d2=d2, bki=bki: e.tensor_copy(
                            accs[:, tt8 * C + d2 * DW: tt8 * C + (d2 + 1) * DW], pb[bki][:, 0:DW]),
                            waits=deps(bmm, accdep.get((tt8, d2))), inc=ve)
                        pbr[bki].r = bcv
                        accdep[(tt8, d2)] = bcv


            for ei in range(E):
                epar = ei % 2
                if ("b1", tb, ei) in carry:
                    b1ld = carry.pop(("b1", tb, ei))
                    w1ld = carry.pop(("w1", tb, ei))
                else:
                    b1ld = dma("g", sb1[epar], b1e[epar][:, :], b1r_h[ei, :, :],
                               waits=deps(b1R[epar].r) + (bd0 if ei < 2 else []))
                    b1R[epar].w = b1ld
                    w1ld = {}
                    for f in range(min(2, NF1)):
                        w1ld[f] = dma("g", sw1[f % 2], w1b[f % 2][:, :],
                                      w1c_d.ap()[ei].rearrange("(c p) h -> p c h", p=128)[:, :, f * 512:(f + 1) * 512],
                                      waits=deps(w1R[f % 2].r) + (bd0 if ei == 0 else []) + cast_gate)
                for hc in range(NHC):
                    f = hc // 4
                    hpar = hc % 2
                    bks = [hpar * NTS + tsv for tsv in range(NTS)]
                    mm_last = [None] * NTS
                    for dc in range(CC):
                        for tsv in range(NTS):
                            bki = bks[tsv]
                            tw = deps(w1ld[f], x1Tld, pbr[bki].r) if dc == 0 else ()
                            ml = P.emit("t", lambda e, f=f, hc=hc, dc=dc, tsv=tsv, bki=bki: e.matmul(
                                pb[bki][:, 0:TSW],
                                w1b[f % 2][:, dc * 512 + (hc % 4) * 128: dc * 512 + (hc % 4) * 128 + 128],
                                x1T_sb[:, dc * TB + tsv * TSW: dc * TB + (tsv + 1) * TSW],
                                start=(dc == 0), stop=(dc == CC - 1)),
                                waits=tw, inc=te if dc == CC - 1 else None)
                            if dc == CC - 1:
                                mm_last[tsv] = ml
                                pbr[bki].w = ml
                                w1R[f % 2].r = ml
                                x1TsbR.r = ml
                    if hc % 4 == 3 and f + 2 < NF1:
                        w1ld[f + 2] = dma("g", sw1[f % 2], w1b[f % 2][:, :],
                                          w1c_d.ap()[ei].rearrange("(c p) h -> p c h", p=128)[:, :, (f + 2) * 512:(f + 3) * 512],
                                          waits=deps(w1R[f % 2].r) + cast_gate)
                    gv = None
                    for tsv in range(NTS):
                        bki = bks[tsv]
                        gv = P.emit("s", lambda e, hc=hc, tsv=tsv, bki=bki, epar=epar: e.activation(
                            h_sb[:, hc * TB + tsv * TSW: hc * TB + (tsv + 1) * TSW],
                            pb[bki][:, 0:TSW],
                            AF.Gelu_apprx_tanh, bias=b1e[epar][:, hc:hc + 1]),
                            waits=deps(mm_last[tsv], b1R[epar].w, hR[hc].r), inc=se)
                        pbr[bki].r = gv
                    hR[hc].w = gv
                    b1R[epar].r = gv
                if ei == 0:
                    if tb == 0:
                        emit_gtok()
                    emit_b2()
                if ei == E - 1 and tb + 1 < NTB:
                    nxw = deps((sxT[0], sxT[0].n), (sxT[1], sxT[1].n), x1TsbR.r, logits_done)
                    dma("y", sx1T, x1T_sb[:, 0:(CC // 2) * TB],
                        x1T_d.ap().rearrange("(c p) t -> p c t", p=128)[:, 0:CC // 2, (tb + 1) * TB:(tb + 2) * TB],
                        waits=nxw)
                    nxT = dma("s", sx1T, x1T_sb[:, (CC // 2) * TB:],
                              x1T_d.ap().rearrange("(c p) t -> p c t", p=128)[:, CC // 2:CC, (tb + 1) * TB:(tb + 2) * TB],
                              waits=nxw)
                    x1TsbR.w = nxT
                    carry[("xT", tb + 1)] = nxT
                    nb1 = dma("g", sb1[0], b1e[0][:, :], b1r_h[0, :, :],
                              waits=deps(b1R[0].r))
                    b1R[0].w = nb1
                    nw1 = {}
                    for f in range(min(2, NF1)):
                        nw1[f] = dma("g", sw1[f % 2], w1b[f % 2][:, :],
                                     w1c_d.ap()[0].rearrange("(c p) h -> p c h", p=128)[:, :, f * 512:(f + 1) * 512],
                                     waits=deps(w1R[f % 2].r) + cast_gate)
                    carry[("b1", tb + 1, 0)] = nb1
                    carry[("w1", tb + 1, 0)] = nw1
                w2ld = {}
                for g8 in range(min(2, ND2 * NG8)):
                    dh0, g80 = divmod(g8, NG8)
                    w2ld[(dh0, g80)] = dma(
                        "g", sw2[g8 % 2], w2b[g8 % 2][:, :],
                        w2c_d.ap()[ei].rearrange("(g p) d -> p g d", p=128)[:, g80 * GH:(g80 + 1) * GH, dh0 * DW:(dh0 + 1) * DW],
                        waits=deps(w2R[g8 % 2].r) + (bd0 if ei == 0 else []) + cast_gate)
                fidx = 0
                last_w2_mm = None
                for dh in range(ND2):
                    for wv_ in range(1):
                        tts = list(range(TT8))
                        for g8 in range(NG8):
                            bpar = (dh * NG8 + g8) % 2
                            glast = {}
                            for hg in range(GH):
                                hc = g8 * GH + hg
                                for ti, tt8 in enumerate(tts):
                                    bki = (4 + ti) % 8
                                    tw = ()
                                    if hg == 0:
                                        tw = deps(w2ld[(dh, g8)], hR[g8 * GH + GH - 1].w)
                                        if g8 == 0:
                                            tw = tw + deps(pbr[bki].r)
                                    isl = (g8 == NG8 - 1 and hg == GH - 1)
                                    m2 = P.emit("t", lambda e, hc=hc, tt8=tt8, hg=hg, bki=bki, bpar=bpar, g8=g8, isl=isl: e.matmul(
                                        pb[bki][:, 0:DW],
                                        h_sb[:, hc * TB + tt8 * 128: hc * TB + tt8 * 128 + 128],
                                        w2b[bpar][:, hg * DW:(hg + 1) * DW],
                                        start=(g8 == 0 and hg == 0), stop=isl),
                                        waits=tw, inc=te if (isl or hg == GH - 1) else None)
                                    if hg == GH - 1:
                                        glast[ti] = m2
                                    if isl:
                                        pbr[bki].w = m2
                                        last_w2_mm = m2
                            grp_done = glast[len(tts) - 1]
                            w2R[bpar].r = grp_done
                            if True:
                                nidx = dh * NG8 + g8 + 2
                                if nidx < ND2 * NG8:
                                    ndh, ng8 = divmod(nidx, NG8)
                                    w2ld[(ndh, ng8)] = dma(
                                        "g", sw2[nidx % 2], w2b[nidx % 2][:, :],
                                        w2c_d.ap()[ei].rearrange("(g p) d -> p g d", p=128)[:, ng8 * GH:(ng8 + 1) * GH, ndh * DW:(ndh + 1) * DW],
                                        waits=deps(w2R[nidx % 2].r) + cast_gate)
                        for ti, tt8 in enumerate(tts):
                            bki = (4 + ti) % 8
                            gcol = (tb * TT8 + tt8) * E + ei
                            av = P.emit("v", lambda e, tt8=tt8, dh=dh, bki=bki, gcol=gcol: e.scalar_tensor_tensor(
                                accs[:, tt8 * C + dh * DW: tt8 * C + (dh + 1) * DW],
                                pb[bki][:, 0:DW], g_tok[:, gcol:gcol + 1],
                                accs[:, tt8 * C + dh * DW: tt8 * C + (dh + 1) * DW],
                                ALU.mult, ALU.add),
                                waits=deps(pbr[bki].w, accdep.get((tt8, dh)), gtok_state["done"]), inc=ve)
                            pbr[bki].r = av
                            accdep[(tt8, dh)] = av
                            if ei == E - 1 and dh == ND2 - 1:
                                emit_tbend_ln(tb, tt8)
                for hc in range(NHC):
                    hR[hc].r = last_w2_mm

        P.emit("y", lambda e: e.nop(),
               waits=[(syT, syT.n), (sx1s, sx1s.n), (sxT[0], sxT[0].n), (sxT[1], sxT[1].n),
                      (sout, sout.n), (scc, scc.n)])

        nc._sems = dict(te=te.n, ve=ve.n, se=se.n, sld=sld.n, syT=syT.n, sout=sout.n)
        nc._dbg = dict(bis_lo=bis_lo, bis_hi=bis_hi, bis_mid=bis_mid, bis_tot=bis_tot, bis_p=bis_p, bis_cnt=bis_cnt, g_tok=g_tok, logitsT=logitsT,
                       h_sb=h_sb, accs=accs, x1T_sb=x1T_sb, lg_all=lg_all)

        return _finish(nc, P, block_holder)


def _finish(nc, P, ctx):
    with nc.Block() as block:
        @block.tensor
        def _(eng):
            for f in P.ops["t"]:
                f(eng)

        @block.vector
        def _(eng):
            for f in P.ops["v"]:
                f(eng)

        @block.scalar
        def _(eng):
            for f in P.ops["s"]:
                f(eng)

        @block.gpsimd
        def _(eng):
            for f in P.ops["g"]:
                f(eng)

        @block.sync
        def _(eng):
            for f in P.ops["y"]:
                f(eng)
    return nc


def make_in_maps(c, inputs):
    A = lambda z: np.ascontiguousarray(np.asarray(z, dtype=np.float32))
    x_all = A(inputs["L0_attn_x"])
    E, NHC = c["E"], c["NHC"]
    shared = {
        "wq": A(inputs["L0_attn_Wq_down"]),
        "wk": A(inputs["L0_attn_Wk_down"]),
        "wv": A(inputs["L0_attn_Wv_down"]),
        "upcq": A(inputs["L0_attn_Wq_up_c"]),
        "upeq": A(inputs["L0_attn_Wq_up_e"]),
        "upeqT": A(np.asarray(inputs["L0_attn_Wq_up_e"], np.float32).T),
        "upck": A(inputs["L0_attn_Wk_up_c"]),
        "upek": A(inputs["L0_attn_Wk_up_e"]),
        "upekT": A(np.asarray(inputs["L0_attn_Wk_up_e"], np.float32).T),
        "wvup": A(inputs["L0_attn_Wv_up"]),
        "wc": A(inputs["L0_attn_Wc"]),
        "gwT": A(np.asarray(inputs["L0_ffn_gate_weight"], np.float32).T),
        "w1T": A(np.asarray(inputs["L0_ffn_w1"], np.float32).transpose(0, 2, 1)),
        "w2T": A(np.asarray(inputs["L0_ffn_w2"], np.float32).transpose(0, 2, 1)),
        "b1r": A(np.asarray(inputs["L0_ffn_b1"], np.float32).reshape(E, NHC, 128).transpose(0, 2, 1)),
        "b2": A(inputs["L0_ffn_b2"]),
    }
    in_maps = []
    for i in range(c["NCORES"]):
        m = dict(shared)
        m["x"] = A(x_all[i])
        m["xT"] = A(x_all[i].T)
        in_maps.append(m)
    return in_maps


def kernel(**inputs):
    c = make_cfg()
    nc = build_program(c)
    in_maps = make_in_maps(c, inputs)
    res = run_bass_kernel_spmd(nc, in_maps, core_ids=list(range(c["NCORES"])))
    out = np.stack([np.asarray(res.results[i]["out"], dtype=np.float32)
                    for i in range(c["NCORES"])], axis=0)
    return out


# version 14
# speedup vs baseline: 1.0061x; 1.0061x over previous
import numpy as np
import ml_dtypes
import concourse.bass as bass
import concourse.mybir as mybir
from concourse.bass_utils import run_bass_kernel_spmd
from contextlib import ExitStack

F32 = mybir.dt.float32
BF16 = mybir.dt.bfloat16
U32 = mybir.dt.uint32
AF = mybir.ActivationFunctionType
ALU = mybir.AluOpType
AX = mybir.AxisListType

GA = 0.7978845608028654
GB = 0.044715
NEG = -3.0e38
ESZ = {F32: 4, BF16: 2, U32: 4}


def make_cfg(T=4096, C=1024, NH=16, HID=4096, E=8, TB=1024, NBIS=44, KTH=24576, NCORES=8):
    c = dict(T=T, C=C, NH=NH, HID=HID, E=E, TB=TB, NBIS=NBIS, KTH=KTH, NCORES=NCORES)
    c["D"] = 64
    c["DHE"] = 32
    c["NLC"] = NH // 2
    c["CC"] = C // 128
    c["TQ"] = T // 512
    c["NKT"] = T // 128
    c["NHC"] = HID // 128
    c["NTB"] = T // TB
    c["TT8"] = TB // 128
    c["NTS"] = max(1, TB // 512)
    c["TSW"] = TB // c["NTS"]
    c["ND2"] = max(1, C // 512)
    c["DW"] = C // c["ND2"]
    c["GH"] = min(8, c["NHC"])
    c["LGC"] = NCORES * E * T // 128
    assert C == NH * 64 and T % 512 == 0 and TB % 128 == 0 and HID % 512 == 0
    assert c["TSW"] <= 512 and c["DW"] <= 512
    return c


class Sem:
    def __init__(self, nc, ctx, name):
        self.h = ctx.enter_context(nc.semaphore(name))
        self.n = 0


class Res:
    __slots__ = ("w", "r")

    def __init__(self):
        self.w = None
        self.r = None


def deps(*items):
    return [it for it in items if it is not None]


class Prog:
    def __init__(self):
        self.ops = {"t": [], "v": [], "s": [], "g": [], "y": []}
        self.waited = {k: {} for k in self.ops}

    def emit(self, eng, builder, waits=(), inc=None, incn=1):
        seen = {}
        for (s, v) in waits:
            if v and v > 0:
                key = id(s.h)
                if key not in seen or seen[key][1] < v:
                    seen[key] = (s.h, v, s)
        done = self.waited[eng]
        wl = []
        for key, (h, v, s) in seen.items():
            if done.get(key, -1) >= v:
                continue
            done[key] = v
            wl.append((h, v))
        if inc is not None:
            inc.n += incn
        ih = inc.h if inc is not None else None

        def run(e, wl=wl, builder=builder, ih=ih, incn=incn):
            for h, v in wl:
                e.wait_ge(h, v)
            inst = builder(e)
            if ih is not None:
                inst.then_inc(ih, incn)
        self.ops[eng].append(run)
        return (inc, inc.n) if inc is not None else None


def rope_tables(T, DHE):
    inv = 1.0 / (10000.0 ** (np.arange(0, DHE, 2, dtype=np.float64) / DHE))
    ang = np.arange(T, dtype=np.float64)[:, None] * inv[None, :]
    emb = np.concatenate([ang, ang], axis=1)
    cosR = np.tile(np.cos(emb).T, (4, 1)).astype(ml_dtypes.bfloat16)
    sinR = np.tile(np.sin(emb).T, (4, 1)).astype(ml_dtypes.bfloat16)
    return cosR, sinR


def perm_matrix(DHE):
    Pm = np.zeros((DHE, DHE), dtype=np.float32)
    for i in range(DHE // 2):
        Pm[2 * i + 1, 2 * i] = -1.0
        Pm[2 * i, 2 * i + 1] = 1.0
    return Pm.astype(ml_dtypes.bfloat16)


def build_program(c):
    T, C, NH, HID, E, TB = c["T"], c["C"], c["NH"], c["HID"], c["E"], c["TB"]
    D, DHE, NLC, CC, TQ, NKT = c["D"], c["DHE"], c["NLC"], c["CC"], c["TQ"], c["NKT"]
    NHC, NTB, TT8 = c["NHC"], c["NTB"], c["TT8"]
    NTS, TSW, ND2, DW, GH = c["NTS"], c["TSW"], c["ND2"], c["DW"], c["GH"]
    NBIS, KTH, NCORES, LGC = c["NBIS"], c["KTH"], c["NCORES"], c["LGC"]
    UW = TB
    NG8 = NHC // GH
    NF1 = max(1, NHC // 4)
    NWV = (TT8 + 3) // 4

    nc = bass.Bass("TRN2", target_bir_lowering=False, debug=False)

    x_h = nc.dram_tensor("x", [T, C], F32, kind="ExternalInput")
    xT_h = nc.dram_tensor("xT", [C, T], F32, kind="ExternalInput")
    wq_h = nc.dram_tensor("wq", [C, C], F32, kind="ExternalInput")
    wk_h = nc.dram_tensor("wk", [C, C], F32, kind="ExternalInput")
    wv_h = nc.dram_tensor("wv", [C, C], F32, kind="ExternalInput")
    upcq_h = nc.dram_tensor("upcq", [D, DHE], F32, kind="ExternalInput")
    upeq_h = nc.dram_tensor("upeq", [D, DHE], F32, kind="ExternalInput")
    upeqT_h = nc.dram_tensor("upeqT", [DHE, D], F32, kind="ExternalInput")
    upck_h = nc.dram_tensor("upck", [D, DHE], F32, kind="ExternalInput")
    upek_h = nc.dram_tensor("upek", [D, DHE], F32, kind="ExternalInput")
    upekT_h = nc.dram_tensor("upekT", [DHE, D], F32, kind="ExternalInput")
    wvup_h = nc.dram_tensor("wvup", [D, D], F32, kind="ExternalInput")
    wc_h = nc.dram_tensor("wc", [C, C], F32, kind="ExternalInput")
    gwT_h = nc.dram_tensor("gwT", [C, E], F32, kind="ExternalInput")
    w1T_h = nc.dram_tensor("w1T", [E, C, HID], F32, kind="ExternalInput")
    w2T_h = nc.dram_tensor("w2T", [E, HID, C], F32, kind="ExternalInput")
    b1r_h = nc.dram_tensor("b1r", [E, 128, NHC], F32, kind="ExternalInput")
    b2_h = nc.dram_tensor("b2", [E, C], F32, kind="ExternalInput")
    out_h = nc.dram_tensor("out", [T, C], F32, kind="ExternalOutput")

    yT_d = nc.dram_tensor("yT_d", [C, T], BF16)
    x1_d = nc.dram_tensor("x1_d", [T, C], F32)
    x1T_d = nc.dram_tensor("x1T_d", [C, T], BF16)
    cc_in = nc.dram_tensor("cc_in", [E, T], F32)
    cc_out = nc.dram_tensor("cc_out", [NCORES * E, T], F32)
    w1c_d = nc.dram_tensor("w1c_d", [E, C, HID], BF16)
    w2c_d = nc.dram_tensor("w2c_d", [E, HID, C], BF16)

    mdiag_np = np.zeros((128, 128), dtype=np.float32)
    for kp in range(128):
        mdiag_np[kp, :kp] = NEG
    cosR, sinR = rope_tables(T, DHE)
    cpool_np = np.zeros((128, 4), np.float32)
    cpool_np[:, 0] = 1e-5
    consts = {
        "mdiag": mdiag_np,
        "mfull": np.full((128, 128), NEG, np.float32),
        "identf": np.eye(128, dtype=np.float32),
        "ones64c": np.ones((1, D), dtype=ml_dtypes.bfloat16),
        "ones128": np.ones((128, 128), dtype=np.float32),
        "pm": perm_matrix(DHE),
        "cosR": cosR,
        "sinR": sinR,
        "cpool": cpool_np,
    }
    cth = {k: nc.inline_tensor(v, k) for k, v in consts.items()}

    with ExitStack() as ctx:
        ARENA0 = nc.sbuf_base + 1536
        off = [ARENA0]

        def sba(name, shape, dt, at=None):
            fb = int(np.prod(shape[1:])) * ESZ[dt]
            o = (off[0] + 63) // 64 * 64 if at is None else at
            if at is None:
                off[0] = o + fb
            return nc.alloc_sbuf_tensor_at(name, shape, dt, offset=o)

        mdiag = sba("mdiag_sb", [128, 128], F32)
        mfull = sba("mfull_sb", [128, 128], F32)
        identf = sba("identf_sb", [128, 128], F32)
        ones64 = sba("ones64_sb", [1, D], BF16)
        ones128 = sba("ones128_sb", [128, 128], F32)
        cpool = sba("cpool_sb", [128, 4], F32)
        pm = sba("pm_sb", [DHE, DHE], BF16)
        upcq = sba("upcq_sb", [128, DHE], BF16)
        upeq = sba("upeq_sb", [128, DHE], BF16)
        upck = sba("upck_sb", [128, DHE], BF16)
        upek = sba("upek_sb", [128, DHE], BF16)
        upeqT = sba("upeqT_sb", [DHE, D], BF16)
        upekT = sba("upekT_sb", [DHE, D], BF16)
        wvup = sba("wvup_sb", [128, D], BF16)
        wpq = sba("wpq_sb", [128, DHE], BF16)
        wpk = sba("wpk_sb", [128, DHE], BF16)
        gwhi = sba("gwhi_sb", [128, CC * E], BF16)
        gwlo = sba("gwlo_sb", [128, CC * E], BF16)
        gwf = sba("gwf_sb", [128, CC * E], F32)
        logitsT = sba("logitsT_sb", [E, T], F32)
        g_tok = sba("g_tok_sb", [128, NKT * E], F32)
        b2b = sba("b2_sb", [E, C], F32)
        b1e = [sba(f"b1e{i}", [128, NHC], F32) for i in range(2)]
        bis_lo = sba("bis_lo", [128, 1], F32)
        bis_hi = sba("bis_hi", [128, 1], F32)
        bis_mid = sba("bis_mid", [128, 1], F32)
        bis_t1 = sba("bis_t1", [128, 1], F32)
        bis_cnt = sba("bis_cnt", [128, 1], F32)
        bis_tot = sba("bis_tot", [128, 1], F32)
        bis_p = sba("bis_p", [128, 1], U32)
        bis_pi = sba("bis_pi", [128, 1], U32)
        lo2 = sba("lo2_sb", [128, 1], F32)
        ln_nm = sba("ln_nm", [128, 1], F32)
        ln_vs = sba("ln_vs", [128, 1], F32)
        ln_t0 = sba("ln_t0", [128, 1], F32)
        ln_rstd = sba("ln_rstd", [128, 1], F32)
        a0 = (off[0] + 63) // 64 * 64

        off[0] = a0
        xT = sba("xT_sb", [128, CC * T], BF16)
        qT = sba("qT_sb", [128, T], BF16)
        kT = sba("kT_sb", [128, T], BF16)
        yTp = sba("yTp_sb", [128, T], BF16)
        vpair = sba("v_sb", [128, 2 * NKT * (D + 1)], BF16)
        latb = [sba(f"lat{i}", [128, T], BF16) for i in range(2)]
        wqb = [sba(f"wqb{i}", [128, CC * 128], BF16) for i in range(2)]
        wkb = [sba(f"wkb{i}", [128, CC * 128], BF16) for i in range(2)]
        wvb = [sba(f"wvb{i}", [128, CC * 128], BF16) for i in range(2)]
        cos = sba("cos_sb", [128, T], BF16)
        sin = sba("sin_sb", [128, T], BF16)
        expb = [sba(f"expb{i}", [128, 512], BF16) for i in range(4)]
        ysb = [sba(f"ysb{i}", [D + 1, 512], F32) for i in range(2)]
        rcp = [sba(f"rcp{i}", [1, 512], F32) for i in range(2)]
        rcpb = [sba(f"rcpb{i}", [1, 512], BF16) for i in range(2)]
        ropeA = sba("ropeA_sb", [128, 512], F32)
        ropeB = sba("ropeB_sb", [128, 512], F32)
        ropeC = sba("ropeC_sb", [128, 512], F32)
        ropeD = sba("ropeD_sb", [128, 512], F32)
        a_end = off[0]

        off[0] = a0
        wc_sb = sba("wc_sb", [128, CC * C], BF16)
        ytb = [sba(f"ytb{i}", [128, CC * 128], BF16) for i in range(2)]
        xb = [sba(f"xb{i}", [128, C], F32) for i in range(2)]
        r_sb = sba("r_sb", [128, C], F32)
        xc_sb = sba("xc_sb", [128, C], F32)
        sq_sb = sba("sq_sb", [128, C], F32)
        x1f = sba("x1f_sb", [128, C], F32)
        hi_b = sba("hi_sb", [128, C], BF16)
        lof = sba("lof_sb", [128, C], F32)
        x1Ts = [sba(f"x1Ts{i}", [128, CC * 128], BF16) for i in range(2)]
        loTs = sba("loTs_sb", [128, CC * 128], BF16)
        b_end = off[0]

        off[0] = a0
        x1T_sb = sba("x1T_sb", [128, CC * TB], BF16)
        h_sb = sba("h_sb", [128, NHC * TB], BF16)
        accs = sba("acc_sb", [128, TT8 * C], F32)
        w1b = [sba(f"w1b{i}", [128, CC * 512], BF16) for i in range(2)]
        w2b = [sba(f"w2b{i}", [128, GH * DW], BF16) for i in range(2)]
        x1r = [sba(f"x1r{i}_sb", [128, C], F32) for i in range(2)]
        lnx = [sba(f"lnx{i}_sb", [128, C], F32) for i in range(2)]
        lnt2 = sba("lnt2_sb", [128, 2], F32)
        lnn2 = sba("lnn2_sb", [128, 2], F32)
        lnv2 = sba("lnv2_sb", [128, 2], F32)
        lnr2 = sba("lnr2_sb", [128, 2], F32)
        d_end = off[0]
        assert max(a_end, b_end, d_end) < nc.sbuf_top - 64, (a_end, b_end, d_end, nc.sbuf_top)

        x1T_off = (a0 + 63) // 64 * 64
        h_off = (x1T_off + CC * TB * 2 + 63) // 64 * 64
        acc_off = (h_off + NHC * TB * 2 + 63) // 64 * 64
        if LGC * 6 + T * 4 <= TT8 * C * 4:
            lg_all = sba("lg_all_sb", [128, LGC], F32, at=acc_off)
            cmpb = sba("cmpb_sb", [128, LGC], BF16, at=acc_off + LGC * 4)
            gtmp = sba("gtmp_sb", [E, T], F32, at=acc_off + LGC * 6)
        else:
            lg_all = sba("lg_all_sb", [128, LGC], F32)
            cmpb = sba("cmpb_sb", [128, LGC], BF16)
            gtmp = sba("gtmp_sb", [E, T], F32)
            d_end = off[0]
            assert d_end < nc.sbuf_top - 64, d_end

        pb = [ctx.enter_context(nc.psum_tensor(f"pb{i}", [128, 512], F32)) for i in range(8)]
        pbr = [Res() for _ in range(8)]

        sld = Sem(nc, ctx, "sld")
        sl2 = Sem(nc, ctx, "sl2")
        wst = [Sem(nc, ctx, f"wst{i}") for i in range(2)]
        swc = Sem(nc, ctx, "swc")
        syt = [Sem(nc, ctx, f"syt{i}") for i in range(2)]
        sxb = [Sem(nc, ctx, f"sxb{i}") for i in range(2)]
        sx1s = Sem(nc, ctx, "sx1s")
        sxT = [Sem(nc, ctx, f"sxT{i}") for i in range(2)]
        syT = Sem(nc, ctx, "syT")
        scc = Sem(nc, ctx, "scc")
        slg = Sem(nc, ctx, "slg")
        sx1T = Sem(nc, ctx, "sx1T")
        sb1 = [Sem(nc, ctx, f"sb1{i}") for i in range(2)]
        sw1 = [Sem(nc, ctx, f"sw1{i}") for i in range(2)]
        sw2 = [Sem(nc, ctx, f"sw2{i}") for i in range(2)]
        sx1r = [Sem(nc, ctx, "sx1ra"), Sem(nc, ctx, "sx1rb")]
        sout = Sem(nc, ctx, "sout")
        te = Sem(nc, ctx, "te")
        ve = Sem(nc, ctx, "ve")
        se = Sem(nc, ctx, "se")
        ge = Sem(nc, ctx, "ge")
        scst = Sem(nc, ctx, "scst")
        ccs = Sem(nc, ctx, "ccs")

        nc.const_aps.register(F32, 1e-5, cpool[:, 0:1])

        block_holder = ctx

        P = Prog()

        def dma(eng, sem, out_ap, in_ap, waits=()):
            return P.emit(eng, lambda e, o=out_ap, i=in_ap: e.dma_start(out=o, in_=i),
                          waits=waits, inc=sem, incn=16)

        for name, dst in [("mdiag", mdiag), ("mfull", mfull), ("identf", identf),
                          ("ones64c", ones64), ("ones128", ones128), ("cpool", cpool),
                          ("pm", pm), ("cosR", cos), ("sinR", sin)]:
            dma("y", sl2, dst[:, :], cth[name][:, :])
        dma("y", sl2, gwf[:, :], gwT_h.ap().rearrange("(c p) e -> p c e", p=128))
        dma("y", sl2, b2b[:, :], b2_h[:, :])
        setup_l2 = (sl2, sl2.n)

        for dst, src in [(upcq, upcq_h), (upeq, upeq_h), (upck, upck_h), (upek, upek_h),
                         (wvup, wvup_h)]:
            dma("g", sld, dst[0:D, :], src[:, :])
            dma("g", sld, dst[D:128, :], src[:, :])
        dma("g", sld, upeqT[:, :], upeqT_h[:, :])
        dma("g", sld, upekT[:, :], upekT_h[:, :])
        dma("g", sld, gwhi[:, :], gwT_h.ap().rearrange("(c p) e -> p c e", p=128))
        for i in range(CC):
            dma("g", sld, xT[:, i * T:(i + 1) * T],
                xT_h.ap().rearrange("(c p) t -> p c t", p=128)[:, i, :])
        setup_ld = (sld, sld.n)
        pair_w_ld = {}
        for lci in range(min(2, NLC)):
            for wb, wh in ((wqb, wq_h), (wkb, wk_h), (wvb, wv_h)):
                dma("g", wst[lci % 2], wb[lci][:, :],
                    wh.ap().rearrange("(c p) l -> p c l", p=128)[:, :, lci * 128:(lci + 1) * 128])
            pair_w_ld[lci] = (wst[lci % 2], wst[lci % 2].n)

        gwlo_v = P.emit("v", lambda e: e.tensor_sub(gwlo[:, :], gwf[:, :], gwhi[:, :]),
                        waits=[setup_l2, setup_ld], inc=ve)
        nvl = 2 * NKT
        vset = None
        for j in range(nvl):
            vset = P.emit("v", lambda e, j=j: e.memset(
                vpair[:, j * (D + 1) + D: j * (D + 1) + D + 1], 1.0),
                waits=deps(setup_ld) if j == 0 else (),
                inc=ve if j == nvl - 1 else None)

        wp_done = None
        for wp, upT in ((wpq, upeqT), (wpk, upekT)):
            last_mm = None
            for h in range(2):
                last_mm = P.emit("t", lambda e, upT=upT, h=h: e.matmul(
                    pb[6][h * D:h * D + D, 0:DHE], upT[:, :], pm[:, :],
                    start=True, stop=True, tile_position=(0, h * D)),
                    waits=deps(setup_ld, setup_l2, vset, pbr[6].r), inc=te)
            pbr[6].w = last_mm
            wp_done = P.emit("s", lambda e, wp=wp: e.copy(wp[:, :], pb[6][:, 0:DHE]),
                             waits=deps(pbr[6].w), inc=se)
            pbr[6].r = wp_done

        NB4 = min(4, T // 512)
        THN = T // (512 * NB4)
        N8 = T // 512
        latr = [Res(), Res()]
        qTr, kTr, vpr, yTr = Res(), Res(), Res(), Res()
        expr = [Res() for _ in range(4)]
        ysbr = [Res(), Res()]
        ropeAr, ropeBr = Res(), Res()
        wbfree = {0: None, 1: None}
        pend_norm = [None]

        def emit_qb_norm(qb, mmyd):
            for h in range(2):
                yev = P.emit("s", lambda e, h=h: e.copy(ysb[h][:, :], pb[4 + h][0:D + 1, :]),
                             waits=deps(mmyd[h], ysbr[h].r), inc=se)
                ysbr[h].w = yev
                pbr[4 + h].r = yev
                rv = P.emit("v", lambda e, h=h: e.reciprocal(rcp[h][:, :], ysb[h][D:D + 1, :]),
                            waits=deps(yev), inc=ve)
                rbv = P.emit("v", lambda e, h=h: e.tensor_copy(rcpb[h][:, :], rcp[h][:, :]),
                             waits=deps(rv), inc=ve)
                bct = P.emit("t", lambda e, h=h: e.matmul(
                    pb[6 + h][0:D, :], ones64[:, :], rcpb[h][:, :], start=True, stop=True),
                    waits=deps(rbv, pbr[6 + h].r), inc=te)
                pbr[6 + h].w = bct
                ym = P.emit("v", lambda e, h=h, qb=qb: e.tensor_tensor(
                    yTp[h * D:h * D + D, qb * 512:(qb + 1) * 512],
                    ysb[h][0:D, :], pb[6 + h][0:D, 0:512], ALU.mult),
                    waits=deps(bct, yTr.r), inc=ve)
                ysbr[h].r = ym
                pbr[6 + h].r = ym
                yTr.w = ym

        cast_chunks = []
        for ei in range(E):
            cp1 = max(1, C // 128)
            r1 = C // cp1
            for p in range(cp1):
                cast_chunks.append((w1c_d[ei, p * r1:(p + 1) * r1, :],
                                    w1T_h[ei, p * r1:(p + 1) * r1, :]))
        for ei in range(E):
            cp2 = max(1, HID // 512)
            r2 = HID // cp2
            for p in range(cp2):
                cast_chunks.append((w2c_d[ei, p * r2:(p + 1) * r2, :],
                                    w2T_h[ei, p * r2:(p + 1) * r2, :]))

        for lc in range(NLC):
            for wi in range(3):
                wb = (wqb, wkb, wvb)[wi]
                lbi = 0 if wi != 1 else 1
                lbuf = latb[lbi]
                lrs = latr[lbi]
                last_lat_mm = None
                for th in range(THN):
                    bank_done = [None] * NB4
                    for cci in range(CC):
                        for n4 in range(NB4):
                            tw = deps(pbr[n4].r, lrs.r, pair_w_ld.get(lc), setup_ld) if cci == 0 else ()
                            isl = cci == CC - 1
                            bd = P.emit("t", lambda e, wb=wb, lc=lc, cci=cci, th=th, n4=n4, isl=isl: e.matmul(
                                pb[n4][:, :], wb[lc % 2][:, cci * 128:(cci + 1) * 128],
                                xT[:, cci * T + th * 512 * NB4 + n4 * 512:
                                   cci * T + th * 512 * NB4 + (n4 + 1) * 512],
                                start=(cci == 0), stop=isl),
                                waits=tw, inc=te if isl else None)
                            if isl:
                                bank_done[n4] = bd
                    last_lat_mm = bank_done[NB4 - 1]
                    for n4 in range(NB4):
                        pbr[n4].w = bank_done[n4]
                        ev = P.emit("v", lambda e, lbuf=lbuf, th=th, n4=n4: e.tensor_copy(
                            lbuf[:, th * 512 * NB4 + n4 * 512: th * 512 * NB4 + (n4 + 1) * 512],
                            pb[n4][:, :]),
                            waits=deps(bank_done[n4]), inc=ve)
                        pbr[n4].r = ev
                        lrs.w = ev
                wbfree[lc % 2] = last_lat_mm

                if wi < 2:
                    dstT = qT if wi == 0 else kT
                    dstR = qTr if wi == 0 else kTr
                    upc = upcq if wi == 0 else upck
                    upe = upeq if wi == 0 else upek
                    wpp = wpq if wi == 0 else wpk
                    last_up_mm = None
                    rsc = [(ropeA, ropeB, ropeAr), (ropeC, ropeD, ropeBr)]
                    for n8 in range(N8):
                        lo_, hi_ = n8 * 512, (n8 + 1) * 512
                        ba = 4 + 2 * (n8 % 2)
                        bb = ba + 1
                        rA, rB, rR = rsc[n8 % 2]
                        specs = [(upc, 0, ba, 0), (upe, 0, ba, DHE), (wpp, 0, bb, DHE),
                                 (upc, 64, ba, 64), (upe, 64, ba, 64 + DHE), (wpp, 64, bb, 64 + DHE)]
                        mmv = []
                        for si, (wsb, hb, bki, po) in enumerate(specs):
                            tw = deps(lrs.w, pbr[ba].r, pbr[bb].r, wp_done) if si == 0 else ()
                            mmv.append(P.emit("t", lambda e, wsb=wsb, hb=hb, bki=bki, po=po,
                                              lbuf=lbuf, lo_=lo_, hi_=hi_: e.matmul(
                                pb[bki][po:po + DHE, 0:512], wsb[hb:hb + D, :],
                                lbuf[hb:hb + D, lo_:hi_],
                                start=True, stop=True, tile_position=(hb, po)),
                                waits=tw, inc=te))
                        last_up_mm = mmv[-1]
                        lastv = None
                        for h in range(2):
                            hb = 64 * h
                            P.emit("v", lambda e, hb=hb, dstT=dstT, lo_=lo_, hi_=hi_, ba=ba: e.tensor_copy(
                                dstT[hb:hb + DHE, lo_:hi_], pb[ba][hb:hb + DHE, 0:512]),
                                waits=deps(mmv[3 * h + 0], dstR.r, rR.r))
                            P.emit("v", lambda e, hb=hb, lo_=lo_, hi_=hi_, ba=ba, rA=rA: e.tensor_mul(
                                rA[hb + DHE:hb + 2 * DHE, 0:512],
                                pb[ba][hb + DHE:hb + 2 * DHE, 0:512],
                                cos[hb + DHE:hb + 2 * DHE, lo_:hi_]),
                                waits=deps(mmv[3 * h + 1]))
                            t2v = P.emit("v", lambda e, hb=hb, lo_=lo_, hi_=hi_, bb=bb, rB=rB: e.tensor_mul(
                                rB[hb + DHE:hb + 2 * DHE, 0:512],
                                pb[bb][hb + DHE:hb + 2 * DHE, 0:512],
                                sin[hb + DHE:hb + 2 * DHE, lo_:hi_]),
                                waits=deps(mmv[3 * h + 2]), inc=ve)
                            lastv = P.emit("v", lambda e, hb=hb, dstT=dstT, lo_=lo_, hi_=hi_, rA=rA, rB=rB: e.tensor_add(
                                dstT[hb + DHE:hb + 2 * DHE, lo_:hi_],
                                rA[hb + DHE:hb + 2 * DHE, 0:512],
                                rB[hb + DHE:hb + 2 * DHE, 0:512]),
                                waits=deps(t2v), inc=ve)
                        pbr[ba].r = lastv
                        pbr[bb].r = lastv
                        rR.r = lastv
                        dstR.w = lastv
                    lrs.r = last_up_mm
                else:
                    last_v_mm = None
                    for h in range(2):
                        for tt in range(NKT):
                            bki = 6 + (tt % 2)
                            vn = P.emit("t", lambda e, h=h, tt=tt, bki=bki, lbuf=lbuf: e.matmul(
                                pb[bki][:, 0:D], lbuf[h * D:h * D + D, tt * 128:(tt + 1) * 128],
                                wvup[h * D:h * D + D, :], start=True, stop=True,
                                tile_position=(h * D, 0)),
                                waits=deps(lrs.w, pbr[bki].r), inc=te)
                            pbr[bki].w = vn
                            last_v_mm = vn
                            ev = P.emit("v", lambda e, h=h, tt=tt, bki=bki: e.tensor_copy(
                                vpair[:, (h * NKT + tt) * (D + 1):(h * NKT + tt) * (D + 1) + D],
                                pb[bki][:, 0:D]),
                                waits=deps(vn, vpr.r), inc=ve)
                            pbr[bki].r = ev
                            vpr.w = ev
                    lrs.r = last_v_mm

            if lc + 2 < NLC:
                for wb, wh in ((wqb, wq_h), (wkb, wk_h), (wvb, wv_h)):
                    dma("g", wst[lc % 2], wb[lc % 2][:, :],
                        wh.ap().rearrange("(c p) l -> p c l", p=128)[:, :, (lc + 2) * 128:(lc + 3) * 128],
                        waits=deps(wbfree[lc % 2]))
                pair_w_ld[lc + 2] = (wst[lc % 2], wst[lc % 2].n)

            ncst = len(cast_chunks)
            per = -(-ncst // NLC)
            for ck in range(lc * per, min((lc + 1) * per, ncst)):
                dst, srcp = cast_chunks[ck]
                dma("g", scst, dst, srcp)

            last_mm_s = None
            for qb in range(TQ):
                if pend_norm[0] is not None:
                    emit_qb_norm(*pend_norm[0])
                    pend_norm[0] = None
                kcm = 4 * qb + 4
                mmy_done = [None, None]

                def emit_av(kc, h, qb=qb, kcm=kcm):
                    sbi = 2 * h + (kc % 2)
                    ebuf = expb[sbi]
                    sev = expr[sbi].w
                    yw = deps(sev)
                    if kc == 0:
                        yw = deps(sev, pbr[4 + h].r, vpr.w, yTr.r)
                    mmy = P.emit("t", lambda e, h=h, kc=kc, kcm=kcm, ebuf=ebuf: e.matmul(
                        pb[4 + h][0:D + 1, :],
                        vpair[:, (h * NKT + kc) * (D + 1):(h * NKT + kc + 1) * (D + 1)],
                        ebuf[:, :], start=(kc == 0), stop=(kc == kcm - 1)),
                        waits=yw, inc=te)
                    expr[sbi].r = mmy
                    if kc == kcm - 1:
                        pbr[4 + h].w = mmy
                        mmy_done[h] = mmy

                for kc in range(kcm):
                    par = kc % 2
                    for h in range(2):
                        sbi = 2 * h + par
                        mm = P.emit("t", lambda e, h=h, kc=kc, qb=qb, sbi=sbi: e.matmul(
                            pb[sbi][:, :], kT[h * D:h * D + D, kc * 128:(kc + 1) * 128],
                            qT[h * D:h * D + D, qb * 512:(qb + 1) * 512],
                            start=True, stop=True),
                            waits=deps(pbr[sbi].r, qTr.w, kTr.w), inc=te)
                        last_mm_s = mm
                        pbr[sbi].w = mm
                        rel = kc - 4 * qb
                        if rel >= 0:
                            mv = None
                            for tt in range(rel + 1):
                                msk = mfull if tt < rel else mdiag
                                mv = P.emit("v", lambda e, sbi=sbi, tt=tt, msk=msk: e.tensor_tensor(
                                    pb[sbi][:, tt * 128:(tt + 1) * 128],
                                    pb[sbi][:, tt * 128:(tt + 1) * 128], msk[:, :], ALU.add),
                                    waits=deps(mm) if tt == 0 else (),
                                    inc=ve if tt == rel else None)
                            pbr[sbi].w = mv
                        ebuf = expb[sbi]
                        sev = P.emit("s", lambda e, ebuf=ebuf, sbi=sbi: e.activation(
                            ebuf[:, :], pb[sbi][:, :], AF.Exp, scale=0.125),
                            waits=deps(pbr[sbi].w, expr[sbi].r), inc=se)
                        expr[sbi].w = sev
                        pbr[sbi].r = sev
                    if kc > 0:
                        for h in range(2):
                            emit_av(kc - 1, h)
                for h in range(2):
                    emit_av(kcm - 1, h)
                pend_norm[0] = (qb, list(mmy_done))
            if pend_norm[0] is not None:
                emit_qb_norm(*pend_norm[0])
                pend_norm[0] = None
            qTr.r = last_mm_s
            kTr.r = last_mm_s
            st_y = dma("y", syT, yT_d[lc * 128:(lc + 1) * 128, :], yTp[:, :], waits=deps(yTr.w))
            yTr.r = st_y

        a_done = deps(yTr.r)

        PH = c.get("PHASES", "ABCD")
        if "B" not in PH:
            P.emit("y", lambda e: e.nop(), waits=[(syT, syT.n)])
            return _finish(nc, P, block_holder)
        cast_gate = [(scst, scst.n)] if scst.n else []
        dma("g", swc, wc_sb[:, :], wc_h.ap().rearrange("(c p) d -> p c d", p=128), waits=a_done)
        wc_ld = (swc, swc.n)
        ytbr = [Res(), Res()]
        xbr = [Res(), Res()]
        x1Tsr = [Res(), Res()]
        loTsr = Res()
        rR, xcR, sqR, x1R, hiR, lofR = Res(), Res(), Res(), Res(), Res(), Res()
        nmR, vsR, t0R, rstdR = Res(), Res(), Res(), Res()
        logits_done = None
        x1_store_done = None
        x1T_store_done = None
        NW = max(1, C // 512)
        WNl = C // NW

        def emit_ln(srcT, srcRes, xcT, xcRes, sqT, sqRes):
            rsv = P.emit("v", lambda e, srcT=srcT: e.reduce_sum(ln_t0[:, :], srcT[:, 0:C], axis=AX.X),
                         waits=deps(srcRes.w, t0R.r, t0R.w), inc=ve)
            t0R.w = rsv
            nmv = P.emit("v", lambda e: e.tensor_scalar(
                ln_nm[:, :], ln_t0[:, :], -1.0 / C, None, ALU.mult),
                waits=deps(rsv, nmR.r), inc=ve)
            nmR.w = nmv
            xcv = P.emit("s", lambda e, srcT=srcT, xcT=xcT: e.activation(
                xcT[:, 0:C], srcT[:, 0:C], AF.Identity, bias=ln_nm[:, 0:1]),
                waits=deps(nmv, xcRes.r, xcRes.w), inc=se)
            xcRes.w = xcv
            nmR.r = xcv
            srcRes.r = xcv
            sqv = P.emit("s", lambda e, xcT=xcT, sqT=sqT: e.activation(
                sqT[:, 0:C], xcT[:, 0:C], AF.Square, accum_out=ln_vs[:, 0:1]),
                waits=deps(xcv, sqRes.r, sqRes.w, vsR.r), inc=se)
            sqRes.w = sqv
            vsR.w = sqv
            lnv = P.emit("s", lambda e: e.activation(
                ln_t0[:, :], ln_vs[:, :], AF.Ln, scale=1.0 / C, bias=1e-5),
                waits=deps(sqv, nmv, t0R.r), inc=se)
            t0R.w = lnv
            vsR.r = lnv
            rsv2 = P.emit("s", lambda e: e.activation(
                ln_rstd[:, :], ln_t0[:, :], AF.Exp, scale=-0.5),
                waits=deps(lnv, rstdR.r), inc=se)
            rstdR.w = rsv2
            t0R.r = rsv2
            return rsv2

        for tt in range(NKT):
            par = tt % 2
            ytld = dma("y", syt[par], ytb[par][:, :],
                       yT_d.ap().rearrange("(c p) t -> p c t", p=128)[:, :, tt * 128:(tt + 1) * 128],
                       waits=deps(ytbr[par].r) + (a_done if tt < 2 else []))
            ytbr[par].w = ytld
            xld = dma("y", sxb[par], xb[par][:, :], x_h[tt * 128:(tt + 1) * 128, :],
                      waits=deps(xbr[par].r) + (a_done if tt < 2 else []))
            xbr[par].w = xld
            mm_done = [None] * NW
            for cci in range(CC):
                for wn in range(NW):
                    tw = deps(ytld, wc_ld, pbr[wn].r) if cci == 0 else ()
                    md = P.emit("t", lambda e, cci=cci, wn=wn, par=par: e.matmul(
                        pb[wn][:, 0:WNl], ytb[par][:, cci * 128:(cci + 1) * 128],
                        wc_sb[:, cci * C + wn * WNl: cci * C + (wn + 1) * WNl],
                        start=(cci == 0), stop=(cci == CC - 1)),
                        waits=tw, inc=te if cci == CC - 1 else None)
                    if cci == CC - 1:
                        mm_done[wn] = md
                        pbr[wn].w = md
            rv = None
            for wn in range(NW):
                rv = P.emit("v", lambda e, wn=wn, par=par: e.tensor_tensor(
                    r_sb[:, wn * WNl:(wn + 1) * WNl], pb[wn][:, 0:WNl],
                    xb[par][:, wn * WNl:(wn + 1) * WNl], ALU.add),
                    waits=deps(mm_done[wn], xld, rR.r) if wn == 0 else deps(mm_done[wn]),
                    inc=ve if wn == NW - 1 else None)
            for wn in range(NW):
                pbr[wn].r = rv
            rR.w = rv
            xbr[par].r = rv
            rstd_done = emit_ln(r_sb, rR, xc_sb, xcR, sq_sb, sqR)
            x1v = P.emit("v", lambda e: e.tensor_scalar(
                x1f[:, :], xc_sb[:, :], ln_rstd[:, 0:1], None, ALU.mult),
                waits=deps(rstd_done, x1R.r, x1_store_done), inc=ve)
            x1R.w = x1v
            rstdR.r = x1v
            xcR.r = x1v
            hiv = P.emit("v", lambda e: e.tensor_copy(hi_b[:, :], x1f[:, :]),
                         waits=deps(x1v, hiR.r), inc=ve)
            hiR.w = hiv
            lov = P.emit("v", lambda e: e.tensor_sub(lof[:, :], x1f[:, :], hi_b[:, :]),
                         waits=deps(hiv, lofR.r), inc=ve)
            lofR.w = lov
            hiR.r = lov
            x1_store_done = dma("y", sx1s, x1_d[tt * 128:(tt + 1) * 128, :], x1f[:, :],
                                waits=deps(x1v))
            ev_x_last = None
            lg_last = None
            tpl = None
            for cci in range(CC):
                bx = 2 + cci % 2
                bl = 5 + cci % 2
                tpx = P.emit("t", lambda e, cci=cci, bx=bx: e.transpose(
                    pb[bx][:, 0:128], x1f[:, cci * 128:(cci + 1) * 128], identf[:, :]),
                    waits=deps(x1v, pbr[bx].r), inc=te)
                pbr[bx].w = tpx
                ev_x = P.emit("s", lambda e, cci=cci, par=par, bx=bx: e.copy(
                    x1Ts[par][:, cci * 128:(cci + 1) * 128], pb[bx][:, 0:128]),
                    waits=deps(tpx, x1Tsr[par].r), inc=se)
                pbr[bx].r = ev_x
                ev_x_last = ev_x
                tpl = P.emit("t", lambda e, cci=cci, bl=bl: e.transpose(
                    pb[bl][:, 0:128], lof[:, cci * 128:(cci + 1) * 128], identf[:, :]),
                    waits=deps(lov, pbr[bl].r), inc=te)
                pbr[bl].w = tpl
                ev_l = P.emit("s", lambda e, cci=cci, bl=bl: e.copy(
                    loTs[:, cci * 128:(cci + 1) * 128], pb[bl][:, 0:128]),
                    waits=deps(tpl, loTsr.r if cci == 0 else None), inc=se)
                pbr[bl].r = ev_l
                loTsr.w = ev_l
                for j in range(3):
                    lhs = gwhi if j < 2 else gwlo
                    rhsT = x1Ts[par] if j != 1 else loTs
                    tw = deps(ev_x, ev_l, gwlo_v)
                    if cci == 0 and j == 0:
                        tw = tw + deps(pbr[4].r)
                    lg_last = P.emit("t", lambda e, cci=cci, j=j, lhs=lhs, rhsT=rhsT: e.matmul(
                        pb[4][0:E, 0:128], lhs[:, cci * E:(cci + 1) * E],
                        rhsT[:, cci * 128:(cci + 1) * 128],
                        start=(cci == 0 and j == 0), stop=(cci == CC - 1 and j == 2)),
                        waits=tw, inc=te if (cci == CC - 1 and j == 2) else None)
            pbr[4].w = lg_last
            loTsr.r = lg_last
            lofR.r = tpl
            lgev = P.emit("s", lambda e, tt=tt: e.copy(
                logitsT[:, tt * 128:(tt + 1) * 128], pb[4][0:E, 0:128]),
                waits=deps(lg_last), inc=se)
            pbr[4].r = lgev
            logits_done = lgev
            x1T_store_done = dma("y", sxT[par],
                                 x1T_d.ap().rearrange("(c p) t -> p c t", p=128)[:, :, tt * 128:(tt + 1) * 128],
                                 x1Ts[par][:, :], waits=deps(ev_x_last))
            x1Tsr[par].r = x1T_store_done
            x1R.r = x1T_store_done

        if "C" not in PH:
            P.emit("y", lambda e: e.nop(),
                   waits=[(syT, syT.n), (sx1s, sx1s.n), (sxT[0], sxT[0].n), (sxT[1], sxT[1].n)])
            return _finish(nc, P, block_holder)
        cin_st = dma("g", scc, cc_in[:, :], logitsT[:, :], waits=deps(logits_done))
        P.emit("g", lambda e: e.collective_compute(
            "AllGather", ALU.bypass, replica_groups=[list(range(NCORES))],
            ins=[cc_in.ap().opt()], outs=[cc_out.ap().opt()]),
            waits=deps(cin_st), inc=ccs, incn=1)
        lg_ld = dma("g", slg, lg_all[:, :],
                    cc_out.ap().rearrange("r (h c) -> (r h) c", h=2), waits=[(ccs, 1)])
        P.emit("v", lambda e: e.memset(bis_lo[:, :], -64.0), inc=ve)
        biv = P.emit("v", lambda e: e.memset(bis_hi[:, :], 64.0), inc=ve)
        smid = P.emit("s", lambda e: e.memzero(bis_mid[:, :]), inc=se)
        selv = None
        for i in range(NBIS):
            cmpv = P.emit("v", lambda e: e.tensor_scalar(
                cmpb[:, :], lg_all[:, :], bis_mid[:, 0:1], None, ALU.is_le),
                waits=deps(lg_ld, smid, biv), inc=ve)
            cntv = P.emit("v", lambda e: e.reduce_sum(bis_cnt[:, :], cmpb[:, :], axis=AX.X),
                          waits=deps(cmpv), inc=ve)
            cmm = P.emit("t", lambda e: e.matmul(pb[0][:, 0:1], ones128[:, :], bis_cnt[:, :],
                                                 start=True, stop=True),
                         waits=deps(cntv, pbr[0].r), inc=te)
            pbr[0].w = cmm
            totS = P.emit("s", lambda e: e.copy(bis_tot[:, :], pb[0][:, 0:1]),
                          waits=deps(cmm), inc=se)
            pbr[0].r = totS
            P.emit("v", lambda e: e.tensor_scalar(
                bis_p[:, :], bis_tot[:, :], float(KTH), None, ALU.is_ge),
                waits=deps(totS))
            pv = P.emit("v", lambda e: e.tensor_scalar(
                bis_pi[:, :], bis_tot[:, :], float(KTH), None, ALU.is_lt),
                waits=deps(totS), inc=ve)
            P.emit("v", lambda e: e.select(bis_hi[:, :], bis_p[:, :], bis_mid[:, :], bis_hi[:, :], add_drain=True),
                   waits=deps(pv))
            selv = P.emit("v", lambda e: e.select(bis_lo[:, :], bis_pi[:, :], bis_mid[:, :], bis_lo[:, :], add_drain=True),
                          inc=ve)
            if i < NBIS - 1:
                t1s = P.emit("s", lambda e: e.activation(
                    bis_t1[:, :], bis_lo[:, :], AF.Identity, bias=bis_hi[:, 0:1]),
                    waits=deps(selv), inc=se)
                smid = P.emit("s", lambda e: e.mul(bis_mid[:, :], bis_t1[:, :], 0.5),
                              waits=deps(t1s), inc=se)
        lo2s = P.emit("s", lambda e: e.copy(lo2[:, :], bis_lo[:, :]), waits=deps(selv), inc=se)
        gt1 = P.emit("v", lambda e: e.scalar_tensor_tensor(
            gtmp[:, :], logitsT[:, :], lo2[0:E, 0:1], logitsT[:, :], ALU.is_gt, ALU.mult),
            waits=deps(lo2s, cin_st), inc=ve)
        gatesv = P.emit("v", lambda e: e.tensor_copy(logitsT[:, :], gtmp[:, :]),
                        waits=deps(gt1), inc=ve)
        gtok_state = {"done": None}

        def emit_gtok():
            gts = None
            for tt in range(NKT):
                gtp = P.emit("t", lambda e, tt=tt: e.transpose(
                    pb[1][:, 0:E], logitsT[:, tt * 128:(tt + 1) * 128], identf[0:E, 0:E]),
                    waits=deps(gatesv, pbr[1].r), inc=te)
                pbr[1].w = gtp
                gts = P.emit("s", lambda e, tt=tt: e.copy(g_tok[:, tt * E:(tt + 1) * E], pb[1][:, 0:E]),
                             waits=deps(gtp), inc=se)
                pbr[1].r = gts
            gtok_state["done"] = gts

        if "D" not in PH:
            P.emit("y", lambda e: e.nop(),
                   waits=[(syT, syT.n), (sx1s, sx1s.n), (sxT[0], sxT[0].n), (sxT[1], sxT[1].n),
                          (scc, scc.n), (ccs, 1)])
            return _finish(nc, P, block_holder)
        x1TsbR = Res()
        hR = [Res() for _ in range(NHC)]
        b1R = [Res(), Res()]
        w1R = [Res(), Res()]
        w2R = [Res(), Res()]
        x1rR2 = [Res(), Res()]
        lnxR2 = [Res(), Res()]
        lt0R2 = [Res(), Res()]
        lnmR2 = [Res(), Res()]
        lvsR2 = [Res(), Res()]
        lrsR2 = [Res(), Res()]
        accdep = {}
        out_state = {"st": None}

        def emit_tbend_ln(tb, tt8):
            p8 = tt8 % 2
            gtt = tb * TT8 + tt8
            x1ld = dma("y", sx1r[p8], x1r[p8][:, :], x1_d[gtt * 128:(gtt + 1) * 128, :],
                       waits=deps(x1_store_done, x1rR2[p8].r, x1rR2[p8].w))
            x1rR2[p8].w = x1ld
            rv2 = P.emit("g", lambda e, tt8=tt8, p8=p8: e.tensor_tensor(
                lnx[p8][:, :], accs[:, tt8 * C:(tt8 + 1) * C], x1r[p8][:, :], ALU.add),
                waits=deps(x1ld, lnxR2[p8].r, lnxR2[p8].w) + [accdep[(tt8, d2)] for d2 in range(ND2)],
                inc=ge)
            lnxR2[p8].w = rv2
            x1rR2[p8].r = rv2
            msv = P.emit("s", lambda e, p8=p8: e.activation(
                x1r[p8][:, :], lnx[p8][:, :], AF.Identity, accum_out=lnt2[:, p8:p8 + 1]),
                waits=deps(rv2, x1rR2[p8].w, lt0R2[p8].r), inc=se)
            x1rR2[p8].w = msv
            lt0R2[p8].w = msv
            nmv = P.emit("v", lambda e, p8=p8: e.tensor_scalar(
                lnn2[:, p8:p8 + 1], lnt2[:, p8:p8 + 1], -1.0 / C, None, ALU.mult),
                waits=deps(msv, lnmR2[p8].r), inc=ve)
            lnmR2[p8].w = nmv
            lt0R2[p8].r = nmv
            xcv = P.emit("s", lambda e, p8=p8: e.activation(
                x1r[p8][:, :], lnx[p8][:, :], AF.Identity, bias=lnn2[:, p8:p8 + 1]),
                waits=deps(nmv, msv), inc=se)
            x1rR2[p8].w = xcv
            lnmR2[p8].r = xcv
            lnxR2[p8].r = xcv
            sqv = P.emit("s", lambda e, p8=p8: e.activation(
                lnx[p8][:, :], x1r[p8][:, :], AF.Square, accum_out=lnv2[:, p8:p8 + 1]),
                waits=deps(xcv, lvsR2[p8].r), inc=se)
            lnxR2[p8].w = sqv
            lvsR2[p8].w = sqv
            lnv = P.emit("s", lambda e, p8=p8: e.activation(
                lnt2[:, p8:p8 + 1], lnv2[:, p8:p8 + 1], AF.Ln, scale=1.0 / C, bias=1e-5),
                waits=deps(sqv, lt0R2[p8].r, lt0R2[p8].w), inc=se)
            lt0R2[p8].w = lnv
            lvsR2[p8].r = lnv
            rsv2 = P.emit("s", lambda e, p8=p8: e.activation(
                lnr2[:, p8:p8 + 1], lnt2[:, p8:p8 + 1], AF.Exp, scale=-0.5),
                waits=deps(lnv, lrsR2[p8].r), inc=se)
            lrsR2[p8].w = rsv2
            lt0R2[p8].r = rsv2
            ov = P.emit("v", lambda e, p8=p8: e.tensor_scalar(
                lnx[p8][:, :], x1r[p8][:, :], lnr2[:, p8:p8 + 1], None, ALU.mult),
                waits=deps(rsv2, sqv, out_state["st"]), inc=ve)
            lnxR2[p8].w = ov
            x1rR2[p8].r = ov
            lrsR2[p8].r = ov
            st = dma("y", sout, out_h[gtt * 128:(gtt + 1) * 128, :], lnx[p8][:, :],
                     waits=deps(ov))
            lnxR2[p8].r = st
            out_state["st"] = st
            for d2 in range(ND2):
                accdep[(tt8, d2)] = rv2

        b_done = [logits_done] if logits_done else []
        carry = {}

        for tb in range(NTB):
            bd0 = b_done + [(sxT[0], sxT[0].n), (sxT[1], sxT[1].n)] if tb == 0 else []
            if ("xT", tb) in carry:
                x1Tld = carry.pop(("xT", tb))
            else:
                xw = deps((sxT[0], sxT[0].n), (sxT[1], sxT[1].n), x1TsbR.r, logits_done)
                dma("y", sx1T, x1T_sb[:, 0:(CC // 2) * TB],
                    x1T_d.ap().rearrange("(c p) t -> p c t", p=128)[:, 0:CC // 2, tb * TB:(tb + 1) * TB],
                    waits=xw)
                x1Tld = dma("s", sx1T, x1T_sb[:, (CC // 2) * TB:],
                            x1T_d.ap().rearrange("(c p) t -> p c t", p=128)[:, CC // 2:CC, tb * TB:(tb + 1) * TB],
                            waits=xw)
                x1TsbR.w = x1Tld

            def emit_b2(tb=tb):
                for tt8 in range(TT8):
                    for d2 in range(ND2):
                        bki = 4 + (tt8 * ND2 + d2) % 2
                        bmm = P.emit("t", lambda e, tb=tb, tt8=tt8, d2=d2, bki=bki: e.matmul(
                            pb[bki][:, 0:DW],
                            logitsT[:, (tb * TT8 + tt8) * 128:(tb * TT8 + tt8 + 1) * 128],
                            b2b[:, d2 * DW:(d2 + 1) * DW], start=True, stop=True),
                            waits=deps(gatesv, pbr[bki].r, setup_l2), inc=te)
                        pbr[bki].w = bmm
                        bcv = P.emit("v", lambda e, tt8=tt8, d2=d2, bki=bki: e.tensor_copy(
                            accs[:, tt8 * C + d2 * DW: tt8 * C + (d2 + 1) * DW], pb[bki][:, 0:DW]),
                            waits=deps(bmm, accdep.get((tt8, d2))), inc=ve)
                        pbr[bki].r = bcv
                        accdep[(tt8, d2)] = bcv


            for ei in range(E):
                epar = ei % 2
                if ("b1", tb, ei) in carry:
                    b1ld = carry.pop(("b1", tb, ei))
                    w1ld = carry.pop(("w1", tb, ei))
                else:
                    b1ld = dma("g", sb1[epar], b1e[epar][:, :], b1r_h[ei, :, :],
                               waits=deps(b1R[epar].r) + (bd0 if ei < 2 else []))
                    b1R[epar].w = b1ld
                    w1ld = {}
                    for f in range(min(2, NF1)):
                        w1ld[f] = dma("g", sw1[f % 2], w1b[f % 2][:, :],
                                      w1c_d.ap()[ei].rearrange("(c p) h -> p c h", p=128)[:, :, f * 512:(f + 1) * 512],
                                      waits=deps(w1R[f % 2].r) + (bd0 if ei == 0 else []) + cast_gate)
                for hc in range(NHC):
                    f = hc // 4
                    hpar = hc % 2
                    bks = [hpar * NTS + tsv for tsv in range(NTS)]
                    mm_last = [None] * NTS
                    for dc in range(CC):
                        for tsv in range(NTS):
                            bki = bks[tsv]
                            tw = deps(w1ld[f], x1Tld, pbr[bki].r) if dc == 0 else ()
                            ml = P.emit("t", lambda e, f=f, hc=hc, dc=dc, tsv=tsv, bki=bki: e.matmul(
                                pb[bki][:, 0:TSW],
                                w1b[f % 2][:, dc * 512 + (hc % 4) * 128: dc * 512 + (hc % 4) * 128 + 128],
                                x1T_sb[:, dc * TB + tsv * TSW: dc * TB + (tsv + 1) * TSW],
                                start=(dc == 0), stop=(dc == CC - 1)),
                                waits=tw, inc=te if dc == CC - 1 else None)
                            if dc == CC - 1:
                                mm_last[tsv] = ml
                                pbr[bki].w = ml
                                w1R[f % 2].r = ml
                                x1TsbR.r = ml
                    if hc % 4 == 3 and f + 2 < NF1:
                        w1ld[f + 2] = dma("g", sw1[f % 2], w1b[f % 2][:, :],
                                          w1c_d.ap()[ei].rearrange("(c p) h -> p c h", p=128)[:, :, (f + 2) * 512:(f + 3) * 512],
                                          waits=deps(w1R[f % 2].r) + cast_gate)
                    gv = None
                    for tsv in range(NTS):
                        bki = bks[tsv]
                        gv = P.emit("s", lambda e, hc=hc, tsv=tsv, bki=bki, epar=epar: e.activation(
                            h_sb[:, hc * TB + tsv * TSW: hc * TB + (tsv + 1) * TSW],
                            pb[bki][:, 0:TSW],
                            AF.Gelu_apprx_tanh, bias=b1e[epar][:, hc:hc + 1]),
                            waits=deps(mm_last[tsv], b1R[epar].w, hR[hc].r), inc=se)
                        pbr[bki].r = gv
                    hR[hc].w = gv
                    b1R[epar].r = gv
                if ei == 0:
                    if tb == 0:
                        emit_gtok()
                    emit_b2()
                if ei == E - 1 and tb + 1 < NTB:
                    nxw = deps((sxT[0], sxT[0].n), (sxT[1], sxT[1].n), x1TsbR.r, logits_done)
                    dma("y", sx1T, x1T_sb[:, 0:(CC // 2) * TB],
                        x1T_d.ap().rearrange("(c p) t -> p c t", p=128)[:, 0:CC // 2, (tb + 1) * TB:(tb + 2) * TB],
                        waits=nxw)
                    nxT = dma("s", sx1T, x1T_sb[:, (CC // 2) * TB:],
                              x1T_d.ap().rearrange("(c p) t -> p c t", p=128)[:, CC // 2:CC, (tb + 1) * TB:(tb + 2) * TB],
                              waits=nxw)
                    x1TsbR.w = nxT
                    carry[("xT", tb + 1)] = nxT
                    nb1 = dma("g", sb1[0], b1e[0][:, :], b1r_h[0, :, :],
                              waits=deps(b1R[0].r))
                    b1R[0].w = nb1
                    nw1 = {}
                    for f in range(min(2, NF1)):
                        nw1[f] = dma("g", sw1[f % 2], w1b[f % 2][:, :],
                                     w1c_d.ap()[0].rearrange("(c p) h -> p c h", p=128)[:, :, f * 512:(f + 1) * 512],
                                     waits=deps(w1R[f % 2].r) + cast_gate)
                    carry[("b1", tb + 1, 0)] = nb1
                    carry[("w1", tb + 1, 0)] = nw1
                w2ld = {}
                for g8 in range(min(2, ND2 * NG8)):
                    dh0, g80 = divmod(g8, NG8)
                    w2ld[(dh0, g80)] = dma(
                        "g", sw2[g8 % 2], w2b[g8 % 2][:, :],
                        w2c_d.ap()[ei].rearrange("(g p) d -> p g d", p=128)[:, g80 * GH:(g80 + 1) * GH, dh0 * DW:(dh0 + 1) * DW],
                        waits=deps(w2R[g8 % 2].r) + (bd0 if ei == 0 else []) + cast_gate)
                fidx = 0
                last_w2_mm = None
                for dh in range(ND2):
                    for wv_ in range(1):
                        tts = list(range(TT8))
                        for g8 in range(NG8):
                            bpar = (dh * NG8 + g8) % 2
                            glast = {}
                            for hg in range(GH):
                                hc = g8 * GH + hg
                                for ti, tt8 in enumerate(tts):
                                    bki = (4 + ti) % 8
                                    tw = ()
                                    if hg == 0:
                                        tw = deps(w2ld[(dh, g8)], hR[g8 * GH + GH - 1].w)
                                        if g8 == 0:
                                            tw = tw + deps(pbr[bki].r)
                                    isl = (g8 == NG8 - 1 and hg == GH - 1)
                                    m2 = P.emit("t", lambda e, hc=hc, tt8=tt8, hg=hg, bki=bki, bpar=bpar, g8=g8, isl=isl: e.matmul(
                                        pb[bki][:, 0:DW],
                                        h_sb[:, hc * TB + tt8 * 128: hc * TB + tt8 * 128 + 128],
                                        w2b[bpar][:, hg * DW:(hg + 1) * DW],
                                        start=(g8 == 0 and hg == 0), stop=isl),
                                        waits=tw, inc=te if (isl or hg == GH - 1) else None)
                                    if hg == GH - 1:
                                        glast[ti] = m2
                                    if isl:
                                        pbr[bki].w = m2
                                        last_w2_mm = m2
                            grp_done = glast[len(tts) - 1]
                            w2R[bpar].r = grp_done
                            if True:
                                nidx = dh * NG8 + g8 + 2
                                if nidx < ND2 * NG8:
                                    ndh, ng8 = divmod(nidx, NG8)
                                    w2ld[(ndh, ng8)] = dma(
                                        "g", sw2[nidx % 2], w2b[nidx % 2][:, :],
                                        w2c_d.ap()[ei].rearrange("(g p) d -> p g d", p=128)[:, ng8 * GH:(ng8 + 1) * GH, ndh * DW:(ndh + 1) * DW],
                                        waits=deps(w2R[nidx % 2].r) + cast_gate)
                        for ti, tt8 in enumerate(tts):
                            bki = (4 + ti) % 8
                            gcol = (tb * TT8 + tt8) * E + ei
                            av = P.emit("v", lambda e, tt8=tt8, dh=dh, bki=bki, gcol=gcol: e.scalar_tensor_tensor(
                                accs[:, tt8 * C + dh * DW: tt8 * C + (dh + 1) * DW],
                                pb[bki][:, 0:DW], g_tok[:, gcol:gcol + 1],
                                accs[:, tt8 * C + dh * DW: tt8 * C + (dh + 1) * DW],
                                ALU.mult, ALU.add),
                                waits=deps(pbr[bki].w, accdep.get((tt8, dh)), gtok_state["done"]), inc=ve)
                            pbr[bki].r = av
                            accdep[(tt8, dh)] = av
                            if ei == E - 1 and dh == ND2 - 1:
                                emit_tbend_ln(tb, tt8)
                for hc in range(NHC):
                    hR[hc].r = last_w2_mm

        P.emit("y", lambda e: e.nop(),
               waits=[(syT, syT.n), (sx1s, sx1s.n), (sxT[0], sxT[0].n), (sxT[1], sxT[1].n),
                      (sout, sout.n), (scc, scc.n)])

        nc._sems = dict(te=te.n, ve=ve.n, se=se.n, sld=sld.n, syT=syT.n, sout=sout.n)
        nc._dbg = dict(bis_lo=bis_lo, bis_hi=bis_hi, bis_mid=bis_mid, bis_tot=bis_tot, bis_p=bis_p, bis_cnt=bis_cnt, g_tok=g_tok, logitsT=logitsT,
                       h_sb=h_sb, accs=accs, x1T_sb=x1T_sb, lg_all=lg_all)

        return _finish(nc, P, block_holder)


def _finish(nc, P, ctx):
    with nc.Block() as block:
        @block.tensor
        def _(eng):
            for f in P.ops["t"]:
                f(eng)

        @block.vector
        def _(eng):
            for f in P.ops["v"]:
                f(eng)

        @block.scalar
        def _(eng):
            for f in P.ops["s"]:
                f(eng)

        @block.gpsimd
        def _(eng):
            for f in P.ops["g"]:
                f(eng)

        @block.sync
        def _(eng):
            for f in P.ops["y"]:
                f(eng)
    return nc


def make_in_maps(c, inputs):
    A = lambda z: np.ascontiguousarray(np.asarray(z, dtype=np.float32))
    x_all = A(inputs["L0_attn_x"])
    E, NHC = c["E"], c["NHC"]
    shared = {
        "wq": A(inputs["L0_attn_Wq_down"]),
        "wk": A(inputs["L0_attn_Wk_down"]),
        "wv": A(inputs["L0_attn_Wv_down"]),
        "upcq": A(inputs["L0_attn_Wq_up_c"]),
        "upeq": A(inputs["L0_attn_Wq_up_e"]),
        "upeqT": A(np.asarray(inputs["L0_attn_Wq_up_e"], np.float32).T),
        "upck": A(inputs["L0_attn_Wk_up_c"]),
        "upek": A(inputs["L0_attn_Wk_up_e"]),
        "upekT": A(np.asarray(inputs["L0_attn_Wk_up_e"], np.float32).T),
        "wvup": A(inputs["L0_attn_Wv_up"]),
        "wc": A(inputs["L0_attn_Wc"]),
        "gwT": A(np.asarray(inputs["L0_ffn_gate_weight"], np.float32).T),
        "w1T": A(np.asarray(inputs["L0_ffn_w1"], np.float32).transpose(0, 2, 1)),
        "w2T": A(np.asarray(inputs["L0_ffn_w2"], np.float32).transpose(0, 2, 1)),
        "b1r": A(np.asarray(inputs["L0_ffn_b1"], np.float32).reshape(E, NHC, 128).transpose(0, 2, 1)),
        "b2": A(inputs["L0_ffn_b2"]),
    }
    in_maps = []
    for i in range(c["NCORES"]):
        m = dict(shared)
        m["x"] = A(x_all[i])
        m["xT"] = A(x_all[i].T)
        in_maps.append(m)
    return in_maps


def kernel(**inputs):
    c = make_cfg()
    nc = build_program(c)
    in_maps = make_in_maps(c, inputs)
    res = run_bass_kernel_spmd(nc, in_maps, core_ids=list(range(c["NCORES"])))
    out = np.stack([np.asarray(res.results[i]["out"], dtype=np.float32)
                    for i in range(c["NCORES"])], axis=0)
    return out


# version 15
# speedup vs baseline: 1.0369x; 1.0307x over previous
import numpy as np
import ml_dtypes
import concourse.bass as bass
import concourse.mybir as mybir
from concourse.bass_utils import run_bass_kernel_spmd
from contextlib import ExitStack

F32 = mybir.dt.float32
BF16 = mybir.dt.bfloat16
U32 = mybir.dt.uint32
AF = mybir.ActivationFunctionType
ALU = mybir.AluOpType
AX = mybir.AxisListType

GA = 0.7978845608028654
GB = 0.044715
NEG = -3.0e38
ESZ = {F32: 4, BF16: 2, U32: 4}


def make_cfg(T=4096, C=1024, NH=16, HID=4096, E=8, TB=1024, NBIS=44, KTH=24576, NCORES=8):
    c = dict(T=T, C=C, NH=NH, HID=HID, E=E, TB=TB, NBIS=NBIS, KTH=KTH, NCORES=NCORES)
    c["D"] = 64
    c["DHE"] = 32
    c["NLC"] = NH // 2
    c["CC"] = C // 128
    c["TQ"] = T // 512
    c["NKT"] = T // 128
    c["NHC"] = HID // 128
    c["NTB"] = T // TB
    c["TT8"] = TB // 128
    c["NTS"] = max(1, TB // 512)
    c["TSW"] = TB // c["NTS"]
    c["ND2"] = max(1, C // 512)
    c["DW"] = C // c["ND2"]
    c["GH"] = min(8, c["NHC"])
    c["LGC"] = NCORES * E * T // 128
    assert C == NH * 64 and T % 512 == 0 and TB % 128 == 0 and HID % 512 == 0
    assert c["TSW"] <= 512 and c["DW"] <= 512
    return c


class Sem:
    def __init__(self, nc, ctx, name):
        self.h = ctx.enter_context(nc.semaphore(name))
        self.n = 0


class Res:
    __slots__ = ("w", "r")

    def __init__(self):
        self.w = None
        self.r = None


def deps(*items):
    return [it for it in items if it is not None]


class Prog:
    def __init__(self):
        self.ops = {"t": [], "v": [], "s": [], "g": [], "y": []}
        self.waited = {k: {} for k in self.ops}

    def emit(self, eng, builder, waits=(), inc=None, incn=1):
        seen = {}
        for (s, v) in waits:
            if v and v > 0:
                key = id(s.h)
                if key not in seen or seen[key][1] < v:
                    seen[key] = (s.h, v, s)
        done = self.waited[eng]
        wl = []
        for key, (h, v, s) in seen.items():
            if done.get(key, -1) >= v:
                continue
            done[key] = v
            wl.append((h, v))
        if inc is not None:
            inc.n += incn
        ih = inc.h if inc is not None else None

        def run(e, wl=wl, builder=builder, ih=ih, incn=incn):
            for h, v in wl:
                e.wait_ge(h, v)
            inst = builder(e)
            if ih is not None:
                inst.then_inc(ih, incn)
        self.ops[eng].append(run)
        return (inc, inc.n) if inc is not None else None


def rope_tables(T, DHE):
    inv = 1.0 / (10000.0 ** (np.arange(0, DHE, 2, dtype=np.float64) / DHE))
    ang = np.arange(T, dtype=np.float64)[:, None] * inv[None, :]
    emb = np.concatenate([ang, ang], axis=1)
    cosR = np.tile(np.cos(emb).T, (4, 1)).astype(ml_dtypes.bfloat16)
    sinR = np.tile(np.sin(emb).T, (4, 1)).astype(ml_dtypes.bfloat16)
    return cosR, sinR


def perm_matrix(DHE):
    Pm = np.zeros((DHE, DHE), dtype=np.float32)
    for i in range(DHE // 2):
        Pm[2 * i + 1, 2 * i] = -1.0
        Pm[2 * i, 2 * i + 1] = 1.0
    return Pm.astype(ml_dtypes.bfloat16)


def build_program(c):
    T, C, NH, HID, E, TB = c["T"], c["C"], c["NH"], c["HID"], c["E"], c["TB"]
    D, DHE, NLC, CC, TQ, NKT = c["D"], c["DHE"], c["NLC"], c["CC"], c["TQ"], c["NKT"]
    NHC, NTB, TT8 = c["NHC"], c["NTB"], c["TT8"]
    NTS, TSW, ND2, DW, GH = c["NTS"], c["TSW"], c["ND2"], c["DW"], c["GH"]
    NBIS, KTH, NCORES, LGC = c["NBIS"], c["KTH"], c["NCORES"], c["LGC"]
    UW = TB
    NG8 = NHC // GH
    NF1 = max(1, NHC // 4)
    NWV = (TT8 + 3) // 4

    nc = bass.Bass("TRN2", target_bir_lowering=False, debug=False)

    x_h = nc.dram_tensor("x", [T, C], F32, kind="ExternalInput")
    xT_h = nc.dram_tensor("xT", [C, T], F32, kind="ExternalInput")
    wq_h = nc.dram_tensor("wq", [C, C], F32, kind="ExternalInput")
    wk_h = nc.dram_tensor("wk", [C, C], F32, kind="ExternalInput")
    wv_h = nc.dram_tensor("wv", [C, C], F32, kind="ExternalInput")
    upcq_h = nc.dram_tensor("upcq", [D, DHE], F32, kind="ExternalInput")
    upeq_h = nc.dram_tensor("upeq", [D, DHE], F32, kind="ExternalInput")
    upeqT_h = nc.dram_tensor("upeqT", [DHE, D], F32, kind="ExternalInput")
    upck_h = nc.dram_tensor("upck", [D, DHE], F32, kind="ExternalInput")
    upek_h = nc.dram_tensor("upek", [D, DHE], F32, kind="ExternalInput")
    upekT_h = nc.dram_tensor("upekT", [DHE, D], F32, kind="ExternalInput")
    wvup_h = nc.dram_tensor("wvup", [D, D], F32, kind="ExternalInput")
    wc_h = nc.dram_tensor("wc", [C, C], F32, kind="ExternalInput")
    gwT_h = nc.dram_tensor("gwT", [C, E], F32, kind="ExternalInput")
    w1T_h = nc.dram_tensor("w1T", [E, C, HID], F32, kind="ExternalInput")
    w2T_h = nc.dram_tensor("w2T", [E, HID, C], F32, kind="ExternalInput")
    b1r_h = nc.dram_tensor("b1r", [E, 128, NHC], F32, kind="ExternalInput")
    b2_h = nc.dram_tensor("b2", [E, C], F32, kind="ExternalInput")
    out_h = nc.dram_tensor("out", [T, C], F32, kind="ExternalOutput")

    yT_d = nc.dram_tensor("yT_d", [C, T], BF16)
    x1_d = nc.dram_tensor("x1_d", [T, C], F32)
    x1T_d = nc.dram_tensor("x1T_d", [C, T], BF16)
    cc_in = nc.dram_tensor("cc_in", [E, T], F32)
    cc_out = nc.dram_tensor("cc_out", [NCORES * E, T], F32)
    w1c_d = nc.dram_tensor("w1c_d", [E, C, HID], BF16)
    w2c_d = nc.dram_tensor("w2c_d", [E, HID, C], BF16)

    mdiag_np = np.zeros((128, 128), dtype=np.float32)
    for kp in range(128):
        mdiag_np[kp, :kp] = NEG
    cosR, sinR = rope_tables(T, DHE)
    cpool_np = np.zeros((128, 4), np.float32)
    cpool_np[:, 0] = 1e-5
    consts = {
        "mdiag": mdiag_np,
        "mfull": np.full((128, 128), NEG, np.float32),
        "identf": np.eye(128, dtype=np.float32),
        "ones64c": np.ones((1, D), dtype=ml_dtypes.bfloat16),
        "ones128": np.ones((128, 128), dtype=np.float32),
        "pm": perm_matrix(DHE),
        "cosR": cosR,
        "sinR": sinR,
        "cpool": cpool_np,
    }
    cth = {k: nc.inline_tensor(v, k) for k, v in consts.items()}

    with ExitStack() as ctx:
        ARENA0 = nc.sbuf_base + 1536
        off = [ARENA0]

        def sba(name, shape, dt, at=None):
            fb = int(np.prod(shape[1:])) * ESZ[dt]
            o = (off[0] + 63) // 64 * 64 if at is None else at
            if at is None:
                off[0] = o + fb
            return nc.alloc_sbuf_tensor_at(name, shape, dt, offset=o)

        mdiag = sba("mdiag_sb", [128, 128], F32)
        mfull = sba("mfull_sb", [128, 128], F32)
        identf = sba("identf_sb", [128, 128], F32)
        ones64 = sba("ones64_sb", [1, D], BF16)
        ones128 = sba("ones128_sb", [128, 128], F32)
        cpool = sba("cpool_sb", [128, 4], F32)
        pm = sba("pm_sb", [DHE, DHE], BF16)
        upcq = sba("upcq_sb", [128, DHE], BF16)
        upeq = sba("upeq_sb", [128, DHE], BF16)
        upck = sba("upck_sb", [128, DHE], BF16)
        upek = sba("upek_sb", [128, DHE], BF16)
        upeqT = sba("upeqT_sb", [DHE, D], BF16)
        upekT = sba("upekT_sb", [DHE, D], BF16)
        wvup = sba("wvup_sb", [128, D], BF16)
        wpq = sba("wpq_sb", [128, DHE], BF16)
        wpk = sba("wpk_sb", [128, DHE], BF16)
        gwhi = sba("gwhi_sb", [128, CC * E], BF16)
        gwlo = sba("gwlo_sb", [128, CC * E], BF16)
        gwf = sba("gwf_sb", [128, CC * E], F32)
        logitsT = sba("logitsT_sb", [E, T], F32)
        g_tok = sba("g_tok_sb", [128, NKT * E], F32)
        b2b = sba("b2_sb", [E, C], F32)
        b1e = [sba(f"b1e{i}", [128, NHC], F32) for i in range(2)]
        bis_lo = sba("bis_lo", [128, 1], F32)
        bis_hi = sba("bis_hi", [128, 1], F32)
        bis_mid = sba("bis_mid", [128, 1], F32)
        bis_t1 = sba("bis_t1", [128, 1], F32)
        bis_cnt = sba("bis_cnt", [128, 1], F32)
        bis_tot = sba("bis_tot", [128, 1], F32)
        bis_p = sba("bis_p", [128, 1], U32)
        bis_pi = sba("bis_pi", [128, 1], U32)
        lo2 = sba("lo2_sb", [128, 1], F32)
        ln_nm = sba("ln_nm", [128, 1], F32)
        ln_vs = sba("ln_vs", [128, 1], F32)
        ln_t0 = sba("ln_t0", [128, 1], F32)
        ln_rstd = sba("ln_rstd", [128, 1], F32)
        a0 = (off[0] + 63) // 64 * 64

        off[0] = a0
        xT = sba("xT_sb", [128, CC * T], BF16)
        qT = sba("qT_sb", [128, T], BF16)
        kT = sba("kT_sb", [128, T], BF16)
        yTp = sba("yTp_sb", [128, T], BF16)
        vpair = sba("v_sb", [128, 2 * NKT * (D + 1)], BF16)
        latb = [sba(f"lat{i}", [128, T], BF16) for i in range(2)]
        wqb = [sba(f"wqb{i}", [128, CC * 128], BF16) for i in range(2)]
        wkb = [sba(f"wkb{i}", [128, CC * 128], BF16) for i in range(2)]
        wvb = [sba(f"wvb{i}", [128, CC * 128], BF16) for i in range(2)]
        cos = sba("cos_sb", [128, T], BF16)
        sin = sba("sin_sb", [128, T], BF16)
        expb = [sba(f"expb{i}", [128, 512], BF16) for i in range(4)]
        ysb = [sba(f"ysb{i}", [D + 1, 512], F32) for i in range(2)]
        rcp = [sba(f"rcp{i}", [1, 512], F32) for i in range(2)]
        rcpb = [sba(f"rcpb{i}", [1, 512], BF16) for i in range(2)]
        ropeA = sba("ropeA_sb", [128, 512], F32)
        ropeB = sba("ropeB_sb", [128, 512], F32)
        ropeC = sba("ropeC_sb", [128, 512], F32)
        ropeD = sba("ropeD_sb", [128, 512], F32)
        a_end = off[0]

        off[0] = a0
        wc_sb = sba("wc_sb", [128, CC * C], BF16)
        ytb = [sba(f"ytb{i}", [128, CC * 128], BF16) for i in range(2)]
        xb = [sba(f"xb{i}", [128, C], F32) for i in range(2)]
        r_sb = sba("r_sb", [128, C], F32)
        xc_sb = sba("xc_sb", [128, C], F32)
        sq_sb = sba("sq_sb", [128, C], F32)
        x1f = sba("x1f_sb", [128, C], F32)
        hi_b = sba("hi_sb", [128, C], BF16)
        lof = sba("lof_sb", [128, C], F32)
        x1Ts = [sba(f"x1Ts{i}", [128, CC * 128], BF16) for i in range(2)]
        loTs = sba("loTs_sb", [128, CC * 128], BF16)
        b_end = off[0]

        off[0] = a0
        x1T_sb = sba("x1T_sb", [128, CC * TB], BF16)
        h_sb = sba("h_sb", [128, NHC * TB], BF16)
        accs = sba("acc_sb", [128, TT8 * C], F32)
        w1b = [sba(f"w1b{i}", [128, CC * 512], BF16) for i in range(2)]
        w2b = [sba(f"w2b{i}", [128, GH * DW], BF16) for i in range(2)]
        x1r = [sba(f"x1r{i}_sb", [128, C], F32) for i in range(2)]
        lnx = [sba(f"lnx{i}_sb", [128, C], F32) for i in range(2)]
        lnt2 = sba("lnt2_sb", [128, 2], F32)
        lnn2 = sba("lnn2_sb", [128, 2], F32)
        lnv2 = sba("lnv2_sb", [128, 2], F32)
        lnr2 = sba("lnr2_sb", [128, 2], F32)
        d_end = off[0]
        assert max(a_end, b_end, d_end) < nc.sbuf_top - 64, (a_end, b_end, d_end, nc.sbuf_top)

        x1T_off = (a0 + 63) // 64 * 64
        h_off = (x1T_off + CC * TB * 2 + 63) // 64 * 64
        acc_off = (h_off + NHC * TB * 2 + 63) // 64 * 64
        if LGC * 6 + T * 4 <= TT8 * C * 4:
            lg_all = sba("lg_all_sb", [128, LGC], F32, at=acc_off)
            cmpb = sba("cmpb_sb", [128, LGC], BF16, at=acc_off + LGC * 4)
            gtmp = sba("gtmp_sb", [E, T], F32, at=acc_off + LGC * 6)
        else:
            lg_all = sba("lg_all_sb", [128, LGC], F32)
            cmpb = sba("cmpb_sb", [128, LGC], BF16)
            gtmp = sba("gtmp_sb", [E, T], F32)
            d_end = off[0]
            assert d_end < nc.sbuf_top - 64, d_end

        pb = [ctx.enter_context(nc.psum_tensor(f"pb{i}", [128, 512], F32)) for i in range(8)]
        pbr = [Res() for _ in range(8)]

        sld = Sem(nc, ctx, "sld")
        sl2 = Sem(nc, ctx, "sl2")
        wst = [Sem(nc, ctx, f"wst{i}") for i in range(2)]
        swc = Sem(nc, ctx, "swc")
        syt = [Sem(nc, ctx, f"syt{i}") for i in range(2)]
        sxb = [Sem(nc, ctx, f"sxb{i}") for i in range(2)]
        sx1s = Sem(nc, ctx, "sx1s")
        sxT = [Sem(nc, ctx, f"sxT{i}") for i in range(2)]
        syT = Sem(nc, ctx, "syT")
        scc = Sem(nc, ctx, "scc")
        slg = Sem(nc, ctx, "slg")
        sx1T = Sem(nc, ctx, "sx1T")
        sb1 = [Sem(nc, ctx, f"sb1{i}") for i in range(2)]
        sw1 = [Sem(nc, ctx, f"sw1{i}") for i in range(2)]
        sw2 = [Sem(nc, ctx, f"sw2{i}") for i in range(2)]
        sx1r = [Sem(nc, ctx, "sx1ra"), Sem(nc, ctx, "sx1rb")]
        sout = Sem(nc, ctx, "sout")
        te = Sem(nc, ctx, "te")
        ve = Sem(nc, ctx, "ve")
        se = Sem(nc, ctx, "se")
        ge = Sem(nc, ctx, "ge")
        scst = Sem(nc, ctx, "scst")
        ccs = Sem(nc, ctx, "ccs")

        nc.const_aps.register(F32, 1e-5, cpool[:, 0:1])

        block_holder = ctx

        P = Prog()

        def dma(eng, sem, out_ap, in_ap, waits=()):
            return P.emit(eng, lambda e, o=out_ap, i=in_ap: e.dma_start(out=o, in_=i),
                          waits=waits, inc=sem, incn=16)

        for name, dst in [("mdiag", mdiag), ("mfull", mfull), ("identf", identf),
                          ("ones64c", ones64), ("ones128", ones128), ("cpool", cpool),
                          ("pm", pm), ("cosR", cos), ("sinR", sin)]:
            dma("y", sl2, dst[:, :], cth[name][:, :])
        dma("y", sl2, gwf[:, :], gwT_h.ap().rearrange("(c p) e -> p c e", p=128))
        dma("y", sl2, b2b[:, :], b2_h[:, :])
        setup_l2 = (sl2, sl2.n)

        for dst, src in [(upcq, upcq_h), (upeq, upeq_h), (upck, upck_h), (upek, upek_h),
                         (wvup, wvup_h)]:
            dma("g", sld, dst[0:D, :], src[:, :])
            dma("g", sld, dst[D:128, :], src[:, :])
        dma("g", sld, upeqT[:, :], upeqT_h[:, :])
        dma("g", sld, upekT[:, :], upekT_h[:, :])
        dma("g", sld, gwhi[:, :], gwT_h.ap().rearrange("(c p) e -> p c e", p=128))
        for i in range(CC):
            dma("g", sld, xT[:, i * T:(i + 1) * T],
                xT_h.ap().rearrange("(c p) t -> p c t", p=128)[:, i, :])
        setup_ld = (sld, sld.n)
        pair_w_ld = {}
        for lci in range(min(2, NLC)):
            for wb, wh in ((wqb, wq_h), (wkb, wk_h), (wvb, wv_h)):
                dma("g", wst[lci % 2], wb[lci][:, :],
                    wh.ap().rearrange("(c p) l -> p c l", p=128)[:, :, lci * 128:(lci + 1) * 128])
            pair_w_ld[lci] = (wst[lci % 2], wst[lci % 2].n)

        gwlo_v = P.emit("v", lambda e: e.tensor_sub(gwlo[:, :], gwf[:, :], gwhi[:, :]),
                        waits=[setup_l2, setup_ld], inc=ve)
        nvl = 2 * NKT
        vset = None
        for j in range(nvl):
            vset = P.emit("v", lambda e, j=j: e.memset(
                vpair[:, j * (D + 1) + D: j * (D + 1) + D + 1], 1.0),
                waits=deps(setup_ld) if j == 0 else (),
                inc=ve if j == nvl - 1 else None)

        wp_done = None
        for wp, upT in ((wpq, upeqT), (wpk, upekT)):
            last_mm = None
            for h in range(2):
                last_mm = P.emit("t", lambda e, upT=upT, h=h: e.matmul(
                    pb[6][h * D:h * D + D, 0:DHE], upT[:, :], pm[:, :],
                    start=True, stop=True, tile_position=(0, h * D)),
                    waits=deps(setup_ld, setup_l2, vset, pbr[6].r), inc=te)
            pbr[6].w = last_mm
            wp_done = P.emit("s", lambda e, wp=wp: e.copy(wp[:, :], pb[6][:, 0:DHE]),
                             waits=deps(pbr[6].w), inc=se)
            pbr[6].r = wp_done

        NB4 = min(4, T // 512)
        THN = T // (512 * NB4)
        N8 = T // 512
        latr = [Res(), Res()]
        qTr, kTr, vpr, yTr = Res(), Res(), Res(), Res()
        expr = [Res() for _ in range(4)]
        ysbr = [Res(), Res()]
        ropeAr, ropeBr = Res(), Res()
        wbfree = {0: None, 1: None}
        pend_norm = [None]
        pend_normb = [None]

        def emit_norm_a(qb, mmyd):
            rbvs = []
            for h in range(2):
                yev = P.emit("s", lambda e, h=h: e.copy(ysb[h][:, :], pb[4 + h][0:D + 1, :]),
                             waits=deps(mmyd[h], ysbr[h].r), inc=se)
                ysbr[h].w = yev
                pbr[4 + h].r = yev
                rv = P.emit("v", lambda e, h=h: e.reciprocal(rcp[h][:, :], ysb[h][D:D + 1, :]),
                            waits=deps(yev), inc=ve)
                rbv = P.emit("v", lambda e, h=h: e.tensor_copy(rcpb[h][:, :], rcp[h][:, :]),
                             waits=deps(rv), inc=ve)
                rbvs.append(rbv)
            return rbvs

        def emit_norm_b(qb, rbvs):
            for h in range(2):
                bct = P.emit("t", lambda e, h=h: e.matmul(
                    pb[6 + h][0:D, :], ones64[:, :], rcpb[h][:, :], start=True, stop=True),
                    waits=deps(rbvs[h], pbr[6 + h].r), inc=te)
                pbr[6 + h].w = bct
                ym = P.emit("v", lambda e, h=h, qb=qb: e.tensor_tensor(
                    yTp[h * D:h * D + D, qb * 512:(qb + 1) * 512],
                    ysb[h][0:D, :], pb[6 + h][0:D, 0:512], ALU.mult),
                    waits=deps(bct, yTr.r), inc=ve)
                ysbr[h].r = ym
                pbr[6 + h].r = ym
                yTr.w = ym

        cast_chunks = []
        for ei in range(E):
            cp1 = max(1, C // 128)
            r1 = C // cp1
            for p in range(cp1):
                cast_chunks.append((w1c_d[ei, p * r1:(p + 1) * r1, :],
                                    w1T_h[ei, p * r1:(p + 1) * r1, :]))
        for ei in range(E):
            cp2 = max(1, HID // 512)
            r2 = HID // cp2
            for p in range(cp2):
                cast_chunks.append((w2c_d[ei, p * r2:(p + 1) * r2, :],
                                    w2T_h[ei, p * r2:(p + 1) * r2, :]))

        for lc in range(NLC):
            for wi in range(3):
                wb = (wqb, wkb, wvb)[wi]
                lbi = 0 if wi != 1 else 1
                lbuf = latb[lbi]
                lrs = latr[lbi]
                last_lat_mm = None
                for th in range(THN):
                    bank_done = [None] * NB4
                    for cci in range(CC):
                        for n4 in range(NB4):
                            tw = deps(pbr[n4].r, lrs.r, pair_w_ld.get(lc), setup_ld) if cci == 0 else ()
                            isl = cci == CC - 1
                            bd = P.emit("t", lambda e, wb=wb, lc=lc, cci=cci, th=th, n4=n4, isl=isl: e.matmul(
                                pb[n4][:, :], wb[lc % 2][:, cci * 128:(cci + 1) * 128],
                                xT[:, cci * T + th * 512 * NB4 + n4 * 512:
                                   cci * T + th * 512 * NB4 + (n4 + 1) * 512],
                                start=(cci == 0), stop=isl),
                                waits=tw, inc=te if isl else None)
                            if isl:
                                bank_done[n4] = bd
                    last_lat_mm = bank_done[NB4 - 1]
                    for n4 in range(NB4):
                        pbr[n4].w = bank_done[n4]
                        ev = P.emit("v", lambda e, lbuf=lbuf, th=th, n4=n4: e.tensor_copy(
                            lbuf[:, th * 512 * NB4 + n4 * 512: th * 512 * NB4 + (n4 + 1) * 512],
                            pb[n4][:, :]),
                            waits=deps(bank_done[n4]), inc=ve)
                        pbr[n4].r = ev
                        lrs.w = ev
                wbfree[lc % 2] = last_lat_mm

                if wi < 2:
                    dstT = qT if wi == 0 else kT
                    dstR = qTr if wi == 0 else kTr
                    upc = upcq if wi == 0 else upck
                    upe = upeq if wi == 0 else upek
                    wpp = wpq if wi == 0 else wpk
                    last_up_mm = None
                    rsc = [(ropeA, ropeB, ropeAr), (ropeC, ropeD, ropeBr)]
                    for n8 in range(N8):
                        lo_, hi_ = n8 * 512, (n8 + 1) * 512
                        ba = 4 + 2 * (n8 % 2)
                        bb = ba + 1
                        rA, rB, rR = rsc[n8 % 2]
                        specs = [(upc, 0, ba, 0), (upe, 0, ba, DHE), (wpp, 0, bb, DHE),
                                 (upc, 64, ba, 64), (upe, 64, ba, 64 + DHE), (wpp, 64, bb, 64 + DHE)]
                        mmv = []
                        for si, (wsb, hb, bki, po) in enumerate(specs):
                            tw = deps(lrs.w, pbr[ba].r, pbr[bb].r, wp_done) if si == 0 else ()
                            mmv.append(P.emit("t", lambda e, wsb=wsb, hb=hb, bki=bki, po=po,
                                              lbuf=lbuf, lo_=lo_, hi_=hi_: e.matmul(
                                pb[bki][po:po + DHE, 0:512], wsb[hb:hb + D, :],
                                lbuf[hb:hb + D, lo_:hi_],
                                start=True, stop=True, tile_position=(hb, po)),
                                waits=tw, inc=te))
                        last_up_mm = mmv[-1]
                        lastv = None
                        for h in range(2):
                            hb = 64 * h
                            P.emit("v", lambda e, hb=hb, dstT=dstT, lo_=lo_, hi_=hi_, ba=ba: e.tensor_copy(
                                dstT[hb:hb + DHE, lo_:hi_], pb[ba][hb:hb + DHE, 0:512]),
                                waits=deps(mmv[3 * h + 0], dstR.r, rR.r))
                            P.emit("v", lambda e, hb=hb, lo_=lo_, hi_=hi_, ba=ba, rA=rA: e.tensor_mul(
                                rA[hb + DHE:hb + 2 * DHE, 0:512],
                                pb[ba][hb + DHE:hb + 2 * DHE, 0:512],
                                cos[hb + DHE:hb + 2 * DHE, lo_:hi_]),
                                waits=deps(mmv[3 * h + 1]))
                            t2v = P.emit("v", lambda e, hb=hb, lo_=lo_, hi_=hi_, bb=bb, rB=rB: e.tensor_mul(
                                rB[hb + DHE:hb + 2 * DHE, 0:512],
                                pb[bb][hb + DHE:hb + 2 * DHE, 0:512],
                                sin[hb + DHE:hb + 2 * DHE, lo_:hi_]),
                                waits=deps(mmv[3 * h + 2]), inc=ve)
                            lastv = P.emit("v", lambda e, hb=hb, dstT=dstT, lo_=lo_, hi_=hi_, rA=rA, rB=rB: e.tensor_add(
                                dstT[hb + DHE:hb + 2 * DHE, lo_:hi_],
                                rA[hb + DHE:hb + 2 * DHE, 0:512],
                                rB[hb + DHE:hb + 2 * DHE, 0:512]),
                                waits=deps(t2v), inc=ve)
                        pbr[ba].r = lastv
                        pbr[bb].r = lastv
                        rR.r = lastv
                        dstR.w = lastv
                    lrs.r = last_up_mm
                else:
                    last_v_mm = None
                    for h in range(2):
                        for tt in range(NKT):
                            bki = 6 + (tt % 2)
                            vn = P.emit("t", lambda e, h=h, tt=tt, bki=bki, lbuf=lbuf: e.matmul(
                                pb[bki][:, 0:D], lbuf[h * D:h * D + D, tt * 128:(tt + 1) * 128],
                                wvup[h * D:h * D + D, :], start=True, stop=True,
                                tile_position=(h * D, 0)),
                                waits=deps(lrs.w, pbr[bki].r), inc=te)
                            pbr[bki].w = vn
                            last_v_mm = vn
                            ev = P.emit("v", lambda e, h=h, tt=tt, bki=bki: e.tensor_copy(
                                vpair[:, (h * NKT + tt) * (D + 1):(h * NKT + tt) * (D + 1) + D],
                                pb[bki][:, 0:D]),
                                waits=deps(vn, vpr.r), inc=ve)
                            pbr[bki].r = ev
                            vpr.w = ev
                    lrs.r = last_v_mm

            if lc + 2 < NLC:
                for wb, wh in ((wqb, wq_h), (wkb, wk_h), (wvb, wv_h)):
                    dma("g", wst[lc % 2], wb[lc % 2][:, :],
                        wh.ap().rearrange("(c p) l -> p c l", p=128)[:, :, (lc + 2) * 128:(lc + 3) * 128],
                        waits=deps(wbfree[lc % 2]))
                pair_w_ld[lc + 2] = (wst[lc % 2], wst[lc % 2].n)

            ncst = len(cast_chunks)
            per = -(-ncst // NLC)
            for ck in range(lc * per, min((lc + 1) * per, ncst)):
                dst, srcp = cast_chunks[ck]
                dma("g", scst, dst, srcp)

            last_mm_s = None
            for qb in range(TQ):
                if pend_norm[0] is not None:
                    pq, pmmyd = pend_norm[0]
                    pend_normb[0] = (pq, emit_norm_a(pq, pmmyd))
                    pend_norm[0] = None
                kcm = 4 * qb + 4
                mmy_done = [None, None]

                def emit_av(kc, h, qb=qb, kcm=kcm):
                    sbi = 2 * h + (kc % 2)
                    ebuf = expb[sbi]
                    sev = expr[sbi].w
                    yw = deps(sev)
                    if kc == 0:
                        yw = deps(sev, pbr[4 + h].r, vpr.w, yTr.r)
                    mmy = P.emit("t", lambda e, h=h, kc=kc, kcm=kcm, ebuf=ebuf: e.matmul(
                        pb[4 + h][0:D + 1, :],
                        vpair[:, (h * NKT + kc) * (D + 1):(h * NKT + kc + 1) * (D + 1)],
                        ebuf[:, :], start=(kc == 0), stop=(kc == kcm - 1)),
                        waits=yw, inc=te)
                    expr[sbi].r = mmy
                    if kc == kcm - 1:
                        pbr[4 + h].w = mmy
                        mmy_done[h] = mmy

                for kc in range(kcm):
                    par = kc % 2
                    for h in range(2):
                        sbi = 2 * h + par
                        mm = P.emit("t", lambda e, h=h, kc=kc, qb=qb, sbi=sbi: e.matmul(
                            pb[sbi][:, :], kT[h * D:h * D + D, kc * 128:(kc + 1) * 128],
                            qT[h * D:h * D + D, qb * 512:(qb + 1) * 512],
                            start=True, stop=True),
                            waits=deps(pbr[sbi].r, qTr.w, kTr.w), inc=te)
                        last_mm_s = mm
                        pbr[sbi].w = mm
                        rel = kc - 4 * qb
                        if rel >= 0:
                            mv = None
                            for tt in range(rel + 1):
                                msk = mfull if tt < rel else mdiag
                                mv = P.emit("v", lambda e, sbi=sbi, tt=tt, msk=msk: e.tensor_tensor(
                                    pb[sbi][:, tt * 128:(tt + 1) * 128],
                                    pb[sbi][:, tt * 128:(tt + 1) * 128], msk[:, :], ALU.add),
                                    waits=deps(mm) if tt == 0 else (),
                                    inc=ve if tt == rel else None)
                            pbr[sbi].w = mv
                        ebuf = expb[sbi]
                        sev = P.emit("s", lambda e, ebuf=ebuf, sbi=sbi: e.activation(
                            ebuf[:, :], pb[sbi][:, :], AF.Exp, scale=0.125),
                            waits=deps(pbr[sbi].w, expr[sbi].r), inc=se)
                        expr[sbi].w = sev
                        pbr[sbi].r = sev
                    if kc > 0:
                        for h in range(2):
                            emit_av(kc - 1, h)
                for h in range(2):
                    emit_av(kcm - 1, h)
                if pend_normb[0] is not None:
                    emit_norm_b(*pend_normb[0])
                    pend_normb[0] = None
                pend_norm[0] = (qb, list(mmy_done))
            if pend_norm[0] is not None:
                pq, pmmyd = pend_norm[0]
                emit_norm_b(pq, emit_norm_a(pq, pmmyd))
                pend_norm[0] = None
            qTr.r = last_mm_s
            kTr.r = last_mm_s
            st_y = dma("y", syT, yT_d[lc * 128:(lc + 1) * 128, :], yTp[:, :], waits=deps(yTr.w))
            yTr.r = st_y

        a_done = deps(yTr.r)

        PH = c.get("PHASES", "ABCD")
        if "B" not in PH:
            P.emit("y", lambda e: e.nop(), waits=[(syT, syT.n)])
            return _finish(nc, P, block_holder)
        cast_gate = [(scst, scst.n)] if scst.n else []
        dma("g", swc, wc_sb[:, :], wc_h.ap().rearrange("(c p) d -> p c d", p=128), waits=a_done)
        wc_ld = (swc, swc.n)
        ytbr = [Res(), Res()]
        xbr = [Res(), Res()]
        x1Tsr = [Res(), Res()]
        loTsr = Res()
        rR, xcR, sqR, x1R, hiR, lofR = Res(), Res(), Res(), Res(), Res(), Res()
        nmR, vsR, t0R, rstdR = Res(), Res(), Res(), Res()
        logits_done = None
        x1_store_done = None
        x1T_store_done = None
        NW = max(1, C // 512)
        WNl = C // NW

        def emit_ln(srcT, srcRes, xcT, xcRes, sqT, sqRes):
            rsv = P.emit("v", lambda e, srcT=srcT: e.reduce_sum(ln_t0[:, :], srcT[:, 0:C], axis=AX.X),
                         waits=deps(srcRes.w, t0R.r, t0R.w), inc=ve)
            t0R.w = rsv
            nmv = P.emit("v", lambda e: e.tensor_scalar(
                ln_nm[:, :], ln_t0[:, :], -1.0 / C, None, ALU.mult),
                waits=deps(rsv, nmR.r), inc=ve)
            nmR.w = nmv
            xcv = P.emit("s", lambda e, srcT=srcT, xcT=xcT: e.activation(
                xcT[:, 0:C], srcT[:, 0:C], AF.Identity, bias=ln_nm[:, 0:1]),
                waits=deps(nmv, xcRes.r, xcRes.w), inc=se)
            xcRes.w = xcv
            nmR.r = xcv
            srcRes.r = xcv
            sqv = P.emit("s", lambda e, xcT=xcT, sqT=sqT: e.activation(
                sqT[:, 0:C], xcT[:, 0:C], AF.Square, accum_out=ln_vs[:, 0:1]),
                waits=deps(xcv, sqRes.r, sqRes.w, vsR.r), inc=se)
            sqRes.w = sqv
            vsR.w = sqv
            lnv = P.emit("s", lambda e: e.activation(
                ln_t0[:, :], ln_vs[:, :], AF.Ln, scale=1.0 / C, bias=1e-5),
                waits=deps(sqv, nmv, t0R.r), inc=se)
            t0R.w = lnv
            vsR.r = lnv
            rsv2 = P.emit("s", lambda e: e.activation(
                ln_rstd[:, :], ln_t0[:, :], AF.Exp, scale=-0.5),
                waits=deps(lnv, rstdR.r), inc=se)
            rstdR.w = rsv2
            t0R.r = rsv2
            return rsv2

        for tt in range(NKT):
            par = tt % 2
            ytld = dma("y", syt[par], ytb[par][:, :],
                       yT_d.ap().rearrange("(c p) t -> p c t", p=128)[:, :, tt * 128:(tt + 1) * 128],
                       waits=deps(ytbr[par].r) + (a_done if tt < 2 else []))
            ytbr[par].w = ytld
            xld = dma("y", sxb[par], xb[par][:, :], x_h[tt * 128:(tt + 1) * 128, :],
                      waits=deps(xbr[par].r) + (a_done if tt < 2 else []))
            xbr[par].w = xld
            mm_done = [None] * NW
            for cci in range(CC):
                for wn in range(NW):
                    tw = deps(ytld, wc_ld, pbr[wn].r) if cci == 0 else ()
                    md = P.emit("t", lambda e, cci=cci, wn=wn, par=par: e.matmul(
                        pb[wn][:, 0:WNl], ytb[par][:, cci * 128:(cci + 1) * 128],
                        wc_sb[:, cci * C + wn * WNl: cci * C + (wn + 1) * WNl],
                        start=(cci == 0), stop=(cci == CC - 1)),
                        waits=tw, inc=te if cci == CC - 1 else None)
                    if cci == CC - 1:
                        mm_done[wn] = md
                        pbr[wn].w = md
            rv = None
            for wn in range(NW):
                rv = P.emit("v", lambda e, wn=wn, par=par: e.tensor_tensor(
                    r_sb[:, wn * WNl:(wn + 1) * WNl], pb[wn][:, 0:WNl],
                    xb[par][:, wn * WNl:(wn + 1) * WNl], ALU.add),
                    waits=deps(mm_done[wn], xld, rR.r) if wn == 0 else deps(mm_done[wn]),
                    inc=ve if wn == NW - 1 else None)
            for wn in range(NW):
                pbr[wn].r = rv
            rR.w = rv
            xbr[par].r = rv
            rstd_done = emit_ln(r_sb, rR, xc_sb, xcR, sq_sb, sqR)
            x1v = P.emit("v", lambda e: e.tensor_scalar(
                x1f[:, :], xc_sb[:, :], ln_rstd[:, 0:1], None, ALU.mult),
                waits=deps(rstd_done, x1R.r, x1_store_done), inc=ve)
            x1R.w = x1v
            rstdR.r = x1v
            xcR.r = x1v
            hiv = P.emit("v", lambda e: e.tensor_copy(hi_b[:, :], x1f[:, :]),
                         waits=deps(x1v, hiR.r), inc=ve)
            hiR.w = hiv
            lov = P.emit("v", lambda e: e.tensor_sub(lof[:, :], x1f[:, :], hi_b[:, :]),
                         waits=deps(hiv, lofR.r), inc=ve)
            lofR.w = lov
            hiR.r = lov
            x1_store_done = dma("y", sx1s, x1_d[tt * 128:(tt + 1) * 128, :], x1f[:, :],
                                waits=deps(x1v))
            ev_x_last = None
            lg_last = None
            tpl = None
            for cci in range(CC):
                bx = 2 + cci % 2
                bl = 5 + cci % 2
                tpx = P.emit("t", lambda e, cci=cci, bx=bx: e.transpose(
                    pb[bx][:, 0:128], x1f[:, cci * 128:(cci + 1) * 128], identf[:, :]),
                    waits=deps(x1v, pbr[bx].r), inc=te)
                pbr[bx].w = tpx
                ev_x = P.emit("s", lambda e, cci=cci, par=par, bx=bx: e.copy(
                    x1Ts[par][:, cci * 128:(cci + 1) * 128], pb[bx][:, 0:128]),
                    waits=deps(tpx, x1Tsr[par].r), inc=se)
                pbr[bx].r = ev_x
                ev_x_last = ev_x
                tpl = P.emit("t", lambda e, cci=cci, bl=bl: e.transpose(
                    pb[bl][:, 0:128], lof[:, cci * 128:(cci + 1) * 128], identf[:, :]),
                    waits=deps(lov, pbr[bl].r), inc=te)
                pbr[bl].w = tpl
                ev_l = P.emit("s", lambda e, cci=cci, bl=bl: e.copy(
                    loTs[:, cci * 128:(cci + 1) * 128], pb[bl][:, 0:128]),
                    waits=deps(tpl, loTsr.r if cci == 0 else None), inc=se)
                pbr[bl].r = ev_l
                loTsr.w = ev_l
                for j in range(3):
                    lhs = gwhi if j < 2 else gwlo
                    rhsT = x1Ts[par] if j != 1 else loTs
                    tw = deps(ev_x, ev_l, gwlo_v)
                    if cci == 0 and j == 0:
                        tw = tw + deps(pbr[4].r)
                    lg_last = P.emit("t", lambda e, cci=cci, j=j, lhs=lhs, rhsT=rhsT: e.matmul(
                        pb[4][0:E, 0:128], lhs[:, cci * E:(cci + 1) * E],
                        rhsT[:, cci * 128:(cci + 1) * 128],
                        start=(cci == 0 and j == 0), stop=(cci == CC - 1 and j == 2)),
                        waits=tw, inc=te if (cci == CC - 1 and j == 2) else None)
            pbr[4].w = lg_last
            loTsr.r = lg_last
            lofR.r = tpl
            lgev = P.emit("s", lambda e, tt=tt: e.copy(
                logitsT[:, tt * 128:(tt + 1) * 128], pb[4][0:E, 0:128]),
                waits=deps(lg_last), inc=se)
            pbr[4].r = lgev
            logits_done = lgev
            x1T_store_done = dma("y", sxT[par],
                                 x1T_d.ap().rearrange("(c p) t -> p c t", p=128)[:, :, tt * 128:(tt + 1) * 128],
                                 x1Ts[par][:, :], waits=deps(ev_x_last))
            x1Tsr[par].r = x1T_store_done
            x1R.r = x1T_store_done

        if "C" not in PH:
            P.emit("y", lambda e: e.nop(),
                   waits=[(syT, syT.n), (sx1s, sx1s.n), (sxT[0], sxT[0].n), (sxT[1], sxT[1].n)])
            return _finish(nc, P, block_holder)
        cin_st = dma("g", scc, cc_in[:, :], logitsT[:, :], waits=deps(logits_done))
        P.emit("g", lambda e: e.collective_compute(
            "AllGather", ALU.bypass, replica_groups=[list(range(NCORES))],
            ins=[cc_in.ap().opt()], outs=[cc_out.ap().opt()]),
            waits=deps(cin_st), inc=ccs, incn=1)
        lg_ld = dma("g", slg, lg_all[:, :],
                    cc_out.ap().rearrange("r (h c) -> (r h) c", h=2), waits=[(ccs, 1)])
        P.emit("v", lambda e: e.memset(bis_lo[:, :], -64.0), inc=ve)
        biv = P.emit("v", lambda e: e.memset(bis_hi[:, :], 64.0), inc=ve)
        smid = P.emit("s", lambda e: e.memzero(bis_mid[:, :]), inc=se)
        selv = None
        for i in range(NBIS):
            cmpv = P.emit("v", lambda e: e.tensor_scalar(
                cmpb[:, :], lg_all[:, :], bis_mid[:, 0:1], None, ALU.is_le),
                waits=deps(lg_ld, smid, biv), inc=ve)
            cntv = P.emit("v", lambda e: e.reduce_sum(bis_cnt[:, :], cmpb[:, :], axis=AX.X),
                          waits=deps(cmpv), inc=ve)
            cmm = P.emit("t", lambda e: e.matmul(pb[0][:, 0:1], ones128[:, :], bis_cnt[:, :],
                                                 start=True, stop=True),
                         waits=deps(cntv, pbr[0].r), inc=te)
            pbr[0].w = cmm
            totS = P.emit("s", lambda e: e.copy(bis_tot[:, :], pb[0][:, 0:1]),
                          waits=deps(cmm), inc=se)
            pbr[0].r = totS
            P.emit("v", lambda e: e.tensor_scalar(
                bis_p[:, :], bis_tot[:, :], float(KTH), None, ALU.is_ge),
                waits=deps(totS))
            pv = P.emit("v", lambda e: e.tensor_scalar(
                bis_pi[:, :], bis_tot[:, :], float(KTH), None, ALU.is_lt),
                waits=deps(totS), inc=ve)
            P.emit("v", lambda e: e.select(bis_hi[:, :], bis_p[:, :], bis_mid[:, :], bis_hi[:, :], add_drain=True),
                   waits=deps(pv))
            selv = P.emit("v", lambda e: e.select(bis_lo[:, :], bis_pi[:, :], bis_mid[:, :], bis_lo[:, :], add_drain=True),
                          inc=ve)
            if i < NBIS - 1:
                t1s = P.emit("s", lambda e: e.activation(
                    bis_t1[:, :], bis_lo[:, :], AF.Identity, bias=bis_hi[:, 0:1]),
                    waits=deps(selv), inc=se)
                smid = P.emit("s", lambda e: e.mul(bis_mid[:, :], bis_t1[:, :], 0.5),
                              waits=deps(t1s), inc=se)
        lo2s = P.emit("s", lambda e: e.copy(lo2[:, :], bis_lo[:, :]), waits=deps(selv), inc=se)
        gt1 = P.emit("v", lambda e: e.scalar_tensor_tensor(
            gtmp[:, :], logitsT[:, :], lo2[0:E, 0:1], logitsT[:, :], ALU.is_gt, ALU.mult),
            waits=deps(lo2s, cin_st), inc=ve)
        gatesv = P.emit("v", lambda e: e.tensor_copy(logitsT[:, :], gtmp[:, :]),
                        waits=deps(gt1), inc=ve)
        gtok_state = {"done": None}

        def emit_gtok():
            gts = None
            for tt in range(NKT):
                gtp = P.emit("t", lambda e, tt=tt: e.transpose(
                    pb[1][:, 0:E], logitsT[:, tt * 128:(tt + 1) * 128], identf[0:E, 0:E]),
                    waits=deps(gatesv, pbr[1].r), inc=te)
                pbr[1].w = gtp
                gts = P.emit("s", lambda e, tt=tt: e.copy(g_tok[:, tt * E:(tt + 1) * E], pb[1][:, 0:E]),
                             waits=deps(gtp), inc=se)
                pbr[1].r = gts
            gtok_state["done"] = gts

        if "D" not in PH:
            P.emit("y", lambda e: e.nop(),
                   waits=[(syT, syT.n), (sx1s, sx1s.n), (sxT[0], sxT[0].n), (sxT[1], sxT[1].n),
                          (scc, scc.n), (ccs, 1)])
            return _finish(nc, P, block_holder)
        x1TsbR = Res()
        hR = [Res() for _ in range(NHC)]
        b1R = [Res(), Res()]
        w1R = [Res(), Res()]
        w2R = [Res(), Res()]
        x1rR2 = [Res(), Res()]
        lnxR2 = [Res(), Res()]
        lt0R2 = [Res(), Res()]
        lnmR2 = [Res(), Res()]
        lvsR2 = [Res(), Res()]
        lrsR2 = [Res(), Res()]
        accdep = {}
        out_state = {"st": None}

        def emit_tbend_ln(tb, tt8):
            p8 = tt8 % 2
            gtt = tb * TT8 + tt8
            x1ld = dma("y", sx1r[p8], x1r[p8][:, :], x1_d[gtt * 128:(gtt + 1) * 128, :],
                       waits=deps(x1_store_done, x1rR2[p8].r, x1rR2[p8].w))
            x1rR2[p8].w = x1ld
            rv2 = P.emit("g", lambda e, tt8=tt8, p8=p8: e.tensor_tensor(
                lnx[p8][:, :], accs[:, tt8 * C:(tt8 + 1) * C], x1r[p8][:, :], ALU.add),
                waits=deps(x1ld, lnxR2[p8].r, lnxR2[p8].w) + [accdep[(tt8, d2)] for d2 in range(ND2)],
                inc=ge)
            lnxR2[p8].w = rv2
            x1rR2[p8].r = rv2
            msv = P.emit("s", lambda e, p8=p8: e.activation(
                x1r[p8][:, :], lnx[p8][:, :], AF.Identity, accum_out=lnt2[:, p8:p8 + 1]),
                waits=deps(rv2, x1rR2[p8].w, lt0R2[p8].r), inc=se)
            x1rR2[p8].w = msv
            lt0R2[p8].w = msv
            nmv = P.emit("v", lambda e, p8=p8: e.tensor_scalar(
                lnn2[:, p8:p8 + 1], lnt2[:, p8:p8 + 1], -1.0 / C, None, ALU.mult),
                waits=deps(msv, lnmR2[p8].r), inc=ve)
            lnmR2[p8].w = nmv
            lt0R2[p8].r = nmv
            xcv = P.emit("s", lambda e, p8=p8: e.activation(
                x1r[p8][:, :], lnx[p8][:, :], AF.Identity, bias=lnn2[:, p8:p8 + 1]),
                waits=deps(nmv, msv), inc=se)
            x1rR2[p8].w = xcv
            lnmR2[p8].r = xcv
            lnxR2[p8].r = xcv
            sqv = P.emit("s", lambda e, p8=p8: e.activation(
                lnx[p8][:, :], x1r[p8][:, :], AF.Square, accum_out=lnv2[:, p8:p8 + 1]),
                waits=deps(xcv, lvsR2[p8].r), inc=se)
            lnxR2[p8].w = sqv
            lvsR2[p8].w = sqv
            lnv = P.emit("s", lambda e, p8=p8: e.activation(
                lnt2[:, p8:p8 + 1], lnv2[:, p8:p8 + 1], AF.Ln, scale=1.0 / C, bias=1e-5),
                waits=deps(sqv, lt0R2[p8].r, lt0R2[p8].w), inc=se)
            lt0R2[p8].w = lnv
            lvsR2[p8].r = lnv
            rsv2 = P.emit("s", lambda e, p8=p8: e.activation(
                lnr2[:, p8:p8 + 1], lnt2[:, p8:p8 + 1], AF.Exp, scale=-0.5),
                waits=deps(lnv, lrsR2[p8].r), inc=se)
            lrsR2[p8].w = rsv2
            lt0R2[p8].r = rsv2
            ov = P.emit("v", lambda e, p8=p8: e.tensor_scalar(
                lnx[p8][:, :], x1r[p8][:, :], lnr2[:, p8:p8 + 1], None, ALU.mult),
                waits=deps(rsv2, sqv, out_state["st"]), inc=ve)
            lnxR2[p8].w = ov
            x1rR2[p8].r = ov
            lrsR2[p8].r = ov
            st = dma("y", sout, out_h[gtt * 128:(gtt + 1) * 128, :], lnx[p8][:, :],
                     waits=deps(ov))
            lnxR2[p8].r = st
            out_state["st"] = st
            for d2 in range(ND2):
                accdep[(tt8, d2)] = rv2

        b_done = [logits_done] if logits_done else []
        carry = {}

        for tb in range(NTB):
            bd0 = b_done + [(sxT[0], sxT[0].n), (sxT[1], sxT[1].n)] if tb == 0 else []
            if ("xT", tb) in carry:
                x1Tld = carry.pop(("xT", tb))
            else:
                xw = deps((sxT[0], sxT[0].n), (sxT[1], sxT[1].n), x1TsbR.r, logits_done)
                dma("y", sx1T, x1T_sb[:, 0:(CC // 2) * TB],
                    x1T_d.ap().rearrange("(c p) t -> p c t", p=128)[:, 0:CC // 2, tb * TB:(tb + 1) * TB],
                    waits=xw)
                x1Tld = dma("s", sx1T, x1T_sb[:, (CC // 2) * TB:],
                            x1T_d.ap().rearrange("(c p) t -> p c t", p=128)[:, CC // 2:CC, tb * TB:(tb + 1) * TB],
                            waits=xw)
                x1TsbR.w = x1Tld

            def emit_b2(tb=tb):
                for tt8 in range(TT8):
                    for d2 in range(ND2):
                        bki = 4 + (tt8 * ND2 + d2) % 2
                        bmm = P.emit("t", lambda e, tb=tb, tt8=tt8, d2=d2, bki=bki: e.matmul(
                            pb[bki][:, 0:DW],
                            logitsT[:, (tb * TT8 + tt8) * 128:(tb * TT8 + tt8 + 1) * 128],
                            b2b[:, d2 * DW:(d2 + 1) * DW], start=True, stop=True),
                            waits=deps(gatesv, pbr[bki].r, setup_l2), inc=te)
                        pbr[bki].w = bmm
                        bcv = P.emit("v", lambda e, tt8=tt8, d2=d2, bki=bki: e.tensor_copy(
                            accs[:, tt8 * C + d2 * DW: tt8 * C + (d2 + 1) * DW], pb[bki][:, 0:DW]),
                            waits=deps(bmm, accdep.get((tt8, d2))), inc=ve)
                        pbr[bki].r = bcv
                        accdep[(tt8, d2)] = bcv


            for ei in range(E):
                epar = ei % 2
                if ("b1", tb, ei) in carry:
                    b1ld = carry.pop(("b1", tb, ei))
                    w1ld = carry.pop(("w1", tb, ei))
                else:
                    b1ld = dma("g", sb1[epar], b1e[epar][:, :], b1r_h[ei, :, :],
                               waits=deps(b1R[epar].r) + (bd0 if ei < 2 else []))
                    b1R[epar].w = b1ld
                    w1ld = {}
                    for f in range(min(2, NF1)):
                        w1ld[f] = dma("g", sw1[f % 2], w1b[f % 2][:, :],
                                      w1c_d.ap()[ei].rearrange("(c p) h -> p c h", p=128)[:, :, f * 512:(f + 1) * 512],
                                      waits=deps(w1R[f % 2].r) + (bd0 if ei == 0 else []) + cast_gate)
                for hc in range(NHC):
                    f = hc // 4
                    hpar = hc % 2
                    bks = [hpar * NTS + tsv for tsv in range(NTS)]
                    mm_last = [None] * NTS
                    for dc in range(CC):
                        for tsv in range(NTS):
                            bki = bks[tsv]
                            tw = deps(w1ld[f], x1Tld, pbr[bki].r) if dc == 0 else ()
                            ml = P.emit("t", lambda e, f=f, hc=hc, dc=dc, tsv=tsv, bki=bki: e.matmul(
                                pb[bki][:, 0:TSW],
                                w1b[f % 2][:, dc * 512 + (hc % 4) * 128: dc * 512 + (hc % 4) * 128 + 128],
                                x1T_sb[:, dc * TB + tsv * TSW: dc * TB + (tsv + 1) * TSW],
                                start=(dc == 0), stop=(dc == CC - 1)),
                                waits=tw, inc=te if dc == CC - 1 else None)
                            if dc == CC - 1:
                                mm_last[tsv] = ml
                                pbr[bki].w = ml
                                w1R[f % 2].r = ml
                                x1TsbR.r = ml
                    if hc % 4 == 3 and f + 2 < NF1:
                        w1ld[f + 2] = dma("g", sw1[f % 2], w1b[f % 2][:, :],
                                          w1c_d.ap()[ei].rearrange("(c p) h -> p c h", p=128)[:, :, (f + 2) * 512:(f + 3) * 512],
                                          waits=deps(w1R[f % 2].r) + cast_gate)
                    gv = None
                    for tsv in range(NTS):
                        bki = bks[tsv]
                        gv = P.emit("s", lambda e, hc=hc, tsv=tsv, bki=bki, epar=epar: e.activation(
                            h_sb[:, hc * TB + tsv * TSW: hc * TB + (tsv + 1) * TSW],
                            pb[bki][:, 0:TSW],
                            AF.Gelu_apprx_tanh, bias=b1e[epar][:, hc:hc + 1]),
                            waits=deps(mm_last[tsv], b1R[epar].w, hR[hc].r), inc=se)
                        pbr[bki].r = gv
                    hR[hc].w = gv
                    b1R[epar].r = gv
                if ei == 0:
                    if tb == 0:
                        emit_gtok()
                    emit_b2()
                if ei == E - 1 and tb + 1 < NTB:
                    nxw = deps((sxT[0], sxT[0].n), (sxT[1], sxT[1].n), x1TsbR.r, logits_done)
                    dma("y", sx1T, x1T_sb[:, 0:(CC // 2) * TB],
                        x1T_d.ap().rearrange("(c p) t -> p c t", p=128)[:, 0:CC // 2, (tb + 1) * TB:(tb + 2) * TB],
                        waits=nxw)
                    nxT = dma("s", sx1T, x1T_sb[:, (CC // 2) * TB:],
                              x1T_d.ap().rearrange("(c p) t -> p c t", p=128)[:, CC // 2:CC, (tb + 1) * TB:(tb + 2) * TB],
                              waits=nxw)
                    x1TsbR.w = nxT
                    carry[("xT", tb + 1)] = nxT
                    nb1 = dma("g", sb1[0], b1e[0][:, :], b1r_h[0, :, :],
                              waits=deps(b1R[0].r))
                    b1R[0].w = nb1
                    nw1 = {}
                    for f in range(min(2, NF1)):
                        nw1[f] = dma("g", sw1[f % 2], w1b[f % 2][:, :],
                                     w1c_d.ap()[0].rearrange("(c p) h -> p c h", p=128)[:, :, f * 512:(f + 1) * 512],
                                     waits=deps(w1R[f % 2].r) + cast_gate)
                    carry[("b1", tb + 1, 0)] = nb1
                    carry[("w1", tb + 1, 0)] = nw1
                w2ld = {}
                for g8 in range(min(2, ND2 * NG8)):
                    dh0, g80 = divmod(g8, NG8)
                    w2ld[(dh0, g80)] = dma(
                        "g", sw2[g8 % 2], w2b[g8 % 2][:, :],
                        w2c_d.ap()[ei].rearrange("(g p) d -> p g d", p=128)[:, g80 * GH:(g80 + 1) * GH, dh0 * DW:(dh0 + 1) * DW],
                        waits=deps(w2R[g8 % 2].r) + (bd0 if ei == 0 else []) + cast_gate)
                fidx = 0
                last_w2_mm = None
                for dh in range(ND2):
                    for wv_ in range(1):
                        tts = list(range(TT8))
                        for g8 in range(NG8):
                            bpar = (dh * NG8 + g8) % 2
                            glast = {}
                            for hg in range(GH):
                                hc = g8 * GH + hg
                                for ti, tt8 in enumerate(tts):
                                    bki = (4 + ti) % 8
                                    tw = ()
                                    if hg == 0:
                                        tw = deps(w2ld[(dh, g8)], hR[g8 * GH + GH - 1].w)
                                        if g8 == 0:
                                            tw = tw + deps(pbr[bki].r)
                                    isl = (g8 == NG8 - 1 and hg == GH - 1)
                                    m2 = P.emit("t", lambda e, hc=hc, tt8=tt8, hg=hg, bki=bki, bpar=bpar, g8=g8, isl=isl: e.matmul(
                                        pb[bki][:, 0:DW],
                                        h_sb[:, hc * TB + tt8 * 128: hc * TB + tt8 * 128 + 128],
                                        w2b[bpar][:, hg * DW:(hg + 1) * DW],
                                        start=(g8 == 0 and hg == 0), stop=isl),
                                        waits=tw, inc=te if (isl or hg == GH - 1) else None)
                                    if hg == GH - 1:
                                        glast[ti] = m2
                                    if isl:
                                        pbr[bki].w = m2
                                        last_w2_mm = m2
                            grp_done = glast[len(tts) - 1]
                            w2R[bpar].r = grp_done
                            if True:
                                nidx = dh * NG8 + g8 + 2
                                if nidx < ND2 * NG8:
                                    ndh, ng8 = divmod(nidx, NG8)
                                    w2ld[(ndh, ng8)] = dma(
                                        "g", sw2[nidx % 2], w2b[nidx % 2][:, :],
                                        w2c_d.ap()[ei].rearrange("(g p) d -> p g d", p=128)[:, ng8 * GH:(ng8 + 1) * GH, ndh * DW:(ndh + 1) * DW],
                                        waits=deps(w2R[nidx % 2].r) + cast_gate)
                        for ti, tt8 in enumerate(tts):
                            bki = (4 + ti) % 8
                            gcol = (tb * TT8 + tt8) * E + ei
                            av = P.emit("v", lambda e, tt8=tt8, dh=dh, bki=bki, gcol=gcol: e.scalar_tensor_tensor(
                                accs[:, tt8 * C + dh * DW: tt8 * C + (dh + 1) * DW],
                                pb[bki][:, 0:DW], g_tok[:, gcol:gcol + 1],
                                accs[:, tt8 * C + dh * DW: tt8 * C + (dh + 1) * DW],
                                ALU.mult, ALU.add),
                                waits=deps(pbr[bki].w, accdep.get((tt8, dh)), gtok_state["done"]), inc=ve)
                            pbr[bki].r = av
                            accdep[(tt8, dh)] = av
                            if ei == E - 1 and dh == ND2 - 1:
                                emit_tbend_ln(tb, tt8)
                for hc in range(NHC):
                    hR[hc].r = last_w2_mm

        P.emit("y", lambda e: e.nop(),
               waits=[(syT, syT.n), (sx1s, sx1s.n), (sxT[0], sxT[0].n), (sxT[1], sxT[1].n),
                      (sout, sout.n), (scc, scc.n)])

        nc._sems = dict(te=te.n, ve=ve.n, se=se.n, sld=sld.n, syT=syT.n, sout=sout.n)
        nc._dbg = dict(bis_lo=bis_lo, bis_hi=bis_hi, bis_mid=bis_mid, bis_tot=bis_tot, bis_p=bis_p, bis_cnt=bis_cnt, g_tok=g_tok, logitsT=logitsT,
                       h_sb=h_sb, accs=accs, x1T_sb=x1T_sb, lg_all=lg_all)

        return _finish(nc, P, block_holder)


def _finish(nc, P, ctx):
    with nc.Block() as block:
        @block.tensor
        def _(eng):
            for f in P.ops["t"]:
                f(eng)

        @block.vector
        def _(eng):
            for f in P.ops["v"]:
                f(eng)

        @block.scalar
        def _(eng):
            for f in P.ops["s"]:
                f(eng)

        @block.gpsimd
        def _(eng):
            for f in P.ops["g"]:
                f(eng)

        @block.sync
        def _(eng):
            for f in P.ops["y"]:
                f(eng)
    return nc


def make_in_maps(c, inputs):
    A = lambda z: np.ascontiguousarray(np.asarray(z, dtype=np.float32))
    x_all = A(inputs["L0_attn_x"])
    E, NHC = c["E"], c["NHC"]
    shared = {
        "wq": A(inputs["L0_attn_Wq_down"]),
        "wk": A(inputs["L0_attn_Wk_down"]),
        "wv": A(inputs["L0_attn_Wv_down"]),
        "upcq": A(inputs["L0_attn_Wq_up_c"]),
        "upeq": A(inputs["L0_attn_Wq_up_e"]),
        "upeqT": A(np.asarray(inputs["L0_attn_Wq_up_e"], np.float32).T),
        "upck": A(inputs["L0_attn_Wk_up_c"]),
        "upek": A(inputs["L0_attn_Wk_up_e"]),
        "upekT": A(np.asarray(inputs["L0_attn_Wk_up_e"], np.float32).T),
        "wvup": A(inputs["L0_attn_Wv_up"]),
        "wc": A(inputs["L0_attn_Wc"]),
        "gwT": A(np.asarray(inputs["L0_ffn_gate_weight"], np.float32).T),
        "w1T": A(np.asarray(inputs["L0_ffn_w1"], np.float32).transpose(0, 2, 1)),
        "w2T": A(np.asarray(inputs["L0_ffn_w2"], np.float32).transpose(0, 2, 1)),
        "b1r": A(np.asarray(inputs["L0_ffn_b1"], np.float32).reshape(E, NHC, 128).transpose(0, 2, 1)),
        "b2": A(inputs["L0_ffn_b2"]),
    }
    in_maps = []
    for i in range(c["NCORES"]):
        m = dict(shared)
        m["x"] = A(x_all[i])
        m["xT"] = A(x_all[i].T)
        in_maps.append(m)
    return in_maps


def kernel(**inputs):
    c = make_cfg()
    nc = build_program(c)
    in_maps = make_in_maps(c, inputs)
    res = run_bass_kernel_spmd(nc, in_maps, core_ids=list(range(c["NCORES"])))
    out = np.stack([np.asarray(res.results[i]["out"], dtype=np.float32)
                    for i in range(c["NCORES"])], axis=0)
    return out


# version 16 (final)
# speedup vs baseline: 1.0426x; 1.0055x over previous
import numpy as np
import ml_dtypes
import concourse.bass as bass
import concourse.mybir as mybir
from concourse.bass_utils import run_bass_kernel_spmd
from contextlib import ExitStack

F32 = mybir.dt.float32
BF16 = mybir.dt.bfloat16
U32 = mybir.dt.uint32
AF = mybir.ActivationFunctionType
ALU = mybir.AluOpType
AX = mybir.AxisListType

GA = 0.7978845608028654
GB = 0.044715
NEG = -3.0e38
ESZ = {F32: 4, BF16: 2, U32: 4}


def make_cfg(T=4096, C=1024, NH=16, HID=4096, E=8, TB=1024, NBIS=44, KTH=24576, NCORES=8):
    c = dict(T=T, C=C, NH=NH, HID=HID, E=E, TB=TB, NBIS=NBIS, KTH=KTH, NCORES=NCORES)
    c["D"] = 64
    c["DHE"] = 32
    c["NLC"] = NH // 2
    c["CC"] = C // 128
    c["TQ"] = T // 512
    c["NKT"] = T // 128
    c["NHC"] = HID // 128
    c["NTB"] = T // TB
    c["TT8"] = TB // 128
    c["NTS"] = max(1, TB // 512)
    c["TSW"] = TB // c["NTS"]
    c["ND2"] = max(1, C // 512)
    c["DW"] = C // c["ND2"]
    c["GH"] = min(8, c["NHC"])
    c["LGC"] = NCORES * E * T // 128
    assert C == NH * 64 and T % 512 == 0 and TB % 128 == 0 and HID % 512 == 0
    assert c["TSW"] <= 512 and c["DW"] <= 512
    return c


class Sem:
    def __init__(self, nc, ctx, name):
        self.h = ctx.enter_context(nc.semaphore(name))
        self.n = 0


class Res:
    __slots__ = ("w", "r")

    def __init__(self):
        self.w = None
        self.r = None


def deps(*items):
    return [it for it in items if it is not None]


class Prog:
    def __init__(self):
        self.ops = {"t": [], "v": [], "s": [], "g": [], "y": []}
        self.waited = {k: {} for k in self.ops}

    def emit(self, eng, builder, waits=(), inc=None, incn=1):
        seen = {}
        for (s, v) in waits:
            if v and v > 0:
                key = id(s.h)
                if key not in seen or seen[key][1] < v:
                    seen[key] = (s.h, v, s)
        done = self.waited[eng]
        wl = []
        for key, (h, v, s) in seen.items():
            if done.get(key, -1) >= v:
                continue
            done[key] = v
            wl.append((h, v))
        if inc is not None:
            inc.n += incn
        ih = inc.h if inc is not None else None

        def run(e, wl=wl, builder=builder, ih=ih, incn=incn):
            for h, v in wl:
                e.wait_ge(h, v)
            inst = builder(e)
            if ih is not None:
                inst.then_inc(ih, incn)
        self.ops[eng].append(run)
        return (inc, inc.n) if inc is not None else None


def rope_tables(T, DHE):
    inv = 1.0 / (10000.0 ** (np.arange(0, DHE, 2, dtype=np.float64) / DHE))
    ang = np.arange(T, dtype=np.float64)[:, None] * inv[None, :]
    emb = np.concatenate([ang, ang], axis=1)
    cosR = np.tile(np.cos(emb).T, (4, 1)).astype(ml_dtypes.bfloat16)
    sinR = np.tile(np.sin(emb).T, (4, 1)).astype(ml_dtypes.bfloat16)
    return cosR, sinR


def perm_matrix(DHE):
    Pm = np.zeros((DHE, DHE), dtype=np.float32)
    for i in range(DHE // 2):
        Pm[2 * i + 1, 2 * i] = -1.0
        Pm[2 * i, 2 * i + 1] = 1.0
    return Pm.astype(ml_dtypes.bfloat16)


def build_program(c):
    T, C, NH, HID, E, TB = c["T"], c["C"], c["NH"], c["HID"], c["E"], c["TB"]
    D, DHE, NLC, CC, TQ, NKT = c["D"], c["DHE"], c["NLC"], c["CC"], c["TQ"], c["NKT"]
    NHC, NTB, TT8 = c["NHC"], c["NTB"], c["TT8"]
    NTS, TSW, ND2, DW, GH = c["NTS"], c["TSW"], c["ND2"], c["DW"], c["GH"]
    NBIS, KTH, NCORES, LGC = c["NBIS"], c["KTH"], c["NCORES"], c["LGC"]
    UW = TB
    NG8 = NHC // GH
    NF1 = max(1, NHC // 4)
    NWV = (TT8 + 3) // 4

    nc = bass.Bass("TRN2", target_bir_lowering=False, debug=False)

    x_h = nc.dram_tensor("x", [T, C], F32, kind="ExternalInput")
    xT_h = nc.dram_tensor("xT", [C, T], F32, kind="ExternalInput")
    wq_h = nc.dram_tensor("wq", [C, C], F32, kind="ExternalInput")
    wk_h = nc.dram_tensor("wk", [C, C], F32, kind="ExternalInput")
    wv_h = nc.dram_tensor("wv", [C, C], F32, kind="ExternalInput")
    upcq_h = nc.dram_tensor("upcq", [D, DHE], F32, kind="ExternalInput")
    upeq_h = nc.dram_tensor("upeq", [D, DHE], F32, kind="ExternalInput")
    upeqT_h = nc.dram_tensor("upeqT", [DHE, D], F32, kind="ExternalInput")
    upck_h = nc.dram_tensor("upck", [D, DHE], F32, kind="ExternalInput")
    upek_h = nc.dram_tensor("upek", [D, DHE], F32, kind="ExternalInput")
    upekT_h = nc.dram_tensor("upekT", [DHE, D], F32, kind="ExternalInput")
    wvup_h = nc.dram_tensor("wvup", [D, D], F32, kind="ExternalInput")
    wc_h = nc.dram_tensor("wc", [C, C], F32, kind="ExternalInput")
    gwT_h = nc.dram_tensor("gwT", [C, E], F32, kind="ExternalInput")
    w1T_h = nc.dram_tensor("w1T", [E, C, HID], F32, kind="ExternalInput")
    w2T_h = nc.dram_tensor("w2T", [E, HID, C], F32, kind="ExternalInput")
    b1r_h = nc.dram_tensor("b1r", [E, 128, NHC], F32, kind="ExternalInput")
    b2_h = nc.dram_tensor("b2", [E, C], F32, kind="ExternalInput")
    out_h = nc.dram_tensor("out", [T, C], F32, kind="ExternalOutput")

    yT_d = nc.dram_tensor("yT_d", [C, T], BF16)
    x1_d = nc.dram_tensor("x1_d", [T, C], F32)
    x1T_d = nc.dram_tensor("x1T_d", [C, T], BF16)
    cc_in = nc.dram_tensor("cc_in", [E, T], F32)
    cc_out = nc.dram_tensor("cc_out", [NCORES * E, T], F32)
    w1c_d = nc.dram_tensor("w1c_d", [E, C, HID], BF16)
    w2c_d = nc.dram_tensor("w2c_d", [E, HID, C], BF16)

    mdiag_np = np.zeros((128, 128), dtype=np.float32)
    for kp in range(128):
        mdiag_np[kp, :kp] = NEG
    cosR, sinR = rope_tables(T, DHE)
    cpool_np = np.zeros((128, 4), np.float32)
    cpool_np[:, 0] = 1e-5
    consts = {
        "mdiag": mdiag_np,
        "mfull": np.full((128, 128), NEG, np.float32),
        "identf": np.eye(128, dtype=np.float32),
        "ones64c": np.ones((1, D), dtype=ml_dtypes.bfloat16),
        "ones128": np.ones((128, 128), dtype=np.float32),
        "pm": perm_matrix(DHE),
        "cosR": cosR,
        "sinR": sinR,
        "cpool": cpool_np,
    }
    cth = {k: nc.inline_tensor(v, k) for k, v in consts.items()}

    with ExitStack() as ctx:
        ARENA0 = nc.sbuf_base + 1536
        off = [ARENA0]

        def sba(name, shape, dt, at=None):
            fb = int(np.prod(shape[1:])) * ESZ[dt]
            o = (off[0] + 63) // 64 * 64 if at is None else at
            if at is None:
                off[0] = o + fb
            return nc.alloc_sbuf_tensor_at(name, shape, dt, offset=o)

        mdiag = sba("mdiag_sb", [128, 128], F32)
        mfull = sba("mfull_sb", [128, 128], F32)
        identf = sba("identf_sb", [128, 128], F32)
        ones64 = sba("ones64_sb", [1, D], BF16)
        ones128 = sba("ones128_sb", [128, 128], F32)
        cpool = sba("cpool_sb", [128, 4], F32)
        pm = sba("pm_sb", [DHE, DHE], BF16)
        upcq = sba("upcq_sb", [128, DHE], BF16)
        upeq = sba("upeq_sb", [128, DHE], BF16)
        upck = sba("upck_sb", [128, DHE], BF16)
        upek = sba("upek_sb", [128, DHE], BF16)
        upeqT = sba("upeqT_sb", [DHE, D], BF16)
        upekT = sba("upekT_sb", [DHE, D], BF16)
        wvup = sba("wvup_sb", [128, D], BF16)
        wpq = sba("wpq_sb", [128, DHE], BF16)
        wpk = sba("wpk_sb", [128, DHE], BF16)
        gwhi = sba("gwhi_sb", [128, CC * E], BF16)
        gwlo = sba("gwlo_sb", [128, CC * E], BF16)
        gwf = sba("gwf_sb", [128, CC * E], F32)
        logitsT = sba("logitsT_sb", [E, T], F32)
        g_tok = sba("g_tok_sb", [128, NKT * E], F32)
        b2b = sba("b2_sb", [E, C], F32)
        b1e = [sba(f"b1e{i}", [128, NHC], F32) for i in range(2)]
        bis_lo = sba("bis_lo", [128, 1], F32)
        bis_hi = sba("bis_hi", [128, 1], F32)
        bis_mid = sba("bis_mid", [128, 1], F32)
        bis_t1 = sba("bis_t1", [128, 1], F32)
        bis_cnt = sba("bis_cnt", [128, 1], F32)
        bis_tot = sba("bis_tot", [128, 1], F32)
        bis_p = sba("bis_p", [128, 1], U32)
        bis_pi = sba("bis_pi", [128, 1], U32)
        lo2 = sba("lo2_sb", [128, 1], F32)
        ln_nm = sba("ln_nm", [128, 1], F32)
        ln_vs = sba("ln_vs", [128, 1], F32)
        ln_t0 = sba("ln_t0", [128, 1], F32)
        ln_rstd = sba("ln_rstd", [128, 1], F32)
        a0 = (off[0] + 63) // 64 * 64

        off[0] = a0
        xT = sba("xT_sb", [128, CC * T], BF16)
        qT = sba("qT_sb", [128, T], BF16)
        kT = sba("kT_sb", [128, T], BF16)
        yTp = sba("yTp_sb", [128, T], BF16)
        vpair = sba("v_sb", [128, 2 * NKT * (D + 1)], BF16)
        latb = [sba(f"lat{i}", [128, T], BF16) for i in range(2)]
        wqb = [sba(f"wqb{i}", [128, CC * 128], BF16) for i in range(2)]
        wkb = [sba(f"wkb{i}", [128, CC * 128], BF16) for i in range(2)]
        wvb = [sba(f"wvb{i}", [128, CC * 128], BF16) for i in range(2)]
        cos = sba("cos_sb", [128, T], BF16)
        sin = sba("sin_sb", [128, T], BF16)
        expb = [sba(f"expb{i}", [128, 512], BF16) for i in range(4)]
        ysb = [sba(f"ysb{i}", [D + 1, 512], F32) for i in range(2)]
        rcp = [sba(f"rcp{i}", [1, 512], F32) for i in range(2)]
        rcpb = [sba(f"rcpb{i}", [1, 512], BF16) for i in range(2)]
        ropeA = sba("ropeA_sb", [128, 512], F32)
        ropeB = sba("ropeB_sb", [128, 512], F32)
        ropeC = sba("ropeC_sb", [128, 512], F32)
        ropeD = sba("ropeD_sb", [128, 512], F32)
        a_end = off[0]

        off[0] = a0
        wc_sb = sba("wc_sb", [128, CC * C], BF16)
        ytb = [sba(f"ytb{i}", [128, CC * 128], BF16) for i in range(2)]
        xb = [sba(f"xb{i}", [128, C], F32) for i in range(2)]
        r_sb = sba("r_sb", [128, C], F32)
        xc_sb = sba("xc_sb", [128, C], F32)
        sq_sb = sba("sq_sb", [128, C], F32)
        x1f = sba("x1f_sb", [128, C], F32)
        hi_b = sba("hi_sb", [128, C], BF16)
        lof = sba("lof_sb", [128, C], F32)
        x1Ts = [sba(f"x1Ts{i}", [128, CC * 128], BF16) for i in range(2)]
        loTs = sba("loTs_sb", [128, CC * 128], BF16)
        b_end = off[0]

        off[0] = a0
        x1T_sb = sba("x1T_sb", [128, CC * TB], BF16)
        h_sb = sba("h_sb", [128, NHC * TB], BF16)
        accs = sba("acc_sb", [128, TT8 * C], F32)
        w1b = [sba(f"w1b{i}", [128, CC * 512], BF16) for i in range(2)]
        w2b = [sba(f"w2b{i}", [128, GH * DW], BF16) for i in range(2)]
        x1r = [sba(f"x1r{i}_sb", [128, C], F32) for i in range(2)]
        lnx = [sba(f"lnx{i}_sb", [128, C], F32) for i in range(2)]
        lnt2 = sba("lnt2_sb", [128, 2], F32)
        lnn2 = sba("lnn2_sb", [128, 2], F32)
        lnv2 = sba("lnv2_sb", [128, 2], F32)
        lnr2 = sba("lnr2_sb", [128, 2], F32)
        d_end = off[0]
        assert max(a_end, b_end, d_end) < nc.sbuf_top - 64, (a_end, b_end, d_end, nc.sbuf_top)

        x1T_off = (a0 + 63) // 64 * 64
        h_off = (x1T_off + CC * TB * 2 + 63) // 64 * 64
        acc_off = (h_off + NHC * TB * 2 + 63) // 64 * 64
        if LGC * 6 + T * 4 <= TT8 * C * 4:
            lg_all = sba("lg_all_sb", [128, LGC], F32, at=acc_off)
            cmpb = sba("cmpb_sb", [128, LGC], BF16, at=acc_off + LGC * 4)
            gtmp = sba("gtmp_sb", [E, T], F32, at=acc_off + LGC * 6)
        else:
            lg_all = sba("lg_all_sb", [128, LGC], F32)
            cmpb = sba("cmpb_sb", [128, LGC], BF16)
            gtmp = sba("gtmp_sb", [E, T], F32)
            d_end = off[0]
            assert d_end < nc.sbuf_top - 64, d_end

        pb = [ctx.enter_context(nc.psum_tensor(f"pb{i}", [128, 512], F32)) for i in range(8)]
        pbr = [Res() for _ in range(8)]

        sld = Sem(nc, ctx, "sld")
        sl2 = Sem(nc, ctx, "sl2")
        wst = [Sem(nc, ctx, f"wst{i}") for i in range(2)]
        swc = Sem(nc, ctx, "swc")
        syt = [Sem(nc, ctx, f"syt{i}") for i in range(2)]
        sxb = [Sem(nc, ctx, f"sxb{i}") for i in range(2)]
        sx1s = Sem(nc, ctx, "sx1s")
        sxT = [Sem(nc, ctx, f"sxT{i}") for i in range(2)]
        syT = Sem(nc, ctx, "syT")
        scc = Sem(nc, ctx, "scc")
        slg = Sem(nc, ctx, "slg")
        sx1T = Sem(nc, ctx, "sx1T")
        sb1 = [Sem(nc, ctx, f"sb1{i}") for i in range(2)]
        sw1 = [Sem(nc, ctx, f"sw1{i}") for i in range(2)]
        sw2 = [Sem(nc, ctx, f"sw2{i}") for i in range(2)]
        sx1r = [Sem(nc, ctx, "sx1ra"), Sem(nc, ctx, "sx1rb")]
        sout = Sem(nc, ctx, "sout")
        te = Sem(nc, ctx, "te")
        ve = Sem(nc, ctx, "ve")
        se = Sem(nc, ctx, "se")
        ge = Sem(nc, ctx, "ge")
        scst = Sem(nc, ctx, "scst")
        ccs = Sem(nc, ctx, "ccs")

        nc.const_aps.register(F32, 1e-5, cpool[:, 0:1])

        block_holder = ctx

        P = Prog()

        def dma(eng, sem, out_ap, in_ap, waits=()):
            return P.emit(eng, lambda e, o=out_ap, i=in_ap: e.dma_start(out=o, in_=i),
                          waits=waits, inc=sem, incn=16)

        for name, dst in [("mdiag", mdiag), ("mfull", mfull), ("identf", identf),
                          ("ones64c", ones64), ("ones128", ones128), ("cpool", cpool),
                          ("pm", pm), ("cosR", cos), ("sinR", sin)]:
            dma("y", sl2, dst[:, :], cth[name][:, :])
        dma("y", sl2, gwf[:, :], gwT_h.ap().rearrange("(c p) e -> p c e", p=128))
        dma("y", sl2, b2b[:, :], b2_h[:, :])
        setup_l2 = (sl2, sl2.n)

        for dst, src in [(upcq, upcq_h), (upeq, upeq_h), (upck, upck_h), (upek, upek_h),
                         (wvup, wvup_h)]:
            dma("g", sld, dst[0:D, :], src[:, :])
            dma("g", sld, dst[D:128, :], src[:, :])
        dma("g", sld, upeqT[:, :], upeqT_h[:, :])
        dma("g", sld, upekT[:, :], upekT_h[:, :])
        dma("g", sld, gwhi[:, :], gwT_h.ap().rearrange("(c p) e -> p c e", p=128))
        for i in range(CC):
            dma("g", sld, xT[:, i * T:(i + 1) * T],
                xT_h.ap().rearrange("(c p) t -> p c t", p=128)[:, i, :])
        setup_ld = (sld, sld.n)
        pair_w_ld = {}
        for lci in range(min(2, NLC)):
            for wb, wh in ((wqb, wq_h), (wkb, wk_h), (wvb, wv_h)):
                dma("g", wst[lci % 2], wb[lci][:, :],
                    wh.ap().rearrange("(c p) l -> p c l", p=128)[:, :, lci * 128:(lci + 1) * 128])
            pair_w_ld[lci] = (wst[lci % 2], wst[lci % 2].n)

        gwlo_v = P.emit("v", lambda e: e.tensor_sub(gwlo[:, :], gwf[:, :], gwhi[:, :]),
                        waits=[setup_l2, setup_ld], inc=ve)
        nvl = 2 * NKT
        vset = None
        for j in range(nvl):
            vset = P.emit("v", lambda e, j=j: e.memset(
                vpair[:, j * (D + 1) + D: j * (D + 1) + D + 1], 1.0),
                waits=deps(setup_ld) if j == 0 else (),
                inc=ve if j == nvl - 1 else None)

        wp_done = None
        for wp, upT in ((wpq, upeqT), (wpk, upekT)):
            last_mm = None
            for h in range(2):
                last_mm = P.emit("t", lambda e, upT=upT, h=h: e.matmul(
                    pb[6][h * D:h * D + D, 0:DHE], upT[:, :], pm[:, :],
                    start=True, stop=True, tile_position=(0, h * D)),
                    waits=deps(setup_ld, setup_l2, vset, pbr[6].r), inc=te)
            pbr[6].w = last_mm
            wp_done = P.emit("s", lambda e, wp=wp: e.copy(wp[:, :], pb[6][:, 0:DHE]),
                             waits=deps(pbr[6].w), inc=se)
            pbr[6].r = wp_done

        NB4 = min(4, T // 512)
        THN = T // (512 * NB4)
        N8 = T // 512
        latr = [Res(), Res()]
        qTr, kTr, vpr, yTr = Res(), Res(), Res(), Res()
        expr = [Res() for _ in range(4)]
        ysbr = [Res(), Res()]
        ropeAr, ropeBr = Res(), Res()
        wbfree = {0: None, 1: None}
        pend_norm = [None]
        pend_normb = [None]

        def emit_norm_a(qb, mmyd):
            rbvs = []
            for h in range(2):
                yev = P.emit("s", lambda e, h=h: e.copy(ysb[h][:, :], pb[4 + h][0:D + 1, :]),
                             waits=deps(mmyd[h], ysbr[h].r), inc=se)
                ysbr[h].w = yev
                pbr[4 + h].r = yev
                rv = P.emit("v", lambda e, h=h: e.reciprocal(rcp[h][:, :], ysb[h][D:D + 1, :]),
                            waits=deps(yev), inc=ve)
                rbv = P.emit("v", lambda e, h=h: e.tensor_copy(rcpb[h][:, :], rcp[h][:, :]),
                             waits=deps(rv), inc=ve)
                rbvs.append(rbv)
            return rbvs

        def emit_norm_b(qb, rbvs):
            for h in range(2):
                bct = P.emit("t", lambda e, h=h: e.matmul(
                    pb[6 + h][0:D, :], ones64[:, :], rcpb[h][:, :], start=True, stop=True),
                    waits=deps(rbvs[h], pbr[6 + h].r), inc=te)
                pbr[6 + h].w = bct
                ym = P.emit("v", lambda e, h=h, qb=qb: e.tensor_tensor(
                    yTp[h * D:h * D + D, qb * 512:(qb + 1) * 512],
                    ysb[h][0:D, :], pb[6 + h][0:D, 0:512], ALU.mult),
                    waits=deps(bct, yTr.r), inc=ve)
                ysbr[h].r = ym
                pbr[6 + h].r = ym
                yTr.w = ym

        cast_chunks = []
        for ei in range(E):
            cp1 = max(1, C // 128)
            r1 = C // cp1
            for p in range(cp1):
                cast_chunks.append((w1c_d[ei, p * r1:(p + 1) * r1, :],
                                    w1T_h[ei, p * r1:(p + 1) * r1, :]))
        for ei in range(E):
            cp2 = max(1, HID // 512)
            r2 = HID // cp2
            for p in range(cp2):
                cast_chunks.append((w2c_d[ei, p * r2:(p + 1) * r2, :],
                                    w2T_h[ei, p * r2:(p + 1) * r2, :]))

        for lc in range(NLC):
            for wi in range(3):
                wb = (wqb, wkb, wvb)[wi]
                lbi = 0 if wi != 1 else 1
                lbuf = latb[lbi]
                lrs = latr[lbi]
                last_lat_mm = None
                for th in range(THN):
                    bank_done = [None] * NB4
                    for cci in range(CC):
                        for n4 in range(NB4):
                            tw = deps(pbr[n4].r, lrs.r, pair_w_ld.get(lc), setup_ld) if cci == 0 else ()
                            isl = cci == CC - 1
                            bd = P.emit("t", lambda e, wb=wb, lc=lc, cci=cci, th=th, n4=n4, isl=isl: e.matmul(
                                pb[n4][:, :], wb[lc % 2][:, cci * 128:(cci + 1) * 128],
                                xT[:, cci * T + th * 512 * NB4 + n4 * 512:
                                   cci * T + th * 512 * NB4 + (n4 + 1) * 512],
                                start=(cci == 0), stop=isl),
                                waits=tw, inc=te if isl else None)
                            if isl:
                                bank_done[n4] = bd
                    last_lat_mm = bank_done[NB4 - 1]
                    for n4 in range(NB4):
                        pbr[n4].w = bank_done[n4]
                        ev = P.emit("v", lambda e, lbuf=lbuf, th=th, n4=n4: e.tensor_copy(
                            lbuf[:, th * 512 * NB4 + n4 * 512: th * 512 * NB4 + (n4 + 1) * 512],
                            pb[n4][:, :]),
                            waits=deps(bank_done[n4]), inc=ve)
                        pbr[n4].r = ev
                        lrs.w = ev
                wbfree[lc % 2] = last_lat_mm

                if wi < 2:
                    dstT = qT if wi == 0 else kT
                    dstR = qTr if wi == 0 else kTr
                    upc = upcq if wi == 0 else upck
                    upe = upeq if wi == 0 else upek
                    wpp = wpq if wi == 0 else wpk
                    last_up_mm = None
                    rsc = [(ropeA, ropeB, ropeAr), (ropeC, ropeD, ropeBr)]
                    for n8 in range(N8):
                        lo_, hi_ = n8 * 512, (n8 + 1) * 512
                        ba = 4 + 2 * (n8 % 2)
                        bb = ba + 1
                        rA, rB, rR = rsc[n8 % 2]
                        specs = [(upc, 0, ba, 0), (upe, 0, ba, DHE), (wpp, 0, bb, DHE),
                                 (upc, 64, ba, 64), (upe, 64, ba, 64 + DHE), (wpp, 64, bb, 64 + DHE)]
                        mmv = []
                        for si, (wsb, hb, bki, po) in enumerate(specs):
                            tw = deps(lrs.w, pbr[ba].r, pbr[bb].r, wp_done) if si == 0 else ()
                            mmv.append(P.emit("t", lambda e, wsb=wsb, hb=hb, bki=bki, po=po,
                                              lbuf=lbuf, lo_=lo_, hi_=hi_: e.matmul(
                                pb[bki][po:po + DHE, 0:512], wsb[hb:hb + D, :],
                                lbuf[hb:hb + D, lo_:hi_],
                                start=True, stop=True, tile_position=(hb, po)),
                                waits=tw, inc=te))
                        last_up_mm = mmv[-1]
                        lastv = None
                        for h in range(2):
                            hb = 64 * h
                            P.emit("v", lambda e, hb=hb, dstT=dstT, lo_=lo_, hi_=hi_, ba=ba: e.tensor_copy(
                                dstT[hb:hb + DHE, lo_:hi_], pb[ba][hb:hb + DHE, 0:512]),
                                waits=deps(mmv[3 * h + 0], dstR.r, rR.r))
                            P.emit("v", lambda e, hb=hb, lo_=lo_, hi_=hi_, ba=ba, rA=rA: e.tensor_mul(
                                rA[hb + DHE:hb + 2 * DHE, 0:512],
                                pb[ba][hb + DHE:hb + 2 * DHE, 0:512],
                                cos[hb + DHE:hb + 2 * DHE, lo_:hi_]),
                                waits=deps(mmv[3 * h + 1]))
                            t2v = P.emit("v", lambda e, hb=hb, lo_=lo_, hi_=hi_, bb=bb, rB=rB: e.tensor_mul(
                                rB[hb + DHE:hb + 2 * DHE, 0:512],
                                pb[bb][hb + DHE:hb + 2 * DHE, 0:512],
                                sin[hb + DHE:hb + 2 * DHE, lo_:hi_]),
                                waits=deps(mmv[3 * h + 2]), inc=ve)
                            lastv = P.emit("v", lambda e, hb=hb, dstT=dstT, lo_=lo_, hi_=hi_, rA=rA, rB=rB: e.tensor_add(
                                dstT[hb + DHE:hb + 2 * DHE, lo_:hi_],
                                rA[hb + DHE:hb + 2 * DHE, 0:512],
                                rB[hb + DHE:hb + 2 * DHE, 0:512]),
                                waits=deps(t2v), inc=ve)
                        pbr[ba].r = lastv
                        pbr[bb].r = lastv
                        rR.r = lastv
                        dstR.w = lastv
                    lrs.r = last_up_mm
                else:
                    last_v_mm = None
                    for h in range(2):
                        for tt in range(NKT):
                            bki = 6 + (tt % 2)
                            vn = P.emit("t", lambda e, h=h, tt=tt, bki=bki, lbuf=lbuf: e.matmul(
                                pb[bki][:, 0:D], lbuf[h * D:h * D + D, tt * 128:(tt + 1) * 128],
                                wvup[h * D:h * D + D, :], start=True, stop=True,
                                tile_position=(h * D, 0)),
                                waits=deps(lrs.w, pbr[bki].r), inc=te)
                            pbr[bki].w = vn
                            last_v_mm = vn
                            ev = P.emit("v", lambda e, h=h, tt=tt, bki=bki: e.tensor_copy(
                                vpair[:, (h * NKT + tt) * (D + 1):(h * NKT + tt) * (D + 1) + D],
                                pb[bki][:, 0:D]),
                                waits=deps(vn, vpr.r), inc=ve)
                            pbr[bki].r = ev
                            vpr.w = ev
                    lrs.r = last_v_mm

            if lc + 2 < NLC:
                for wb, wh in ((wqb, wq_h), (wkb, wk_h), (wvb, wv_h)):
                    dma("g", wst[lc % 2], wb[lc % 2][:, :],
                        wh.ap().rearrange("(c p) l -> p c l", p=128)[:, :, (lc + 2) * 128:(lc + 3) * 128],
                        waits=deps(wbfree[lc % 2]))
                pair_w_ld[lc + 2] = (wst[lc % 2], wst[lc % 2].n)

            ncst = len(cast_chunks)
            per = -(-ncst // NLC)
            for ck in range(lc * per, min((lc + 1) * per, ncst)):
                dst, srcp = cast_chunks[ck]
                dma("g", scst, dst, srcp)

            last_mm_s = None
            for qb in range(TQ):
                if pend_norm[0] is not None:
                    pq, pmmyd = pend_norm[0]
                    pend_normb[0] = (pq, emit_norm_a(pq, pmmyd))
                    pend_norm[0] = None
                kcm = 4 * qb + 4
                mmy_done = [None, None]

                def emit_av(kc, h, qb=qb, kcm=kcm):
                    sbi = 2 * h + (kc % 2)
                    ebuf = expb[sbi]
                    sev = expr[sbi].w
                    yw = deps(sev)
                    if kc == 0:
                        yw = deps(sev, pbr[4 + h].r, vpr.w, yTr.r)
                    mmy = P.emit("t", lambda e, h=h, kc=kc, kcm=kcm, ebuf=ebuf: e.matmul(
                        pb[4 + h][0:D + 1, :],
                        vpair[:, (h * NKT + kc) * (D + 1):(h * NKT + kc + 1) * (D + 1)],
                        ebuf[:, :], start=(kc == 0), stop=(kc == kcm - 1)),
                        waits=yw, inc=te)
                    expr[sbi].r = mmy
                    if kc == kcm - 1:
                        pbr[4 + h].w = mmy
                        mmy_done[h] = mmy

                for kc in range(kcm):
                    par = kc % 2
                    for h in range(2):
                        sbi = 2 * h + par
                        mm = P.emit("t", lambda e, h=h, kc=kc, qb=qb, sbi=sbi: e.matmul(
                            pb[sbi][:, :], kT[h * D:h * D + D, kc * 128:(kc + 1) * 128],
                            qT[h * D:h * D + D, qb * 512:(qb + 1) * 512],
                            start=True, stop=True),
                            waits=deps(pbr[sbi].r, qTr.w, kTr.w), inc=te)
                        last_mm_s = mm
                        pbr[sbi].w = mm
                        rel = kc - 4 * qb
                        if rel >= 0:
                            mv = None
                            for tt in range(rel + 1):
                                msk = mfull if tt < rel else mdiag
                                mv = P.emit("v", lambda e, sbi=sbi, tt=tt, msk=msk: e.tensor_tensor(
                                    pb[sbi][:, tt * 128:(tt + 1) * 128],
                                    pb[sbi][:, tt * 128:(tt + 1) * 128], msk[:, :], ALU.add),
                                    waits=deps(mm) if tt == 0 else (),
                                    inc=ve if tt == rel else None)
                            pbr[sbi].w = mv
                        ebuf = expb[sbi]
                        sev = P.emit("s", lambda e, ebuf=ebuf, sbi=sbi: e.activation(
                            ebuf[:, :], pb[sbi][:, :], AF.Exp, scale=0.125),
                            waits=deps(pbr[sbi].w, expr[sbi].r), inc=se)
                        expr[sbi].w = sev
                        pbr[sbi].r = sev
                    if kc > 0:
                        for h in range(2):
                            emit_av(kc - 1, h)
                for h in range(2):
                    emit_av(kcm - 1, h)
                if pend_normb[0] is not None:
                    emit_norm_b(*pend_normb[0])
                    pend_normb[0] = None
                pend_norm[0] = (qb, list(mmy_done))
            if pend_norm[0] is not None:
                pq, pmmyd = pend_norm[0]
                emit_norm_b(pq, emit_norm_a(pq, pmmyd))
                pend_norm[0] = None
            qTr.r = last_mm_s
            kTr.r = last_mm_s
            st_y = dma("y", syT, yT_d[lc * 128:(lc + 1) * 128, :], yTp[:, :], waits=deps(yTr.w))
            yTr.r = st_y

        a_done = deps(yTr.r)

        PH = c.get("PHASES", "ABCD")
        if "B" not in PH:
            P.emit("y", lambda e: e.nop(), waits=[(syT, syT.n)])
            return _finish(nc, P, block_holder)
        cast_gate = [(scst, scst.n)] if scst.n else []
        dma("g", swc, wc_sb[:, :], wc_h.ap().rearrange("(c p) d -> p c d", p=128), waits=a_done)
        wc_ld = (swc, swc.n)
        ytbr = [Res(), Res()]
        xbr = [Res(), Res()]
        x1Tsr = [Res(), Res()]
        loTsr = Res()
        rR, xcR, sqR, x1R, hiR, lofR = Res(), Res(), Res(), Res(), Res(), Res()
        nmR, vsR, t0R, rstdR = Res(), Res(), Res(), Res()
        logits_done = None
        x1_store_done = None
        x1T_store_done = None
        NW = max(1, C // 512)
        WNl = C // NW

        def emit_ln(srcT, srcRes, xcT, xcRes, sqT, sqRes):
            if NW == 1:
                rsv = P.emit("v", lambda e: e.tensor_copy(ln_t0[:, :], lnt2[:, 0:1]),
                             waits=deps(srcRes.w, t0R.r, t0R.w), inc=ve)
            else:
                rsv = P.emit("v", lambda e: e.tensor_scalar(
                    ln_t0[:, :], lnt2[:, 0:1], lnt2[:, 1:2], None, ALU.add),
                    waits=deps(srcRes.w, t0R.r, t0R.w), inc=ve)
            t0R.w = rsv
            nmv = P.emit("v", lambda e: e.tensor_scalar(
                ln_nm[:, :], ln_t0[:, :], -1.0 / C, None, ALU.mult),
                waits=deps(rsv, nmR.r), inc=ve)
            nmR.w = nmv
            xcv = P.emit("s", lambda e, srcT=srcT, xcT=xcT: e.activation(
                xcT[:, 0:C], srcT[:, 0:C], AF.Identity, bias=ln_nm[:, 0:1]),
                waits=deps(nmv, xcRes.r, xcRes.w), inc=se)
            xcRes.w = xcv
            nmR.r = xcv
            srcRes.r = xcv
            sqv = P.emit("s", lambda e, xcT=xcT, sqT=sqT: e.activation(
                sqT[:, 0:C], xcT[:, 0:C], AF.Square, accum_out=ln_vs[:, 0:1]),
                waits=deps(xcv, sqRes.r, sqRes.w, vsR.r), inc=se)
            sqRes.w = sqv
            vsR.w = sqv
            lnv = P.emit("s", lambda e: e.activation(
                ln_t0[:, :], ln_vs[:, :], AF.Ln, scale=1.0 / C, bias=1e-5),
                waits=deps(sqv, nmv, t0R.r), inc=se)
            t0R.w = lnv
            vsR.r = lnv
            rsv2 = P.emit("s", lambda e: e.activation(
                ln_rstd[:, :], ln_t0[:, :], AF.Exp, scale=-0.5),
                waits=deps(lnv, rstdR.r), inc=se)
            rstdR.w = rsv2
            t0R.r = rsv2
            return rsv2

        for tt in range(NKT):
            par = tt % 2
            ytld = dma("y", syt[par], ytb[par][:, :],
                       yT_d.ap().rearrange("(c p) t -> p c t", p=128)[:, :, tt * 128:(tt + 1) * 128],
                       waits=deps(ytbr[par].r) + (a_done if tt < 2 else []))
            ytbr[par].w = ytld
            xld = dma("y", sxb[par], xb[par][:, :], x_h[tt * 128:(tt + 1) * 128, :],
                      waits=deps(xbr[par].r) + (a_done if tt < 2 else []))
            xbr[par].w = xld
            mm_done = [None] * NW
            for cci in range(CC):
                for wn in range(NW):
                    tw = deps(ytld, wc_ld, pbr[wn].r) if cci == 0 else ()
                    md = P.emit("t", lambda e, cci=cci, wn=wn, par=par: e.matmul(
                        pb[wn][:, 0:WNl], ytb[par][:, cci * 128:(cci + 1) * 128],
                        wc_sb[:, cci * C + wn * WNl: cci * C + (wn + 1) * WNl],
                        start=(cci == 0), stop=(cci == CC - 1)),
                        waits=tw, inc=te if cci == CC - 1 else None)
                    if cci == CC - 1:
                        mm_done[wn] = md
                        pbr[wn].w = md
            rv = None
            for wn in range(NW):
                rv = P.emit("v", lambda e, wn=wn, par=par: e.scalar_tensor_tensor(
                    r_sb[:, wn * WNl:(wn + 1) * WNl], pb[wn][:, 0:WNl], 1.0,
                    xb[par][:, wn * WNl:(wn + 1) * WNl], ALU.mult, ALU.add,
                    accum_out=lnt2[:, wn:wn + 1]),
                    waits=(deps(mm_done[wn], xld, rR.r, t0R.r, t0R.w)
                           if wn == 0 else deps(mm_done[wn])),
                    inc=ve)
            for wn in range(NW):
                pbr[wn].r = rv
            rR.w = rv
            t0R.w = rv
            xbr[par].r = rv
            rstd_done = emit_ln(r_sb, rR, xc_sb, xcR, sq_sb, sqR)
            x1v = P.emit("v", lambda e: e.tensor_scalar(
                x1f[:, :], xc_sb[:, :], ln_rstd[:, 0:1], None, ALU.mult),
                waits=deps(rstd_done, x1R.r, x1_store_done), inc=ve)
            x1R.w = x1v
            rstdR.r = x1v
            xcR.r = x1v
            hiv = P.emit("v", lambda e: e.tensor_copy(hi_b[:, :], x1f[:, :]),
                         waits=deps(x1v, hiR.r), inc=ve)
            hiR.w = hiv
            lov = P.emit("v", lambda e: e.tensor_sub(lof[:, :], x1f[:, :], hi_b[:, :]),
                         waits=deps(hiv, lofR.r), inc=ve)
            lofR.w = lov
            hiR.r = lov
            x1_store_done = dma("y", sx1s, x1_d[tt * 128:(tt + 1) * 128, :], x1f[:, :],
                                waits=deps(x1v))
            ev_x_last = None
            lg_last = None
            tpl = None
            for cci in range(CC):
                bx = 2 + cci % 2
                bl = 5 + cci % 2
                tpx = P.emit("t", lambda e, cci=cci, bx=bx: e.transpose(
                    pb[bx][:, 0:128], x1f[:, cci * 128:(cci + 1) * 128], identf[:, :]),
                    waits=deps(x1v, pbr[bx].r), inc=te)
                pbr[bx].w = tpx
                ev_x = P.emit("s", lambda e, cci=cci, par=par, bx=bx: e.copy(
                    x1Ts[par][:, cci * 128:(cci + 1) * 128], pb[bx][:, 0:128]),
                    waits=deps(tpx, x1Tsr[par].r), inc=se)
                pbr[bx].r = ev_x
                ev_x_last = ev_x
                tpl = P.emit("t", lambda e, cci=cci, bl=bl: e.transpose(
                    pb[bl][:, 0:128], lof[:, cci * 128:(cci + 1) * 128], identf[:, :]),
                    waits=deps(lov, pbr[bl].r), inc=te)
                pbr[bl].w = tpl
                ev_l = P.emit("s", lambda e, cci=cci, bl=bl: e.copy(
                    loTs[:, cci * 128:(cci + 1) * 128], pb[bl][:, 0:128]),
                    waits=deps(tpl, loTsr.r if cci == 0 else None), inc=se)
                pbr[bl].r = ev_l
                loTsr.w = ev_l
                for j in range(3):
                    lhs = gwhi if j < 2 else gwlo
                    rhsT = x1Ts[par] if j != 1 else loTs
                    tw = deps(ev_x, ev_l, gwlo_v)
                    if cci == 0 and j == 0:
                        tw = tw + deps(pbr[4].r)
                    lg_last = P.emit("t", lambda e, cci=cci, j=j, lhs=lhs, rhsT=rhsT: e.matmul(
                        pb[4][0:E, 0:128], lhs[:, cci * E:(cci + 1) * E],
                        rhsT[:, cci * 128:(cci + 1) * 128],
                        start=(cci == 0 and j == 0), stop=(cci == CC - 1 and j == 2)),
                        waits=tw, inc=te if (cci == CC - 1 and j == 2) else None)
            pbr[4].w = lg_last
            loTsr.r = lg_last
            lofR.r = tpl
            lgev = P.emit("s", lambda e, tt=tt: e.copy(
                logitsT[:, tt * 128:(tt + 1) * 128], pb[4][0:E, 0:128]),
                waits=deps(lg_last), inc=se)
            pbr[4].r = lgev
            logits_done = lgev
            x1T_store_done = dma("y", sxT[par],
                                 x1T_d.ap().rearrange("(c p) t -> p c t", p=128)[:, :, tt * 128:(tt + 1) * 128],
                                 x1Ts[par][:, :], waits=deps(ev_x_last))
            x1Tsr[par].r = x1T_store_done
            x1R.r = x1T_store_done

        if "C" not in PH:
            P.emit("y", lambda e: e.nop(),
                   waits=[(syT, syT.n), (sx1s, sx1s.n), (sxT[0], sxT[0].n), (sxT[1], sxT[1].n)])
            return _finish(nc, P, block_holder)
        cin_st = dma("g", scc, cc_in[:, :], logitsT[:, :], waits=deps(logits_done))
        P.emit("g", lambda e: e.collective_compute(
            "AllGather", ALU.bypass, replica_groups=[list(range(NCORES))],
            ins=[cc_in.ap().opt()], outs=[cc_out.ap().opt()]),
            waits=deps(cin_st), inc=ccs, incn=1)
        lg_ld = dma("g", slg, lg_all[:, :],
                    cc_out.ap().rearrange("r (h c) -> (r h) c", h=2), waits=[(ccs, 1)])
        P.emit("v", lambda e: e.memset(bis_lo[:, :], -64.0), inc=ve)
        biv = P.emit("v", lambda e: e.memset(bis_hi[:, :], 64.0), inc=ve)
        smid = P.emit("s", lambda e: e.memzero(bis_mid[:, :]), inc=se)
        selv = None
        for i in range(NBIS):
            cmpv = P.emit("v", lambda e: e.tensor_scalar(
                cmpb[:, :], lg_all[:, :], bis_mid[:, 0:1], None, ALU.is_le),
                waits=deps(lg_ld, smid, biv), inc=ve)
            cntv = P.emit("v", lambda e: e.reduce_sum(bis_cnt[:, :], cmpb[:, :], axis=AX.X),
                          waits=deps(cmpv), inc=ve)
            cmm = P.emit("t", lambda e: e.matmul(pb[0][:, 0:1], ones128[:, :], bis_cnt[:, :],
                                                 start=True, stop=True),
                         waits=deps(cntv, pbr[0].r), inc=te)
            pbr[0].w = cmm
            totS = P.emit("s", lambda e: e.copy(bis_tot[:, :], pb[0][:, 0:1]),
                          waits=deps(cmm), inc=se)
            pbr[0].r = totS
            P.emit("v", lambda e: e.tensor_scalar(
                bis_p[:, :], bis_tot[:, :], float(KTH), None, ALU.is_ge),
                waits=deps(totS))
            pv = P.emit("v", lambda e: e.tensor_scalar(
                bis_pi[:, :], bis_tot[:, :], float(KTH), None, ALU.is_lt),
                waits=deps(totS), inc=ve)
            P.emit("v", lambda e: e.select(bis_hi[:, :], bis_p[:, :], bis_mid[:, :], bis_hi[:, :], add_drain=True),
                   waits=deps(pv))
            selv = P.emit("v", lambda e: e.select(bis_lo[:, :], bis_pi[:, :], bis_mid[:, :], bis_lo[:, :], add_drain=True),
                          inc=ve)
            if i < NBIS - 1:
                t1s = P.emit("s", lambda e: e.activation(
                    bis_t1[:, :], bis_lo[:, :], AF.Identity, bias=bis_hi[:, 0:1]),
                    waits=deps(selv), inc=se)
                smid = P.emit("s", lambda e: e.mul(bis_mid[:, :], bis_t1[:, :], 0.5),
                              waits=deps(t1s), inc=se)
        lo2s = P.emit("s", lambda e: e.copy(lo2[:, :], bis_lo[:, :]), waits=deps(selv), inc=se)
        gt1 = P.emit("v", lambda e: e.scalar_tensor_tensor(
            gtmp[:, :], logitsT[:, :], lo2[0:E, 0:1], logitsT[:, :], ALU.is_gt, ALU.mult),
            waits=deps(lo2s, cin_st), inc=ve)
        gatesv = P.emit("v", lambda e: e.tensor_copy(logitsT[:, :], gtmp[:, :]),
                        waits=deps(gt1), inc=ve)
        gtok_state = {"done": None}

        def emit_gtok():
            gts = None
            for tt in range(NKT):
                gtp = P.emit("t", lambda e, tt=tt: e.transpose(
                    pb[1][:, 0:E], logitsT[:, tt * 128:(tt + 1) * 128], identf[0:E, 0:E]),
                    waits=deps(gatesv, pbr[1].r), inc=te)
                pbr[1].w = gtp
                gts = P.emit("s", lambda e, tt=tt: e.copy(g_tok[:, tt * E:(tt + 1) * E], pb[1][:, 0:E]),
                             waits=deps(gtp), inc=se)
                pbr[1].r = gts
            gtok_state["done"] = gts

        if "D" not in PH:
            P.emit("y", lambda e: e.nop(),
                   waits=[(syT, syT.n), (sx1s, sx1s.n), (sxT[0], sxT[0].n), (sxT[1], sxT[1].n),
                          (scc, scc.n), (ccs, 1)])
            return _finish(nc, P, block_holder)
        x1TsbR = Res()
        hR = [Res() for _ in range(NHC)]
        b1R = [Res(), Res()]
        w1R = [Res(), Res()]
        w2R = [Res(), Res()]
        x1rR2 = [Res(), Res()]
        lnxR2 = [Res(), Res()]
        lt0R2 = [Res(), Res()]
        lnmR2 = [Res(), Res()]
        lvsR2 = [Res(), Res()]
        lrsR2 = [Res(), Res()]
        accdep = {}
        out_state = {"st": None}

        def emit_tbend_ln(tb, tt8):
            p8 = tt8 % 2
            gtt = tb * TT8 + tt8
            x1ld = dma("y", sx1r[p8], x1r[p8][:, :], x1_d[gtt * 128:(gtt + 1) * 128, :],
                       waits=deps(x1_store_done, x1rR2[p8].r, x1rR2[p8].w))
            x1rR2[p8].w = x1ld
            rv2 = P.emit("g", lambda e, tt8=tt8, p8=p8: e.tensor_tensor(
                lnx[p8][:, :], accs[:, tt8 * C:(tt8 + 1) * C], x1r[p8][:, :], ALU.add),
                waits=deps(x1ld, lnxR2[p8].r, lnxR2[p8].w) + [accdep[(tt8, d2)] for d2 in range(ND2)],
                inc=ge)
            lnxR2[p8].w = rv2
            x1rR2[p8].r = rv2
            msv = P.emit("s", lambda e, p8=p8: e.activation(
                x1r[p8][:, :], lnx[p8][:, :], AF.Identity, accum_out=lnt2[:, p8:p8 + 1]),
                waits=deps(rv2, x1rR2[p8].w, lt0R2[p8].r), inc=se)
            x1rR2[p8].w = msv
            lt0R2[p8].w = msv
            nmv = P.emit("v", lambda e, p8=p8: e.tensor_scalar(
                lnn2[:, p8:p8 + 1], lnt2[:, p8:p8 + 1], -1.0 / C, None, ALU.mult),
                waits=deps(msv, lnmR2[p8].r), inc=ve)
            lnmR2[p8].w = nmv
            lt0R2[p8].r = nmv
            xcv = P.emit("s", lambda e, p8=p8: e.activation(
                x1r[p8][:, :], lnx[p8][:, :], AF.Identity, bias=lnn2[:, p8:p8 + 1]),
                waits=deps(nmv, msv), inc=se)
            x1rR2[p8].w = xcv
            lnmR2[p8].r = xcv
            lnxR2[p8].r = xcv
            sqv = P.emit("s", lambda e, p8=p8: e.activation(
                lnx[p8][:, :], x1r[p8][:, :], AF.Square, accum_out=lnv2[:, p8:p8 + 1]),
                waits=deps(xcv, lvsR2[p8].r), inc=se)
            lnxR2[p8].w = sqv
            lvsR2[p8].w = sqv
            lnv = P.emit("s", lambda e, p8=p8: e.activation(
                lnt2[:, p8:p8 + 1], lnv2[:, p8:p8 + 1], AF.Ln, scale=1.0 / C, bias=1e-5),
                waits=deps(sqv, lt0R2[p8].r, lt0R2[p8].w), inc=se)
            lt0R2[p8].w = lnv
            lvsR2[p8].r = lnv
            rsv2 = P.emit("s", lambda e, p8=p8: e.activation(
                lnr2[:, p8:p8 + 1], lnt2[:, p8:p8 + 1], AF.Exp, scale=-0.5),
                waits=deps(lnv, lrsR2[p8].r), inc=se)
            lrsR2[p8].w = rsv2
            lt0R2[p8].r = rsv2
            ov = P.emit("v", lambda e, p8=p8: e.tensor_scalar(
                lnx[p8][:, :], x1r[p8][:, :], lnr2[:, p8:p8 + 1], None, ALU.mult),
                waits=deps(rsv2, sqv, out_state["st"]), inc=ve)
            lnxR2[p8].w = ov
            x1rR2[p8].r = ov
            lrsR2[p8].r = ov
            st = dma("y", sout, out_h[gtt * 128:(gtt + 1) * 128, :], lnx[p8][:, :],
                     waits=deps(ov))
            lnxR2[p8].r = st
            out_state["st"] = st
            for d2 in range(ND2):
                accdep[(tt8, d2)] = rv2

        b_done = [logits_done] if logits_done else []
        carry = {}

        for tb in range(NTB):
            bd0 = b_done + [(sxT[0], sxT[0].n), (sxT[1], sxT[1].n)] if tb == 0 else []
            if ("xT", tb) in carry:
                x1Tld = carry.pop(("xT", tb))
            else:
                xw = deps((sxT[0], sxT[0].n), (sxT[1], sxT[1].n), x1TsbR.r, logits_done)
                dma("y", sx1T, x1T_sb[:, 0:(CC // 2) * TB],
                    x1T_d.ap().rearrange("(c p) t -> p c t", p=128)[:, 0:CC // 2, tb * TB:(tb + 1) * TB],
                    waits=xw)
                x1Tld = dma("s", sx1T, x1T_sb[:, (CC // 2) * TB:],
                            x1T_d.ap().rearrange("(c p) t -> p c t", p=128)[:, CC // 2:CC, tb * TB:(tb + 1) * TB],
                            waits=xw)
                x1TsbR.w = x1Tld

            def emit_b2(tb=tb):
                for tt8 in range(TT8):
                    for d2 in range(ND2):
                        bki = 4 + (tt8 * ND2 + d2) % 2
                        bmm = P.emit("t", lambda e, tb=tb, tt8=tt8, d2=d2, bki=bki: e.matmul(
                            pb[bki][:, 0:DW],
                            logitsT[:, (tb * TT8 + tt8) * 128:(tb * TT8 + tt8 + 1) * 128],
                            b2b[:, d2 * DW:(d2 + 1) * DW], start=True, stop=True),
                            waits=deps(gatesv, pbr[bki].r, setup_l2), inc=te)
                        pbr[bki].w = bmm
                        bcv = P.emit("v", lambda e, tt8=tt8, d2=d2, bki=bki: e.tensor_copy(
                            accs[:, tt8 * C + d2 * DW: tt8 * C + (d2 + 1) * DW], pb[bki][:, 0:DW]),
                            waits=deps(bmm, accdep.get((tt8, d2))), inc=ve)
                        pbr[bki].r = bcv
                        accdep[(tt8, d2)] = bcv


            for ei in range(E):
                epar = ei % 2
                if ("b1", tb, ei) in carry:
                    b1ld = carry.pop(("b1", tb, ei))
                    w1ld = carry.pop(("w1", tb, ei))
                else:
                    b1ld = dma("g", sb1[epar], b1e[epar][:, :], b1r_h[ei, :, :],
                               waits=deps(b1R[epar].r) + (bd0 if ei < 2 else []))
                    b1R[epar].w = b1ld
                    w1ld = {}
                    for f in range(min(2, NF1)):
                        w1ld[f] = dma("g", sw1[f % 2], w1b[f % 2][:, :],
                                      w1c_d.ap()[ei].rearrange("(c p) h -> p c h", p=128)[:, :, f * 512:(f + 1) * 512],
                                      waits=deps(w1R[f % 2].r) + (bd0 if ei == 0 else []) + cast_gate)
                for hc in range(NHC):
                    f = hc // 4
                    hpar = hc % 2
                    bks = [hpar * NTS + tsv for tsv in range(NTS)]
                    mm_last = [None] * NTS
                    for dc in range(CC):
                        for tsv in range(NTS):
                            bki = bks[tsv]
                            tw = deps(w1ld[f], x1Tld, pbr[bki].r) if dc == 0 else ()
                            ml = P.emit("t", lambda e, f=f, hc=hc, dc=dc, tsv=tsv, bki=bki: e.matmul(
                                pb[bki][:, 0:TSW],
                                w1b[f % 2][:, dc * 512 + (hc % 4) * 128: dc * 512 + (hc % 4) * 128 + 128],
                                x1T_sb[:, dc * TB + tsv * TSW: dc * TB + (tsv + 1) * TSW],
                                start=(dc == 0), stop=(dc == CC - 1)),
                                waits=tw, inc=te if dc == CC - 1 else None)
                            if dc == CC - 1:
                                mm_last[tsv] = ml
                                pbr[bki].w = ml
                                w1R[f % 2].r = ml
                                x1TsbR.r = ml
                    if hc % 4 == 3 and f + 2 < NF1:
                        w1ld[f + 2] = dma("g", sw1[f % 2], w1b[f % 2][:, :],
                                          w1c_d.ap()[ei].rearrange("(c p) h -> p c h", p=128)[:, :, (f + 2) * 512:(f + 3) * 512],
                                          waits=deps(w1R[f % 2].r) + cast_gate)
                    gv = None
                    for tsv in range(NTS):
                        bki = bks[tsv]
                        gv = P.emit("s", lambda e, hc=hc, tsv=tsv, bki=bki, epar=epar: e.activation(
                            h_sb[:, hc * TB + tsv * TSW: hc * TB + (tsv + 1) * TSW],
                            pb[bki][:, 0:TSW],
                            AF.Gelu_apprx_tanh, bias=b1e[epar][:, hc:hc + 1]),
                            waits=deps(mm_last[tsv], b1R[epar].w, hR[hc].r), inc=se)
                        pbr[bki].r = gv
                    hR[hc].w = gv
                    b1R[epar].r = gv
                if ei == 0:
                    if tb == 0:
                        emit_gtok()
                    emit_b2()
                if ei == E - 1 and tb + 1 < NTB:
                    nxw = deps((sxT[0], sxT[0].n), (sxT[1], sxT[1].n), x1TsbR.r, logits_done)
                    dma("y", sx1T, x1T_sb[:, 0:(CC // 2) * TB],
                        x1T_d.ap().rearrange("(c p) t -> p c t", p=128)[:, 0:CC // 2, (tb + 1) * TB:(tb + 2) * TB],
                        waits=nxw)
                    nxT = dma("s", sx1T, x1T_sb[:, (CC // 2) * TB:],
                              x1T_d.ap().rearrange("(c p) t -> p c t", p=128)[:, CC // 2:CC, (tb + 1) * TB:(tb + 2) * TB],
                              waits=nxw)
                    x1TsbR.w = nxT
                    carry[("xT", tb + 1)] = nxT
                    nb1 = dma("g", sb1[0], b1e[0][:, :], b1r_h[0, :, :],
                              waits=deps(b1R[0].r))
                    b1R[0].w = nb1
                    nw1 = {}
                    for f in range(min(2, NF1)):
                        nw1[f] = dma("g", sw1[f % 2], w1b[f % 2][:, :],
                                     w1c_d.ap()[0].rearrange("(c p) h -> p c h", p=128)[:, :, f * 512:(f + 1) * 512],
                                     waits=deps(w1R[f % 2].r) + cast_gate)
                    carry[("b1", tb + 1, 0)] = nb1
                    carry[("w1", tb + 1, 0)] = nw1
                w2ld = {}
                for g8 in range(min(2, ND2 * NG8)):
                    dh0, g80 = divmod(g8, NG8)
                    w2ld[(dh0, g80)] = dma(
                        "g", sw2[g8 % 2], w2b[g8 % 2][:, :],
                        w2c_d.ap()[ei].rearrange("(g p) d -> p g d", p=128)[:, g80 * GH:(g80 + 1) * GH, dh0 * DW:(dh0 + 1) * DW],
                        waits=deps(w2R[g8 % 2].r) + (bd0 if ei == 0 else []) + cast_gate)
                fidx = 0
                last_w2_mm = None
                for dh in range(ND2):
                    for wv_ in range(1):
                        tts = list(range(TT8))
                        for g8 in range(NG8):
                            bpar = (dh * NG8 + g8) % 2
                            glast = {}
                            for hg in range(GH):
                                hc = g8 * GH + hg
                                for ti, tt8 in enumerate(tts):
                                    bki = (4 + ti) % 8
                                    tw = ()
                                    if hg == 0:
                                        tw = deps(w2ld[(dh, g8)], hR[g8 * GH + GH - 1].w)
                                        if g8 == 0:
                                            tw = tw + deps(pbr[bki].r)
                                    isl = (g8 == NG8 - 1 and hg == GH - 1)
                                    m2 = P.emit("t", lambda e, hc=hc, tt8=tt8, hg=hg, bki=bki, bpar=bpar, g8=g8, isl=isl: e.matmul(
                                        pb[bki][:, 0:DW],
                                        h_sb[:, hc * TB + tt8 * 128: hc * TB + tt8 * 128 + 128],
                                        w2b[bpar][:, hg * DW:(hg + 1) * DW],
                                        start=(g8 == 0 and hg == 0), stop=isl),
                                        waits=tw, inc=te if (isl or hg == GH - 1) else None)
                                    if hg == GH - 1:
                                        glast[ti] = m2
                                    if isl:
                                        pbr[bki].w = m2
                                        last_w2_mm = m2
                            grp_done = glast[len(tts) - 1]
                            w2R[bpar].r = grp_done
                            if True:
                                nidx = dh * NG8 + g8 + 2
                                if nidx < ND2 * NG8:
                                    ndh, ng8 = divmod(nidx, NG8)
                                    w2ld[(ndh, ng8)] = dma(
                                        "g", sw2[nidx % 2], w2b[nidx % 2][:, :],
                                        w2c_d.ap()[ei].rearrange("(g p) d -> p g d", p=128)[:, ng8 * GH:(ng8 + 1) * GH, ndh * DW:(ndh + 1) * DW],
                                        waits=deps(w2R[nidx % 2].r) + cast_gate)
                        for ti, tt8 in enumerate(tts):
                            bki = (4 + ti) % 8
                            gcol = (tb * TT8 + tt8) * E + ei
                            av = P.emit("v", lambda e, tt8=tt8, dh=dh, bki=bki, gcol=gcol: e.scalar_tensor_tensor(
                                accs[:, tt8 * C + dh * DW: tt8 * C + (dh + 1) * DW],
                                pb[bki][:, 0:DW], g_tok[:, gcol:gcol + 1],
                                accs[:, tt8 * C + dh * DW: tt8 * C + (dh + 1) * DW],
                                ALU.mult, ALU.add),
                                waits=deps(pbr[bki].w, accdep.get((tt8, dh)), gtok_state["done"]), inc=ve)
                            pbr[bki].r = av
                            accdep[(tt8, dh)] = av
                            if ei == E - 1 and dh == ND2 - 1:
                                emit_tbend_ln(tb, tt8)
                for hc in range(NHC):
                    hR[hc].r = last_w2_mm

        P.emit("y", lambda e: e.nop(),
               waits=[(syT, syT.n), (sx1s, sx1s.n), (sxT[0], sxT[0].n), (sxT[1], sxT[1].n),
                      (sout, sout.n), (scc, scc.n)])

        nc._sems = dict(te=te.n, ve=ve.n, se=se.n, sld=sld.n, syT=syT.n, sout=sout.n)
        nc._dbg = dict(bis_lo=bis_lo, bis_hi=bis_hi, bis_mid=bis_mid, bis_tot=bis_tot, bis_p=bis_p, bis_cnt=bis_cnt, g_tok=g_tok, logitsT=logitsT,
                       h_sb=h_sb, accs=accs, x1T_sb=x1T_sb, lg_all=lg_all)

        return _finish(nc, P, block_holder)


def _finish(nc, P, ctx):
    with nc.Block() as block:
        @block.tensor
        def _(eng):
            for f in P.ops["t"]:
                f(eng)

        @block.vector
        def _(eng):
            for f in P.ops["v"]:
                f(eng)

        @block.scalar
        def _(eng):
            for f in P.ops["s"]:
                f(eng)

        @block.gpsimd
        def _(eng):
            for f in P.ops["g"]:
                f(eng)

        @block.sync
        def _(eng):
            for f in P.ops["y"]:
                f(eng)
    return nc


def make_in_maps(c, inputs):
    A = lambda z: np.ascontiguousarray(np.asarray(z, dtype=np.float32))
    x_all = A(inputs["L0_attn_x"])
    E, NHC = c["E"], c["NHC"]
    shared = {
        "wq": A(inputs["L0_attn_Wq_down"]),
        "wk": A(inputs["L0_attn_Wk_down"]),
        "wv": A(inputs["L0_attn_Wv_down"]),
        "upcq": A(inputs["L0_attn_Wq_up_c"]),
        "upeq": A(inputs["L0_attn_Wq_up_e"]),
        "upeqT": A(np.asarray(inputs["L0_attn_Wq_up_e"], np.float32).T),
        "upck": A(inputs["L0_attn_Wk_up_c"]),
        "upek": A(inputs["L0_attn_Wk_up_e"]),
        "upekT": A(np.asarray(inputs["L0_attn_Wk_up_e"], np.float32).T),
        "wvup": A(inputs["L0_attn_Wv_up"]),
        "wc": A(inputs["L0_attn_Wc"]),
        "gwT": A(np.asarray(inputs["L0_ffn_gate_weight"], np.float32).T),
        "w1T": A(np.asarray(inputs["L0_ffn_w1"], np.float32).transpose(0, 2, 1)),
        "w2T": A(np.asarray(inputs["L0_ffn_w2"], np.float32).transpose(0, 2, 1)),
        "b1r": A(np.asarray(inputs["L0_ffn_b1"], np.float32).reshape(E, NHC, 128).transpose(0, 2, 1)),
        "b2": A(inputs["L0_ffn_b2"]),
    }
    in_maps = []
    for i in range(c["NCORES"]):
        m = dict(shared)
        m["x"] = A(x_all[i])
        m["xT"] = A(x_all[i].T)
        in_maps.append(m)
    return in_maps


def kernel(**inputs):
    c = make_cfg()
    nc = build_program(c)
    in_maps = make_in_maps(c, inputs)
    res = run_bass_kernel_spmd(nc, in_maps, core_ids=list(range(c["NCORES"])))
    out = np.stack([np.asarray(res.results[i]["out"], dtype=np.float32)
                    for i in range(c["NCORES"])], axis=0)
    return out
